# version 01
# baseline (speedup 1.0000x reference)
import math
from contextlib import ExitStack

import numpy as np

import concourse.bass as bass
import concourse.mybir as mybir
from concourse.bass_utils import run_bass_kernel_spmd

F32 = mybir.dt.float32
BF16 = mybir.dt.bfloat16
AF = mybir.ActivationFunctionType
ALU = mybir.AluOpType
AX = mybir.AxisListType

T = 2048
E = 2048
H = 16
KVH = 4
D = 128
NTT = 16
HALF = 8
I_FF = 1024
NEXP = 16
NEG = -1e30
MAGIC = 12582912.0
ESCALE = 1.0 / math.sqrt(128.0)
LAM_INIT = 0.2
BIG = 1e32

ENGS = ("sync", "gpsimd", "vector", "scalar", "tensor")
ATT_SUB = 99
ATT_QMAX = 99


class S:
    def __init__(self, h):
        self.h = h
        self.v = 0


class Bank:
    def __init__(self, t):
        self.t = t
        self.prod = None
        self.cons = []


class Builder:
    def __init__(self, nc):
        self.nc = nc
        self.rec = {k: [] for k in ENGS}
        self.waited = {}

    def _filter(self, eng, waits):
        wl = []
        for (s, v) in waits:
            if v <= 0:
                continue
            key = (eng, s.h.name)
            if self.waited.get(key, -1) >= v:
                continue
            assert v <= s.v, (
                f"wait on future value {v} > {s.v} of {s.h.name} on {eng}")
            self.waited[key] = v
            wl.append((s.h, v))
        return tuple(wl)

    def op(self, eng, method, *args, waits=(), inc=None, incn=1, **kw):
        wl = self._filter(eng, waits)
        ih = None
        ret = None
        if inc is not None:
            inc.v += incn
            ret = inc.v
            ih = inc.h

        def fn(e, method=method, args=args, kw=kw, wl=wl, ih=ih, iv=incn):
            for (h, v) in wl:
                e.wait_ge(h, v)
            ins = getattr(e, method)(*args, **kw)
            if ih is not None:
                ins.then_inc(ih, iv)

        self.rec[eng].append(fn)
        return ret

    def wait(self, eng, *waits):
        wl = self._filter(eng, waits)
        if not wl:
            return

        def fn(e, wl=wl):
            for (h, v) in wl:
                e.wait_ge(h, v)

        self.rec[eng].append(fn)

    def replay(self, block):
        recs = self.rec

        @block.sync
        def _(e):
            for fn in recs["sync"]:
                fn(e)

        @block.gpsimd
        def _(e):
            for fn in recs["gpsimd"]:
                fn(e)

        @block.vector
        def _(e):
            for fn in recs["vector"]:
                fn(e)

        @block.scalar
        def _(e):
            for fn in recs["scalar"]:
                fn(e)

        @block.tensor
        def _(e):
            for fn in recs["tensor"]:
                fn(e)


def build(nc: bass.Bass, dbg: bool = False, phases: int = 99):
    b = Builder(nc)
    ctx = ExitStack()
    e = ctx.enter_context

    x_h = nc.dram_tensor("x", [T, E], F32, kind="ExternalInput")
    wq_h = nc.dram_tensor("wq", [E, 4096], F32, kind="ExternalInput")
    wk_h = nc.dram_tensor("wk", [E, 1024], F32, kind="ExternalInput")
    wv_h = nc.dram_tensor("wv", [E, 512], F32, kind="ExternalInput")
    wo_h = nc.dram_tensor("wo", [E, E], F32, kind="ExternalInput")
    lq1_h = nc.dram_tensor("lq1", [H, D], F32, kind="ExternalInput")
    lq2_h = nc.dram_tensor("lq2", [H, D], F32, kind="ExternalInput")
    lk1_h = nc.dram_tensor("lk1", [H, D], F32, kind="ExternalInput")
    lk2_h = nc.dram_tensor("lk2", [H, D], F32, kind="ExternalInput")
    rms_h = nc.dram_tensor("rms", [1, D], F32, kind="ExternalInput")
    gwt_h = nc.dram_tensor("gwt", [E, NEXP], F32, kind="ExternalInput")
    eb_h = nc.dram_tensor("eb", [1, NEXP], F32, kind="ExternalInput")
    gt_h = nc.dram_tensor("gt", [NEXP, E, I_FF], F32, kind="ExternalInput")
    ut_h = nc.dram_tensor("ut", [NEXP, E, I_FF], F32, kind="ExternalInput")
    dt_h = nc.dram_tensor("dtp", [NEXP, I_FF, E], F32, kind="ExternalInput")
    sgt_h = nc.dram_tensor("sgt", [E, I_FF], F32, kind="ExternalInput")
    sut_h = nc.dram_tensor("sut", [E, I_FF], F32, kind="ExternalInput")
    sdt_h = nc.dram_tensor("sdt", [I_FF, E], F32, kind="ExternalInput")
    out_h = nc.dram_tensor("out", [T, E], F32, kind="ExternalOutput")

    skind = "ExternalOutput" if dbg else "Internal"
    xn_hbm = nc.dram_tensor("xn_hbm", [128, 16 * T], BF16, kind=skind)
    q_hbm = nc.dram_tensor("q_hbm", [128, 32 * T], BF16, kind=skind)

    ACC = e(nc.sbuf_tensor("ACC", [128, HALF * E], F32))
    XNH = e(nc.sbuf_tensor("XNH", [128, 16 * 1024], BF16))
    KFM = e(nc.sbuf_tensor("KFM", [128, 8 * T], BF16))
    VTOK = e(nc.sbuf_tensor("VTOK", [128, NTT * 512], BF16))
    CW = e(nc.sbuf_tensor("CWb", [128, NTT * NEXP], F32))
    WST = e(nc.sbuf_tensor("WST", [128, 2 * 8192], BF16))
    OVL = e(nc.sbuf_tensor("OVL", [128, 12800], BF16))
    TRIB = e(nc.sbuf_tensor("TRIB", [128, 128], BF16))
    IDB = e(nc.sbuf_tensor("IDB", [128, 128], BF16))
    IDF = e(nc.sbuf_tensor("IDF", [16, 16], F32))
    RMSREP = e(nc.sbuf_tensor("RMSREP", [128, 128], BF16))
    LAMBC = e(nc.sbuf_tensor("LAMBC", [128, 16], F32))
    EBREP = e(nc.sbuf_tensor("EBREP", [128, 16], F32))
    ROWS = e(nc.sbuf_tensor("ROWS", [1, 544], F32))
    COLS = e(nc.sbuf_tensor("COLS", [128, 16], F32))
    RW = e(nc.sbuf_tensor("RW", [128, 112], F32))
    HSQ = e(nc.sbuf_tensor("HSQ", [128, 16], F32))
    HST = e(nc.sbuf_tensor("HST", [128, 64], F32))

    ONESROW = ROWS.ap()[:, 0:128]
    THROW = ROWS.ap()[:, 128:256]
    TOKR = ROWS.ap()[:, 256:384]
    RMSROW = ROWS.ap()[:, 384:512]
    LAMROW = ROWS.ap()[:, 512:528]
    EBROW = ROWS.ap()[:, 528:544]

    ZERO1 = COLS.ap()[:, 0:1]
    EPS5C = COLS.ap()[:, 1:2]
    EPS8C = COLS.ap()[:, 2:3]
    EPS20C = COLS.ap()[:, 3:4]
    ONEC = COLS.ap()[:, 4:5]
    LI16 = COLS.ap()[:, 5:6]
    C_A = COLS.ap()[:, 6:7]
    C_B = COLS.ap()[:, 7:8]
    C_C = COLS.ap()[:, 8:9]
    C_D = COLS.ap()[:, 9:10]
    C_E = COLS.ap()[:, 10:11]
    C_F = COLS.ap()[:, 11:12]
    C_G = COLS.ap()[:, 12:13]
    C_H = COLS.ap()[:, 13:14]
    C_I = COLS.ap()[:, 14:15]
    C_J = COLS.ap()[:, 15:16]

    H_D0 = HST.ap()[:, 0:16]
    H_D1 = HST.ap()[:, 16:32]
    H_D2 = HST.ap()[:, 32:48]
    H_SC = HST.ap()[:, 48:64]

    R_SCO = RW.ap()[:, 0:16]
    R_BIA = RW.ap()[:, 16:32]
    R_GS = RW.ap()[:, 32:36]
    R_GM = RW.ap()[:, 36:40]
    R_GMN = RW.ap()[:, 40:44]
    R_MSK = RW.ap()[:, 48:64]
    R_SEL = RW.ap()[:, 64:80]
    R_PAD = RW.ap()[:, 80:88]
    R_MX8 = RW.ap()[:, 88:96]
    R_T2 = RW.ap()[:, 44:45]
    R_T4 = RW.ap()[:, 45:46]
    R_WS = RW.ap()[:, 46:47]
    R_WI = RW.ap()[:, 47:48]
    R_D01 = RW.ap()[:, 100:104]
    R_D02 = RW.ap()[:, 104:108]
    R_MSK2 = RW.ap()[:, 80:96]
    R_EQ = RW.ap()[:, 96:112]
    A_T1 = RW.ap()[:, 0:16]
    A_T2 = RW.ap()[:, 16:32]
    A_T3 = RW.ap()[:, 32:48]
    A_T4 = RW.ap()[:, 48:64]

    def bfv(off, n):
        return OVL.ap()[:, off:off + n]

    def f32v(off, n):
        return OVL.ap()[:, off:off + 2 * n].bitcast(F32)

    V_XT = f32v(0, 2048)
    V_XNT = bfv(4096, 2048)
    TRIG_OFF = 6144

    def trig_sin(tt):
        return bfv(TRIG_OFF + tt * 256, 128)

    def trig_cos(tt):
        return bfv(TRIG_OFF + tt * 256 + 128, 128)

    V_RT1 = f32v(0, 128)
    V_RT2 = f32v(256, 128)
    V_RT3 = f32v(512, 128)
    V_RT4 = f32v(768, 128)
    V_STG0 = bfv(1024, 512)
    V_STG1 = bfv(1536, 512)
    V_TRA = f32v(2048, 128)
    V_TRB = f32v(2304, 128)
    V_LAMT = f32v(2560, 1024)

    A_QB = bfv(0, 4096)
    A_PT = bfv(4096, 2048)
    A_E0 = bfv(6144, 2048)
    A_E1 = bfv(8192, 2048)
    A_ATT = bfv(10240, 2048)
    A_STG = bfv(4096, 512)

    def a_att(h):
        return bfv(10240 + h * 128, 128)

    def a_att4(h0):
        return OVL.ap()[:, 10240 + h0 * 128: 10240 + (h0 + 4) * 128]

    F_HF = bfv(0, 8192)
    F_SIL = bfv(8192, 512)
    def f_dtc(par, ib4):
        off = 8704 + par * 2048
        return OVL.ap()[:, off + ib4 * 512: off + (ib4 + 1) * 512]

    def f_dtc3(par):
        off = 8704 + par * 2048
        return OVL.ap()[:, off:off + 2048].rearrange(
            "p (c n) -> p c n", n=512)

    GWT = WST.ap()[:, 0:256]

    def wst(par, c):
        return WST.ap()[:, par * 8192 + c * 512: par * 8192 + (c + 1) * 512]

    def wst3(par, ncols=512):
        return WST.ap()[:, par * 8192: par * 8192 + 16 * ncols].rearrange(
            "p (c n) -> p c n", n=ncols)

    def wst_g(par, ec):
        return WST.ap()[:, par * 8192 + ec * 128:
                        par * 8192 + ec * 128 + 128]

    def wst_g3(par):
        return WST.ap()[:, par * 8192: par * 8192 + 2048].rearrange(
            "p (c n) -> p c n", n=128)

    def wst_u(par, ec):
        return WST.ap()[:, par * 8192 + 2048 + ec * 128:
                        par * 8192 + 2048 + ec * 128 + 128]

    def wst_u3(par):
        return WST.ap()[:, par * 8192 + 2048: par * 8192 + 4096].rearrange(
            "p (c n) -> p c n", n=128)

    XNB_ALL = ACC.ap().bitcast(BF16)
    XNH3 = XNH.ap().rearrange("p (c n) -> p c n", n=1024)
    XNB3 = XNB_ALL[:, 0:16384].rearrange("p (c n) -> p c n", n=1024)

    def xn_tile(ec, tt):
        if tt < HALF:
            return XNH.ap()[:, ec * 1024 + tt * 128:
                            ec * 1024 + (tt + 1) * 128]
        return XNB_ALL[:, ec * 1024 + (tt - HALF) * 128:
                       ec * 1024 + (tt - HALF + 1) * 128]

    def xn_rhs(ec, tg):
        return XNH.ap()[:, ec * 1024 + tg * 512: ec * 1024 + (tg + 1) * 512]

    PA = Bank(e(nc.psum_tensor("PA", [128, 512], F32)))
    PB = Bank(e(nc.psum_tensor("PBK", [128, 512], F32)))
    PC = Bank(e(nc.psum_tensor("PCK", [128, 512], F32)))
    PD = Bank(e(nc.psum_tensor("PDK", [128, 512], F32)))
    PE_ = Bank(e(nc.psum_tensor("PEK", [128, 512], F32)))
    PF = Bank(e(nc.psum_tensor("PFK", [128, 512], F32)))
    PT1 = Bank(e(nc.psum_tensor("PT1", [128, 1024], BF16)))
    PT2 = Bank(e(nc.psum_tensor("PT2", [128, 1024], BF16)))

    sd = S(e(nc.semaphore("sd")))
    sg = S(e(nc.semaphore("sg")))
    sv = S(e(nc.semaphore("sv")))
    sa = S(e(nc.semaphore("sa")))
    sp = S(e(nc.semaphore("sp")))
    ESEM = {"vector": sv, "scalar": sa, "tensor": sp, "gpsimd": sg,
            "sync": sd}

    def vop(method, *args, waits=(), **kw):
        v = b.op("vector", method, *args, waits=waits, inc=sv, **kw)
        b.wait("vector", (sv, v))
        return v

    def pe_open(bank):
        if bank.cons:
            b.wait("tensor", *bank.cons)

    def pe_close(bank):
        bank.prod = (sp, sp.v)
        bank.cons = []

    def consume(bank, eng):
        assert bank.prod is not None
        b.wait(eng, bank.prod)

    def consumed(bank, eng):
        s = ESEM[eng]
        bank.cons.append((s, s.v))

    b.op("gpsimd", "memset", ONESROW, 1.0)
    b.op("gpsimd", "memset", COLS[:, :], 0.0)
    b.op("gpsimd", "memset", EPS5C, 1e-5)
    b.op("gpsimd", "memset", EPS8C, 1e-8)
    b.op("gpsimd", "memset", EPS20C, 1e-20)
    b.op("gpsimd", "memset", ONEC, 1.0)
    b.op("gpsimd", "memset", LI16, LAM_INIT)
    b.op("gpsimd", "memset", R_PAD, -3e38)
    b.op("gpsimd", "memset", TRIB[:, :], 0.0)
    b.op("gpsimd", "affine_select", TRIB[:, :], TRIB[:, :], [[-1, 128]],
         ALU.is_ge, NEG, base=0, channel_multiplier=1)
    b.op("gpsimd", "memset", IDB[:, :], 1.0)
    b.op("gpsimd", "affine_select", IDB[:, :], IDB[:, :], [[1, 128]],
         ALU.is_equal, 0.0, base=0, channel_multiplier=-1)
    b.op("gpsimd", "memset", IDF[:, :], 1.0)
    g_init = b.op("gpsimd", "affine_select", IDF[:, :], IDF[:, :], [[1, 16]],
                  ALU.is_equal, 0.0, base=0, channel_multiplier=-1, inc=sg)
    g_iota = b.op("gpsimd", "iota", THROW, [[1, 128]], base=0,
                  channel_multiplier=0,
                  allow_small_or_imprecise_dtypes=True, inc=sg)
    a_th = b.op("scalar", "activation", THROW, THROW, AF.Exp, bias=0.0,
                scale=-math.log(10000.0) / 128.0,
                waits=[(sg, g_iota)], inc=sa)
    v_th = b.op("vector", "tensor_scalar", THROW, THROW,
                1.0 / (2 * math.pi), None, ALU.mult,
                waits=[(sa, a_th)], inc=sv)

    d_rms = b.op("sync", "dma_start", out=RMSROW, in_=rms_h.ap(), inc=sd,
                 incn=16)
    lamt_q1 = V_LAMT[0:16, 0:128]
    lamt_k1 = V_LAMT[0:16, 128:256]
    lamt_q2 = V_LAMT[0:16, 256:384]
    lamt_k2 = V_LAMT[0:16, 384:512]
    b.op("sync", "dma_start", out=lamt_q1, in_=lq1_h.ap(), inc=sd, incn=16)
    b.op("sync", "dma_start", out=lamt_k1, in_=lk1_h.ap(), inc=sd, incn=16)
    b.op("sync", "dma_start", out=lamt_q2, in_=lq2_h.ap(), inc=sd, incn=16)
    d_l = b.op("sync", "dma_start", out=lamt_k2, in_=lk2_h.ap(), inc=sd,
               incn=16)
    d_eb = b.op("sync", "dma_start", out=EBROW, in_=eb_h.ap(), inc=sd,
                incn=16)
    d_gwt = b.op("gpsimd", "dma_start",
                 out=GWT.rearrange("p (c n) -> p c n", n=16),
                 in_=bass.AP(gwt_h, 0, [[16, 128], [128 * 16, 16], [1, 16]]),
                 inc=sg, incn=16)

    lam_c = V_LAMT[0:16, 512:513]
    lam_t = V_LAMT[0:16, 516:517]
    p1 = V_LAMT[0:16, 640:768]
    vop("tensor_tensor", p1, lamt_q1, lamt_k1, ALU.mult,
        waits=[(sd, d_l), (sg, g_init)])
    vop("reduce_sum", lam_c, p1, axis=AX.X)
    vop("tensor_tensor", p1, lamt_q2, lamt_k2, ALU.mult)
    v_l1 = vop("reduce_sum", lam_t, p1, axis=AX.X)
    b.op("scalar", "activation", lam_c, lam_c, AF.Exp, bias=0.0, scale=1.0,
         waits=[(sv, v_l1)])
    a_l2 = b.op("scalar", "activation", lam_t, lam_t, AF.Exp, bias=0.0,
                scale=1.0, inc=sa)
    vop("tensor_tensor", lam_c, lam_c, lam_t, ALU.subtract,
        waits=[(sa, a_l2)])
    vop("tensor_tensor", lam_c, lam_c, LI16[0:16, :], ALU.add)
    v_lam = vop("tensor_tensor", lam_c, ZERO1[0:16, :], lam_c, ALU.subtract)
    pe_open(PE_)
    b.op("tensor", "transpose", PE_.t[0:1, 0:16], lam_c, IDF[:, :],
         waits=[(sv, v_lam), (sg, g_init)], inc=sp)
    pe_close(PE_)
    consume(PE_, "vector")
    v_lr = b.op("vector", "tensor_copy", LAMROW, PE_.t[0:1, 0:16], inc=sv)
    consumed(PE_, "vector")
    pe_open(PE_)
    b.op("tensor", "matmul", PE_.t[:, 0:16], ONESROW, LAMROW,
         start=True, stop=True, waits=[(sv, v_lr)], inc=sp)
    pe_close(PE_)
    consume(PE_, "vector")
    v_lambc = b.op("vector", "tensor_copy", LAMBC[:, :], PE_.t[:, 0:16],
                   inc=sv)
    consumed(PE_, "vector")
    pe_open(PF)
    b.op("tensor", "matmul", PF.t[:, 0:16], ONESROW, EBROW,
         start=True, stop=True, waits=[(sd, d_eb)], inc=sp)
    pe_close(PF)
    consume(PF, "vector")
    v_ebrep = b.op("vector", "tensor_copy", EBREP[:, :], PF.t[:, 0:16],
                   inc=sv)
    consumed(PF, "vector")
    pe_open(PC)
    b.op("tensor", "matmul", PC.t[:, 0:128], ONESROW, RMSROW,
         start=True, stop=True, waits=[(sd, d_rms)], inc=sp)
    pe_close(PC)
    consume(PC, "vector")
    v_rms = b.op("vector", "tensor_scalar", RMSREP[:, :], PC.t[:, 0:128],
                 1.0 - LAM_INIT, None, ALU.mult, inc=sv)
    consumed(PC, "vector")

    for tt in range(NTT):
        g_tr = b.op("gpsimd", "iota", TOKR, [[1, 128]], base=tt * 128,
                    channel_multiplier=0,
                    allow_small_or_imprecise_dtypes=True, inc=sg,
                    waits=[(sa, sa.v)])
        bank = PE_ if tt % 2 == 0 else PF
        pe_open(bank)
        b.op("tensor", "matmul", bank.t[:, 0:128], TOKR, THROW,
             start=True, stop=True, waits=[(sg, g_tr), (sv, v_th)], inc=sp)
        pe_close(bank)
        consume(bank, "vector")
        b.op("vector", "tensor_scalar", V_TRA, bank.t[:, 0:128], MAGIC,
             MAGIC, ALU.add, ALU.subtract)
        b.op("vector", "tensor_tensor", trig_sin(tt), bank.t[:, 0:128],
             V_TRA, ALU.subtract)
        b.op("vector", "tensor_scalar", V_TRB, bank.t[:, 0:128], 0.25, None,
             ALU.add)
        consumed(bank, "vector")
        b.op("vector", "tensor_scalar", V_TRA, V_TRB, MAGIC, MAGIC,
             ALU.add, ALU.subtract)
        v_tr = b.op("vector", "tensor_tensor", trig_cos(tt), V_TRB, V_TRA,
                    ALU.subtract, inc=sv)
        b.op("scalar", "activation", trig_sin(tt), trig_sin(tt), AF.Sin,
             bias=0.0, scale=2 * math.pi, waits=[(sv, v_tr)])
        b.op("scalar", "activation", trig_cos(tt), trig_cos(tt),
             AF.Sin, bias=0.0, scale=2 * math.pi, inc=sa)
    trig_done = sa.v

    v_xt_guard = sv.v
    a_xt_guard = trig_done
    x_guard_sp = sp.v
    b.wait("vector", (sg, g_init))
    b.wait("scalar", (sg, g_init))
    b.wait("tensor", (sg, g_init))
    for tt in range(NTT if phases >= 1 else 0):
        half_a = tt < HALF
        d_x = b.op("sync", "dma_start", out=V_XT,
                   in_=bass.AP(x_h, tt * 128 * E, [[E, 128], [1, E]]),
                   waits=[(sv, v_xt_guard), (sa, a_xt_guard),
                          (sp, x_guard_sp)],
                   inc=sd, incn=16)
        v_m = vop("reduce_sum", C_A, V_XT, axis=AX.X, waits=[(sd, d_x)])
        b.op("scalar", "mul", C_B, C_A, 1.0 / E, waits=[(sv, v_m)])
        a_xt_guard = b.op("scalar", "activation", V_XNT, V_XT, AF.Square,
                          bias=0.0, scale=1.0, accum_out=C_C, inc=sa)
        vop("tensor_tensor", C_D, C_B, C_B, ALU.mult,
            waits=[(sa, a_xt_guard)])
        v_s = vop("tensor_tensor", C_E, EPS5C, C_D, ALU.subtract)
        a_v = b.op("scalar", "activation", C_G, C_C, AF.Sqrt, bias=C_E,
                   scale=1.0 / E, waits=[(sv, v_s)], inc=sa)
        vop("reciprocal", C_H, C_G, waits=[(sa, a_v)])
        vop("tensor_tensor", C_I, C_B, C_H, ALU.mult)
        vop("tensor_tensor", C_J, ZERO1, C_I, ALU.subtract)
        v_xn = b.op("vector", "tensor_scalar", V_XNT, V_XT, C_H, C_J,
                    ALU.mult, ALU.add, inc=sv)
        v_xt_guard = v_xn
        for grp in range(4):
            tb = PT1 if grp % 2 == 0 else PT2
            pe_open(tb)
            for j in range(4):
                ec = grp * 4 + j
                b.op("tensor", "transpose", tb.t[:, j * 128:(j + 1) * 128],
                     V_XNT[:, ec * 128:(ec + 1) * 128], IDB[:, :],
                     waits=[(sv, v_xn)] if j == 0 else (),
                     inc=sp if j == 3 else None)
            pe_close(tb)
            consume(tb, "vector")
            if half_a:
                dst = XNH3[:, grp * 4:grp * 4 + 4, tt * 128:(tt + 1) * 128]
            else:
                dst = XNB3[:, grp * 4:grp * 4 + 4,
                           (tt - HALF) * 128:(tt - HALF + 1) * 128]
            b.op("vector", "tensor_copy", dst,
                 tb.t.ap().rearrange("p (c n) -> p c n", n=128)[:, 0:4, :], inc=sv)
            consumed(tb, "vector")
        v_tiles = sv.v
        src3 = (XNH3 if half_a else XNB3)[
            :, :, (tt % HALF) * 128:(tt % HALF + 1) * 128]
        b.op("sync", "dma_start",
             out=bass.AP(xn_hbm, tt * 128,
                         [[16 * T, 128], [T, 16], [1, 128]]),
             in_=src3, waits=[(sv, v_tiles)], inc=sd, incn=16)
        lbank = PE_ if tt % 2 == 0 else PF
        pe_open(lbank)
        for ec in range(16):
            b.op("tensor", "matmul", lbank.t[:, 0:16],
                 xn_tile(ec, tt), GWT[:, ec * 16:(ec + 1) * 16],
                 start=(ec == 0), stop=(ec == 15),
                 waits=([(sv, v_tiles), (sg, d_gwt)] if ec == 0 else ()),
                 inc=sp if ec == 15 else None)
        pe_close(lbank)
        consume(lbank, "scalar")
        a_sig = b.op("scalar", "activation", R_SCO, lbank.t[:, 0:16],
                     AF.Sigmoid, bias=0.0, scale=1.0, inc=sa)
        consumed(lbank, "scalar")
        vop("tensor_tensor", R_BIA, R_SCO, EBREP[:, :], ALU.add,
            waits=[(sa, a_sig), (sv, v_ebrep)])
        A0 = RW.ap()[:, 16:32][:, 0::4]
        A1 = RW.ap()[:, 16:32][:, 1::4]
        A2 = RW.ap()[:, 16:32][:, 2::4]
        A3 = RW.ap()[:, 16:32][:, 3::4]
        T_M1 = RW.ap()[:, 96:100]
        T_m1 = RW.ap()[:, 100:104]
        T_M2 = RW.ap()[:, 104:108]
        T_m2 = RW.ap()[:, 108:112]
        vop("tensor_tensor", T_M1, A0, A1, ALU.max)
        vop("tensor_tensor", T_m1, A0, A1, ALU.min)
        vop("tensor_tensor", T_M2, A2, A3, ALU.max)
        vop("tensor_tensor", T_m2, A2, A3, ALU.min)
        vop("tensor_tensor", R_GS, T_M1, T_M2, ALU.max)
        vop("tensor_tensor", T_m1, T_m1, T_m2, ALU.max)
        vop("tensor_tensor", T_M1, T_M1, T_M2, ALU.min)
        vop("tensor_tensor", T_M1, T_M1, T_m1, ALU.max)
        vop("tensor_tensor", R_GS, R_GS, T_M1, ALU.add)
        g0 = R_GS[:, 0:1]; g1 = R_GS[:, 1:2]
        g2 = R_GS[:, 2:3]; g3 = R_GS[:, 3:4]
        c1 = RW.ap()[:, 96:97]; c2 = RW.ap()[:, 97:98]
        c3 = RW.ap()[:, 98:99]; c4 = RW.ap()[:, 99:100]
        vop("tensor_tensor", c1, g0, g1, ALU.max)
        vop("tensor_tensor", c2, g0, g1, ALU.min)
        vop("tensor_tensor", c3, g2, g3, ALU.max)
        vop("tensor_tensor", c4, g2, g3, ALU.min)
        vop("tensor_tensor", c2, c2, c4, ALU.max)
        vop("tensor_tensor", c1, c1, c3, ALU.min)
        vop("tensor_tensor", R_T2, c1, c2, ALU.max)
        vop("tensor_single_scalar", R_GM, R_GS, R_T2, ALU.is_ge)
        vop("tensor_scalar", R_GMN, R_GM, 1.0, BIG, ALU.subtract, ALU.mult)
        for g in range(4):
            vop("tensor_single_scalar", R_MSK[:, 4 * g:4 * g + 4],
                R_BIA[:, 4 * g:4 * g + 4], R_GMN[:, g:g + 1], ALU.add)
        vop("tensor_copy", R_MSK2, R_MSK)
        for kext in range(4):
            vop("reduce_max", R_T4, R_MSK2, axis=AX.X)
            if kext < 3:
                vop("tensor_single_scalar", R_EQ, R_MSK2, R_T4, ALU.is_ge)
                vop("tensor_scalar", R_EQ, R_EQ, -BIG, None, ALU.mult)
                vop("tensor_tensor", R_MSK2, R_MSK2, R_EQ, ALU.add)
        vop("tensor_single_scalar", R_SEL, R_MSK, R_T4, ALU.is_ge)
        vop("tensor_tensor", R_SEL, R_SEL, R_SCO, ALU.mult)
        vop("reduce_sum", R_WS, R_SEL, axis=AX.X)
        vop("tensor_tensor", R_WS, R_WS, EPS20C, ALU.add)
        vop("reciprocal", R_WI, R_WS)
        b.op("vector", "tensor_scalar", CW.ap()[:, tt * 16:(tt + 1) * 16],
             R_SEL, R_WI, 2.5, ALU.mult, ALU.mult, inc=sv)
    xn_all_v = sv.v

    wst_free = [None, None]

    def load_slab(par, src_h, col0, ncols=512):
        w = [] if wst_free[par] is None else [wst_free[par]]
        return b.op("gpsimd", "dma_start", out=wst3(par)[:, :, 0:ncols],
                    in_=bass.AP(src_h, col0,
                                [[src_h.shape[1], 128],
                                 [128 * src_h.shape[1], 16], [1, ncols]]),
                    waits=w, inc=sg, incn=16)

    wst_free[0] = (sp, sp.v)
    d_wv = load_slab(0, wv_h, 0) if phases >= 2 else 0
    for tt in range(NTT if phases >= 2 else 0):
        bank = PC if tt % 2 == 0 else PD
        pe_open(bank)
        for ec in range(16):
            b.op("tensor", "matmul", bank.t[:, :], xn_tile(ec, tt),
                 wst(0, ec), start=(ec == 0), stop=(ec == 15),
                 waits=([(sg, d_wv), (sv, xn_all_v)] if ec == 0 else ()),
                 inc=sp if ec == 15 else None)
        pe_close(bank)
        consume(bank, "scalar")
        b.op("scalar", "activation", VTOK.ap()[:, tt * 512:(tt + 1) * 512],
             bank.t[:, :], AF.Copy, bias=0.0, scale=1.0, inc=sa)
        consumed(bank, "scalar")
    wst_free[0] = (sp, sp.v)
    vtok_done = sa.v

    def rope_emit(bank, tt, out_stage):
        last = None
        for hs in range(2):
            base = hs * 256
            re = bank.t[:, base:base + 256:2]
            im = bank.t[:, base + 1:base + 256:2]
            sre = out_stage[:, base:base + 256:2]
            sim = out_stage[:, base + 1:base + 256:2]
            b.op("vector", "tensor_tensor", V_RT1, re, trig_cos(tt),
                 ALU.mult)
            b.op("vector", "tensor_tensor", V_RT2, im, trig_sin(tt),
                 ALU.mult)
            b.op("vector", "tensor_tensor", sre, V_RT1, V_RT2, ALU.subtract)
            b.op("vector", "tensor_tensor", V_RT3, re, trig_sin(tt),
                 ALU.mult)
            b.op("vector", "tensor_tensor", V_RT4, im, trig_cos(tt),
                 ALU.mult)
            last = b.op("vector", "tensor_tensor", sim, V_RT3, V_RT4,
                        ALU.add, inc=sv if hs == 1 else None)
        return last

    stg0_guard = 0
    for kfc in range(2 if phases >= 2 else 0):
        par = kfc % 2
        d_wk = load_slab(par, wk_h, kfc * 512)
        for tt in range(NTT):
            bank = PC if tt % 2 == 0 else PD
            pe_open(bank)
            for ec in range(16):
                b.op("tensor", "matmul", bank.t[:, :], xn_tile(ec, tt),
                     wst(par, ec), start=(ec == 0), stop=(ec == 15),
                     waits=([(sg, d_wk)] if ec == 0 else ()),
                     inc=sp if ec == 15 else None)
            pe_close(bank)
            consume(bank, "vector")
            b.wait("vector", (sa, trig_done), (sp, stg0_guard))
            v_r = rope_emit(bank, tt, V_STG0)
            consumed(bank, "vector")
            tb = PT1 if tt % 2 == 0 else PT2
            pe_open(tb)
            for j in range(4):
                b.op("tensor", "transpose", tb.t[:, j * 128:(j + 1) * 128],
                     V_STG0[:, j * 128:(j + 1) * 128], IDB[:, :],
                     waits=[(sv, v_r)] if j == 0 else (),
                     inc=sp if j == 3 else None)
            pe_close(tb)
            stg0_guard = sp.v
            consume(tb, "vector")
            dst = KFM.ap().rearrange("p (c n) -> p c n", n=T)[
                :, kfc * 4:kfc * 4 + 4, tt * 128:(tt + 1) * 128]
            b.op("vector", "tensor_copy", dst,
                 tb.t.ap().rearrange("p (c n) -> p c n", n=128)[:, 0:4, :], inc=sv)
            consumed(tb, "vector")
        wst_free[par] = (sp, sp.v)
    kfm_done = sv.v

    q_stg_dma = {0: 0, 1: 0}
    q_stg_sp = {0: stg0_guard, 1: 0}
    for qfc in range(8 if phases >= 2 else 0):
        par = qfc % 2
        d_wq = load_slab(par, wq_h, qfc * 512)
        for tt in range(NTT):
            bank = PC if tt % 2 == 0 else PD
            pe_open(bank)
            for ec in range(16):
                b.op("tensor", "matmul", bank.t[:, :], xn_tile(ec, tt),
                     wst(par, ec), start=(ec == 0), stop=(ec == 15),
                     waits=([(sg, d_wq)] if ec == 0 else ()),
                     inc=sp if ec == 15 else None)
            pe_close(bank)
            consume(bank, "vector")
            sid = tt % 2
            stg = V_STG0 if sid == 0 else V_STG1
            b.wait("vector", (sd, q_stg_dma[sid]), (sp, q_stg_sp[sid]))
            v_r = rope_emit(bank, tt, stg)
            consumed(bank, "vector")
            tb = PT1 if tt % 2 == 0 else PT2
            pe_open(tb)
            for j in range(4):
                b.op("tensor", "transpose", tb.t[:, j * 128:(j + 1) * 128],
                     stg[:, j * 128:(j + 1) * 128], IDB[:, :],
                     waits=[(sv, v_r)] if j == 0 else (),
                     inc=sp if j == 3 else None)
            pe_close(tb)
            q_stg_sp[sid] = sp.v
            consume(tb, "vector")
            v_q = b.op("vector", "tensor_copy",
                       stg.rearrange("p (c n) -> p c n", n=128),
                       tb.t.ap().rearrange("p (c n) -> p c n", n=128)[:, 0:4, :],
                       inc=sv)
            consumed(tb, "vector")
            c0 = qfc * 4
            q_stg_dma[sid] = b.op(
                "sync", "dma_start",
                out=bass.AP(q_hbm, c0 * T + tt * 128,
                            [[32 * T, 128], [T, 4], [1, 128]]),
                in_=stg.rearrange("p (c n) -> p c n", n=128),
                waits=[(sv, v_q)], inc=sd, incn=16)
        wst_free[par] = (sp, sp.v)
    q_spill_done = sd.v

    att_guard = [0, 0]

    def attn_half(hb):
        entry_sp = sp.v
        entry_sv = sv.v
        att_guard[0] = max(att_guard[0], entry_sp)
        att_guard[1] = max(att_guard[1], entry_sp)
        acc_init = {}
        for tl in range(HALF):
            tt = hb * HALF + tl
            acc_init[tl] = b.op(
                "sync", "dma_start",
                out=ACC.ap()[:, tl * E:(tl + 1) * E],
                in_=bass.AP(x_h, tt * 128 * E, [[E, 128], [1, E]]),
                waits=([(sd, q_spill_done), (sp, sp.v)]
                       if (hb == 0 and tl == 0) else ()),
                inc=sd, incn=16)
        qb_guard = entry_sp
        v_u_prev = entry_sv
        epi_prev = entry_sv
        pt_sp_prev = entry_sp
        for qi in range(min(HALF, ATT_QMAX)):
            qt = hb * HALF + qi
            kn = qt + 1
            span = kn * 128
            nch = (span + 511) // 512
            d_qb = b.op("sync", "dma_start",
                        out=A_QB.rearrange("p (c n) -> p c n", n=128),
                        in_=bass.AP(q_hbm, qt * 128,
                                    [[32 * T, 128], [T, 32], [1, 128]]),
                        waits=[(sp, qb_guard)], inc=sd, incn=16)
            for h in range(H if ATT_SUB >= 1 else 0):
                kvh = h // 4
                ebuf = [A_E0, A_E1]
                dcols = [R_D01, R_D02]
                for a_ in range(2):
                    lhsT = A_QB[:, (2 * h + a_) * 128:(2 * h + a_ + 1) * 128]
                    kc = kvh * 2 + a_
                    for ch in range(nch):
                        w = min(512, span - ch * 512)
                        bank = PA if ch % 2 == 0 else PB
                        is_last = ch == nch - 1
                        pe_open(bank)
                        b.op("tensor", "matmul", bank.t[:, 0:w], lhsT,
                             KFM.ap()[:, kc * T + ch * 512:
                                      kc * T + ch * 512 + w],
                             start=True, stop=not is_last,
                             waits=([(sd, d_qb), (sv, kfm_done)]
                                    if ch == 0 else ()),
                             inc=None if is_last else sp)
                        if is_last:
                            b.op("tensor", "matmul", bank.t[:, w - 128:w],
                                 IDB[:, :], TRIB[:, :], start=False,
                                 stop=True, inc=sp)
                        pe_close(bank)
                        consume(bank, "scalar")
                        if a_ == 0:
                            b.wait("scalar", (sp, pt_sp_prev))
                        else:
                            b.wait("scalar", (sv, v_u_prev))
                        b.op("scalar", "activation",
                             ebuf[a_][:, ch * 512:ch * 512 + w],
                             bank.t[:, 0:w], AF.Exp, bias=0.0,
                             scale=ESCALE,
                             accum_out=dcols[a_][:, ch:ch + 1], inc=sa)
                        consumed(bank, "scalar")
                    dcol = (H_D0 if a_ == 0 else H_D1)[:, h:h + 1]
                    vop("reduce_sum", dcol, dcols[a_][:, 0:nch],
                        axis=AX.X, waits=[(sa, sa.v)])
                if ATT_SUB < 2:
                    continue
                b.op("vector", "reciprocal", C_A, H_D0[:, h:h + 1],
                     inc=sv)
                v_rb = b.op("vector", "reciprocal", C_B, H_D1[:, h:h + 1],
                            inc=sv)
                b.wait("vector", (sv, v_rb))
                vop("tensor_tensor", C_B, C_B, LAMBC[:, h:h + 1], ALU.mult,
                    waits=[(sv, v_lambc)])
                b.wait("vector", (sp, pt_sp_prev))
                vop("tensor_single_scalar", A_E0[:, 0:span],
                    A_E0[:, 0:span], C_A, ALU.mult)
                dsp = span - 128
                vop("tensor_tensor", A_E0[:, dsp:span],
                    A_E0[:, dsp:span], TRIB[:, :], ALU.add)
                v_u = b.op("vector", "scalar_tensor_tensor", A_E0[:, 0:span],
                           A_E1[:, 0:span], C_B, A_E0[:, 0:span], ALU.mult,
                           ALU.add, inc=sv)
                v_u_prev = v_u
                a_p = b.op("scalar", "activation", A_E0[:, 0:span],
                           A_E0[:, 0:span], AF.Exp, bias=0.0, scale=1.0,
                           accum_out=H_D2[:, h:h + 1], waits=[(sv, v_u)],
                           inc=sa)
                if ATT_SUB < 3:
                    continue
                for grp in range((kn + 3) // 4):
                    tb = PT1 if grp % 2 == 0 else PT2
                    nj = min(4, kn - grp * 4)
                    pe_open(tb)
                    for j in range(nj):
                        kt = grp * 4 + j
                        b.op("tensor", "transpose",
                             tb.t[:, j * 128:(j + 1) * 128],
                             A_E0[:, kt * 128:(kt + 1) * 128], IDB[:, :],
                             waits=[(sa, a_p)] if j == 0 else (),
                             inc=sp if j == nj - 1 else None)
                    pe_close(tb)
                    consume(tb, "vector")
                    b.op("vector", "tensor_copy",
                         A_PT.rearrange("p (c n) -> p c n", n=128)[
                             :, grp * 4:grp * 4 + nj, :],
                         tb.t.ap().rearrange("p (c n) -> p c n", n=128)[
                             :, 0:nj, :], inc=sv)
                    consumed(tb, "vector")
                pt_sp_prev = sp.v
                pt_done = sv.v
                ab = PE_ if h % 2 == 0 else PF
                pe_open(ab)
                for kt in range(kn):
                    b.op("tensor", "matmul", ab.t[:, 0:128],
                         A_PT[:, kt * 128:(kt + 1) * 128],
                         VTOK.ap()[:, kt * 512 + kvh * 128:
                                   kt * 512 + kvh * 128 + 128],
                         start=(kt == 0), stop=(kt == kn - 1),
                         waits=([(sv, pt_done), (sa, vtok_done)]
                                if kt == 0 else ()),
                         inc=sp if kt == kn - 1 else None)
                pe_close(ab)
                if ATT_SUB < 4:
                    continue
                consume(ab, "scalar")
                b.wait("scalar", (sv, epi_prev))
                b.wait("scalar", (sp, att_guard[0]))
                b.op("scalar", "activation", a_att(h), ab.t[:, 0:128],
                     AF.Square, bias=0.0, scale=1.0,
                     accum_out=HSQ.ap()[:, h:h + 1])
                b.op("scalar", "activation",
                     a_att(h), ab.t[:, 0:128],
                     AF.Copy, bias=0.0, scale=1.0, inc=sa)
                consumed(ab, "scalar")
            heads_a = sa.v
            if ATT_SUB < 5:
                continue
            b.wait("vector", (sa, heads_a))
            v_r3 = b.op("vector", "reciprocal", A_T3, H_D2, inc=sv)
            b.wait("vector", (sv, v_r3))
            vop("tensor_tensor", A_T4, A_T3, A_T3, ALU.mult)
            v_t = vop("tensor_tensor", A_T1, HSQ.ap()[:, :], A_T4, ALU.mult)
            a_sq = b.op("scalar", "activation", A_T2, A_T1, AF.Sqrt,
                        bias=EPS8C, scale=1.0 / 128.0,
                        waits=[(sv, v_t)], inc=sa)
            v_r2 = b.op("vector", "reciprocal", A_T2, A_T2,
                        waits=[(sa, a_sq)], inc=sv)
            b.wait("vector", (sv, v_r2))
            v_sc = vop("tensor_tensor", H_SC, A_T2, A_T3, ALU.mult)
            stg_wait = sp.v
            for h4 in range(4):
                tb = PT1 if h4 % 2 == 0 else PT2
                b.wait("vector", (sp, stg_wait))
                v_stg = None
                for j in range(4):
                    h = h4 * 4 + j
                    v_stg = b.op("vector", "scalar_tensor_tensor",
                                 A_STG[:, j * 128:(j + 1) * 128],
                                 a_att(h),
                                 H_SC[:, h:h + 1], RMSREP[:, :],
                                 ALU.mult, ALU.mult,
                                 waits=([(sv, v_sc), (sv, v_rms)]
                                        if j == 0 else ()),
                                 inc=sv if j == 3 else None)
                pe_open(tb)
                for j in range(4):
                    b.op("tensor", "transpose",
                         tb.t[:, j * 128:(j + 1) * 128],
                         A_STG[:, j * 128:(j + 1) * 128], IDB[:, :],
                         waits=[(sv, v_stg)] if j == 0 else (),
                         inc=sp if j == 3 else None)
                pe_close(tb)
                pt_sp_prev = sp.v
                stg_wait = sp.v
                consume(tb, "vector")
                b.wait("vector", (sp, att_guard[0]))
                b.op("vector", "tensor_copy",
                     a_att4(h4 * 4).rearrange("p (c n) -> p c n", n=128),
                     tb.t.ap().rearrange("p (c n) -> p c n", n=128)[:, 0:4, :],
                     inc=sv)
                consumed(tb, "vector")
            epi_prev = sv.v
            qb_guard = sp.v

            if ATT_SUB < 6:
                continue
            att_v = sv.v
            for eg in range(4):
                par = eg % 2
                w = [] if wst_free[par] is None else [wst_free[par]]
                d_wo = b.op("gpsimd", "dma_start", out=wst3(par),
                            in_=bass.AP(wo_h, eg * 512,
                                        [[E, 128], [128 * E, 16],
                                         [1, 512]]),
                            waits=w, inc=sg, incn=16)
                obank = PC if eg % 2 == 0 else PD
                pe_open(obank)
                for hd in range(16):
                    b.op("tensor", "matmul", obank.t[:, :],
                         a_att(hd), wst(par, hd),
                         start=(hd == 0), stop=(hd == 15),
                         waits=([(sg, d_wo), (sv, att_v)]
                                if hd == 0 else ()),
                         inc=sp if hd == 15 else None)
                pe_close(obank)
                consume(obank, "vector")
                dst = ACC.ap()[:, qi * E + eg * 512:
                               qi * E + eg * 512 + 512]
                b.op("vector", "tensor_tensor", dst, dst,
                     obank.t[:, :], ALU.add,
                     waits=[(sd, acc_init[qi])], inc=sv)
                consumed(obank, "vector")
                wst_free[par] = (sp, sp.v)
            att_guard[0] = sp.v
            att_guard[1] = sp.v
        return acc_init

    def ffn_half(hb, acc_init, xnh_ready):
        def ffn_load_gu(par, ei, ib):
            w = [] if wst_free[par] is None else [wst_free[par]]
            if ei < NEXP:
                gsrc = bass.AP(gt_h, ei * E * I_FF + ib * 128,
                               [[I_FF, 128], [128 * I_FF, 16], [1, 128]])
                usrc = bass.AP(ut_h, ei * E * I_FF + ib * 128,
                               [[I_FF, 128], [128 * I_FF, 16], [1, 128]])
            else:
                gsrc = bass.AP(sgt_h, ib * 128,
                               [[I_FF, 128], [128 * I_FF, 16], [1, 128]])
                usrc = bass.AP(sut_h, ib * 128,
                               [[I_FF, 128], [128 * I_FF, 16], [1, 128]])
            b.op("gpsimd", "dma_start", out=wst_g3(par), in_=gsrc, waits=w,
                 inc=sg, incn=16)
            return b.op("gpsimd", "dma_start", out=wst_u3(par), in_=usrc,
                        inc=sg, incn=16)

        dtc_free = [(sp, sp.v), (sp, sp.v)]

        def ffn_load_dtc(par, ei, hc, ibh):
            w = [dtc_free[par]]
            if ei < NEXP:
                src = bass.AP(dt_h,
                              ei * I_FF * E + ibh * 4 * 128 * E + hc * 512,
                              [[E, 128], [128 * E, 4], [1, 512]])
            else:
                src = bass.AP(sdt_h, ibh * 4 * 128 * E + hc * 512,
                              [[E, 128], [128 * E, 4], [1, 512]])
            return b.op("gpsimd", "dma_start", out=f_dtc3(par), in_=src,
                        waits=w, inc=sg, incn=16)

        hf_guard = sp.v
        last_acc_v = {tl: 0 for tl in range(HALF)}
        pend_gu = {(0, 0): ffn_load_gu(0, 0, 0)}
        pend_dtc = {}
        for ei in range(NEXP + 1):
            for ib in range(8):
                par = (ei * 8 + ib) % 2
                d_w = pend_gu.pop((ei, ib))
                nei, nib = (ei, ib + 1) if ib < 7 else (ei + 1, 0)
                if nei <= NEXP:
                    pend_gu[(nei, nib)] = ffn_load_gu(1 - par, nei, nib)
                for tg in range(2):
                    pe_open(PA)
                    for ec in range(16):
                        b.op("tensor", "matmul", PA.t[:, :],
                             wst_g(par, ec), xn_rhs(ec, tg),
                             start=(ec == 0), stop=(ec == 15),
                             waits=([(sg, d_w), (sd, xnh_ready)]
                                    if ec == 0 else ()),
                             inc=sp if ec == 15 else None)
                    pe_close(PA)
                    pe_open(PB)
                    for ec in range(16):
                        b.op("tensor", "matmul", PB.t[:, :],
                             wst_u(par, ec), xn_rhs(ec, tg),
                             start=(ec == 0), stop=(ec == 15),
                             inc=sp if ec == 15 else None)
                    pe_close(PB)
                    consume(PA, "scalar")
                    a_sil = b.op("scalar", "activation", F_SIL, PA.t[:, :],
                                 AF.Silu, bias=0.0, scale=1.0, inc=sa)
                    consumed(PA, "scalar")
                    consume(PB, "vector")
                    b.wait("vector", (sp, hf_guard))
                    b.op("vector", "tensor_tensor",
                         F_HF[:, ib * 1024 + tg * 512:
                              ib * 1024 + tg * 512 + 512],
                         F_SIL, PB.t[:, :], ALU.mult,
                         waits=[(sa, a_sil)], inc=sv)
                    consumed(PB, "vector")
                wst_free[par] = (sp, sp.v)
            hf_v = sv.v
            pend_dtc[(ei, 0, 0)] = ffn_load_dtc(0, ei, 0, 0)
            pend_dtc[(ei, 0, 1)] = ffn_load_dtc(1, ei, 0, 1)
            banks4 = [PC, PD, PE_, PF]
            tt0 = hb * HALF
            for hc in range(4):
                d_dtc = {}
                for ibh in range(2):
                    par = (hc * 2 + ibh) % 2
                    d_dtc[ibh] = pend_dtc.pop((ei, hc, ibh))
                    nxt = (ei, hc, ibh + 1) if ibh == 0 else (ei, hc + 1, 0)
                    if nxt[1] < 4 and nxt not in pend_dtc:
                        pass
                for tlg in range(2):
                    for bk in banks4:
                        pe_open(bk)
                    for ib in range(8):
                        ibh = ib // 4
                        par = (hc * 2 + ibh) % 2
                        for t4 in range(4):
                            tl = tlg * 4 + t4
                            b.op("tensor", "matmul", banks4[t4].t[:, :],
                                 F_HF[:, ib * 1024 + tl * 128:
                                      ib * 1024 + tl * 128 + 128],
                                 f_dtc(par, ib % 4),
                                 start=(ib == 0), stop=(ib == 7),
                                 waits=([(sv, hf_v), (sg, d_dtc[ibh])]
                                        if t4 == 0 and ib % 4 == 0 else ()),
                                 inc=sp if (ib == 7) else None)
                    for bk in banks4:
                        pe_close(bk)
                    for t4 in range(4):
                        tl = tlg * 4 + t4
                        cwcol = (CW.ap()[:, (tt0 + tl) * 16 + ei:
                                         (tt0 + tl) * 16 + ei + 1]
                                 if ei < NEXP else ONEC)
                        bk = banks4[t4]
                        consume(bk, "vector")
                        dst = ACC.ap()[:, tl * E + hc * 512:
                                       tl * E + hc * 512 + 512]
                        last_acc_v[tl] = b.op(
                            "vector", "scalar_tensor_tensor", dst,
                            bk.t[:, :], cwcol, dst, ALU.mult, ALU.add,
                            waits=[(sd, acc_init[tl])], inc=sv)
                        consumed(bk, "vector")
                for ibh in range(2):
                    par = (hc * 2 + ibh) % 2
                    dtc_free[par] = (sp, sp.v)
                if hc < 3:
                    for ibh in range(2):
                        par = ((hc + 1) * 2 + ibh) % 2
                        pend_dtc[(ei, hc + 1, ibh)] = ffn_load_dtc(
                            par, ei, hc + 1, ibh)
            hf_guard = sp.v
        for tl in range(HALF):
            tt = hb * HALF + tl
            b.op("sync", "dma_start",
                 out=bass.AP(out_h, tt * 128 * E, [[E, 128], [1, E]]),
                 in_=ACC.ap()[:, tl * E:(tl + 1) * E],
                 waits=[(sv, last_acc_v[tl])], inc=sd, incn=16)

    if phases >= 3:
        acc_init = attn_half(0)
    if phases >= 4:
        ffn_half(0, acc_init, 0)
    if phases >= 5:
        xnh_b = b.op("sync", "dma_start",
                     out=XNH3[:, :, :],
                     in_=bass.AP(xn_hbm, HALF * 128,
                                 [[16 * T, 128], [T, 16], [1, 1024]]),
                     waits=[(sp, sp.v)], inc=sd, incn=16)
        acc_init = attn_half(1)
        ffn_half(1, acc_init, xnh_b)

    if dbg:
        cw_d = nc.dram_tensor("cw_dump", [128, NTT * NEXP], F32,
                              kind="ExternalOutput")
        kfm_d = nc.dram_tensor("kfm_dump", [128, 8 * T], BF16,
                               kind="ExternalOutput")
        vtok_d = nc.dram_tensor("vtok_dump", [128, NTT * 512], BF16,
                                kind="ExternalOutput")
        ovl_d = nc.dram_tensor("ovl_dump", [128, 12800], BF16,
                               kind="ExternalOutput")
        hst_d = nc.dram_tensor("hst_dump", [128, 64], F32,
                               kind="ExternalOutput")
        rw_d = nc.dram_tensor("rw_dump", [128, 112], F32,
                              kind="ExternalOutput")
        hsq_d = nc.dram_tensor("hsq_dump", [128, 16], F32,
                               kind="ExternalOutput")
        acc_d = nc.dram_tensor("acc_dump", [128, HALF * E], F32,
                               kind="ExternalOutput")
        b.wait("sync", (sv, sv.v), (sa, sa.v), (sp, sp.v))
        b.op("sync", "dma_start", out=cw_d.ap(), in_=CW.ap(), inc=sd,
             incn=16)
        b.op("sync", "dma_start", out=kfm_d.ap(), in_=KFM.ap(), inc=sd,
             incn=16)
        b.op("sync", "dma_start", out=vtok_d.ap(), in_=VTOK.ap(), inc=sd,
             incn=16)
        b.op("sync", "dma_start", out=ovl_d.ap(), in_=OVL.ap(), inc=sd,
             incn=16)
        b.op("sync", "dma_start", out=hst_d.ap(), in_=HST.ap(), inc=sd,
             incn=16)
        b.op("sync", "dma_start", out=rw_d.ap(), in_=RW.ap(), inc=sd,
             incn=16)
        b.op("sync", "dma_start", out=hsq_d.ap(), in_=HSQ.ap(), inc=sd,
             incn=16)
        b.op("sync", "dma_start", out=acc_d.ap(), in_=ACC.ap(), inc=sd,
             incn=16)
        lam_d = nc.dram_tensor("lam_dump", [128, 16], F32,
                               kind="ExternalOutput")
        eb_d = nc.dram_tensor("ebrep_dump", [128, 16], F32,
                              kind="ExternalOutput")
        rms_d = nc.dram_tensor("rms_dump", [128, 128], F32,
                               kind="ExternalOutput")
        rows_d = nc.dram_tensor("rows_dump", [1, 544], F32,
                                kind="ExternalOutput")
        b.op("sync", "dma_start", out=lam_d.ap(), in_=LAMBC.ap(), inc=sd,
             incn=16)
        b.op("sync", "dma_start", out=eb_d.ap(), in_=EBREP.ap(), inc=sd,
             incn=16)
        b.op("gpsimd", "dma_start", out=rms_d.ap(), in_=RMSREP.ap(), inc=sg,
             incn=16)
        b.op("sync", "dma_start", out=rows_d.ap(), in_=ROWS.ap(), inc=sd,
             incn=16)
    b.wait("sync", (sd, sd.v))
    b.wait("gpsimd", (sg, sg.v))

    with nc.Block() as block:
        b.replay(block)
    ctx.close()
    return nc


def kernel(**inputs):
    x = np.ascontiguousarray(np.asarray(inputs["L0_attn_x"], np.float32))
    base = {
        "wq": np.ascontiguousarray(inputs["L0_attn_Wq"], dtype=np.float32),
        "wk": np.ascontiguousarray(inputs["L0_attn_Wk"], dtype=np.float32),
        "wv": np.ascontiguousarray(inputs["L0_attn_Wv"], dtype=np.float32),
        "wo": np.ascontiguousarray(inputs["L0_attn_Wo"], dtype=np.float32),
        "lq1": np.ascontiguousarray(inputs["L0_attn_lq1"], dtype=np.float32),
        "lq2": np.ascontiguousarray(inputs["L0_attn_lq2"], dtype=np.float32),
        "lk1": np.ascontiguousarray(inputs["L0_attn_lk1"], dtype=np.float32),
        "lk2": np.ascontiguousarray(inputs["L0_attn_lk2"], dtype=np.float32),
        "rms": np.ascontiguousarray(
            np.asarray(inputs["L0_attn_rms_w"], np.float32).reshape(1, D)),
        "gwt": np.ascontiguousarray(
            np.asarray(inputs["L0_ffn_gate_weight"], np.float32).T),
        "eb": np.ascontiguousarray(
            np.asarray(inputs["L0_ffn_expert_bias"],
                       np.float32).reshape(1, NEXP)),
        "gt": np.ascontiguousarray(
            np.asarray(inputs["L0_ffn_gate_proj"],
                       np.float32).transpose(0, 2, 1)),
        "ut": np.ascontiguousarray(
            np.asarray(inputs["L0_ffn_up_proj"],
                       np.float32).transpose(0, 2, 1)),
        "dtp": np.ascontiguousarray(
            np.asarray(inputs["L0_ffn_down_proj"],
                       np.float32).transpose(0, 2, 1)),
        "sgt": np.ascontiguousarray(
            np.asarray(inputs["L0_ffn_shared_gate"], np.float32).T),
        "sut": np.ascontiguousarray(
            np.asarray(inputs["L0_ffn_shared_up"], np.float32).T),
        "sdt": np.ascontiguousarray(
            np.asarray(inputs["L0_ffn_shared_down"], np.float32).T),
    }
    nc = bass.Bass("TRN2", target_bir_lowering=False)
    build(nc)
    in_maps = []
    for c in range(8):
        m = dict(base)
        m["x"] = np.ascontiguousarray(x[c])
        in_maps.append(m)
    res = run_bass_kernel_spmd(nc, in_maps, core_ids=list(range(8)))
    out = np.stack([np.asarray(r["out"]) for r in res.results], axis=0)
    return out.astype(np.float32)


# version 02
# speedup vs baseline: 1.1197x; 1.1197x over previous
import math
from contextlib import ExitStack

import numpy as np

import concourse.bass as bass
import concourse.mybir as mybir
from concourse.bass_utils import run_bass_kernel_spmd

F32 = mybir.dt.float32
BF16 = mybir.dt.bfloat16
AF = mybir.ActivationFunctionType
ALU = mybir.AluOpType
AX = mybir.AxisListType

T = 2048
E = 2048
H = 16
KVH = 4
D = 128
NTT = 16
HALF = 8
I_FF = 1024
NEXP = 16
NEG = -1e30
MAGIC = 12582912.0
ESCALE = 1.0 / math.sqrt(128.0)
LAM_INIT = 0.2
BIG = 1e32

ENGS = ("sync", "gpsimd", "vector", "scalar", "tensor")
ATT_SUB = 99
ATT_QMAX = 99


class S:
    def __init__(self, h):
        self.h = h
        self.v = 0


class Bank:
    def __init__(self, t):
        self.t = t
        self.prod = None
        self.cons = []


class Builder:
    def __init__(self, nc):
        self.nc = nc
        self.rec = {k: [] for k in ENGS}
        self.waited = {}

    def _filter(self, eng, waits):
        wl = []
        for (s, v) in waits:
            if v <= 0:
                continue
            key = (eng, s.h.name)
            if self.waited.get(key, -1) >= v:
                continue
            assert v <= s.v, (
                f"wait on future value {v} > {s.v} of {s.h.name} on {eng}")
            self.waited[key] = v
            wl.append((s.h, v))
        return tuple(wl)

    def op(self, eng, method, *args, waits=(), inc=None, incn=1, **kw):
        wl = self._filter(eng, waits)
        ih = None
        ret = None
        if inc is not None:
            inc.v += incn
            ret = inc.v
            ih = inc.h

        def fn(e, method=method, args=args, kw=kw, wl=wl, ih=ih, iv=incn):
            for (h, v) in wl:
                e.wait_ge(h, v)
            ins = getattr(e, method)(*args, **kw)
            if ih is not None:
                ins.then_inc(ih, iv)

        self.rec[eng].append(fn)
        return ret

    def wait(self, eng, *waits):
        wl = self._filter(eng, waits)
        if not wl:
            return

        def fn(e, wl=wl):
            for (h, v) in wl:
                e.wait_ge(h, v)

        self.rec[eng].append(fn)

    def replay(self, block):
        recs = self.rec

        @block.sync
        def _(e):
            for fn in recs["sync"]:
                fn(e)

        @block.gpsimd
        def _(e):
            for fn in recs["gpsimd"]:
                fn(e)

        @block.vector
        def _(e):
            for fn in recs["vector"]:
                fn(e)

        @block.scalar
        def _(e):
            for fn in recs["scalar"]:
                fn(e)

        @block.tensor
        def _(e):
            for fn in recs["tensor"]:
                fn(e)


def build(nc: bass.Bass, dbg: bool = False, phases: int = 99):
    b = Builder(nc)
    ctx = ExitStack()
    e = ctx.enter_context

    x_h = nc.dram_tensor("x", [T, E], F32, kind="ExternalInput")
    wq_h = nc.dram_tensor("wq", [E, 4096], F32, kind="ExternalInput")
    wk_h = nc.dram_tensor("wk", [E, 1024], F32, kind="ExternalInput")
    wv_h = nc.dram_tensor("wv", [E, 512], F32, kind="ExternalInput")
    wo_h = nc.dram_tensor("wo", [E, E], F32, kind="ExternalInput")
    lq1_h = nc.dram_tensor("lq1", [H, D], F32, kind="ExternalInput")
    lq2_h = nc.dram_tensor("lq2", [H, D], F32, kind="ExternalInput")
    lk1_h = nc.dram_tensor("lk1", [H, D], F32, kind="ExternalInput")
    lk2_h = nc.dram_tensor("lk2", [H, D], F32, kind="ExternalInput")
    rms_h = nc.dram_tensor("rms", [1, D], F32, kind="ExternalInput")
    gwt_h = nc.dram_tensor("gwt", [E, NEXP], F32, kind="ExternalInput")
    eb_h = nc.dram_tensor("eb", [1, NEXP], F32, kind="ExternalInput")
    gt_h = nc.dram_tensor("gt", [NEXP, E, I_FF], F32, kind="ExternalInput")
    ut_h = nc.dram_tensor("ut", [NEXP, E, I_FF], F32, kind="ExternalInput")
    dt_h = nc.dram_tensor("dtp", [NEXP, I_FF, E], F32, kind="ExternalInput")
    sgt_h = nc.dram_tensor("sgt", [E, I_FF], F32, kind="ExternalInput")
    sut_h = nc.dram_tensor("sut", [E, I_FF], F32, kind="ExternalInput")
    sdt_h = nc.dram_tensor("sdt", [I_FF, E], F32, kind="ExternalInput")
    out_h = nc.dram_tensor("out", [T, E], F32, kind="ExternalOutput")

    skind = "ExternalOutput" if dbg else "Internal"
    xn_hbm = nc.dram_tensor("xn_hbm", [128, 16 * T], BF16, kind=skind)
    q_hbm = nc.dram_tensor("q_hbm", [128, 32 * T], BF16, kind=skind)

    ACC = e(nc.sbuf_tensor("ACC", [128, 4 * E], F32))
    XNH = e(nc.sbuf_tensor("XNH", [128, 16 * 512], BF16))
    KFM = e(nc.sbuf_tensor("KFM", [128, 8 * T], BF16))
    VTOK = e(nc.sbuf_tensor("VTOK", [128, NTT * 512], BF16))
    CW = e(nc.sbuf_tensor("CWb", [128, NTT * NEXP], F32))
    WST = e(nc.sbuf_tensor("WST", [128, 2 * 8192], BF16))
    OVL = e(nc.sbuf_tensor("OVL", [128, 35328], BF16))
    TRIB = e(nc.sbuf_tensor("TRIB", [128, 128], BF16))
    IDB = e(nc.sbuf_tensor("IDB", [128, 128], BF16))
    IDF = e(nc.sbuf_tensor("IDF", [16, 16], F32))
    RMSREP = e(nc.sbuf_tensor("RMSREP", [128, 128], BF16))
    LAMBC = e(nc.sbuf_tensor("LAMBC", [128, 16], F32))
    EBREP = e(nc.sbuf_tensor("EBREP", [128, 16], F32))
    ROWS = e(nc.sbuf_tensor("ROWS", [1, 544], F32))
    COLS = e(nc.sbuf_tensor("COLS", [128, 16], F32))
    RW = e(nc.sbuf_tensor("RW", [128, 112], F32))
    HSQ = e(nc.sbuf_tensor("HSQ", [128, 16], F32))
    HST = e(nc.sbuf_tensor("HST", [128, 64], F32))

    ONESROW = ROWS.ap()[:, 0:128]
    THROW = ROWS.ap()[:, 128:256]
    TOKR = ROWS.ap()[:, 256:384]
    RMSROW = ROWS.ap()[:, 384:512]
    LAMROW = ROWS.ap()[:, 512:528]
    EBROW = ROWS.ap()[:, 528:544]

    ZERO1 = COLS.ap()[:, 0:1]
    EPS5C = COLS.ap()[:, 1:2]
    EPS8C = COLS.ap()[:, 2:3]
    EPS20C = COLS.ap()[:, 3:4]
    ONEC = COLS.ap()[:, 4:5]
    LI16 = COLS.ap()[:, 5:6]
    C_A = COLS.ap()[:, 6:7]
    C_B = COLS.ap()[:, 7:8]
    C_C = COLS.ap()[:, 8:9]
    C_D = COLS.ap()[:, 9:10]
    C_E = COLS.ap()[:, 10:11]
    C_F = COLS.ap()[:, 11:12]
    C_G = COLS.ap()[:, 12:13]
    C_H = COLS.ap()[:, 13:14]
    C_I = COLS.ap()[:, 14:15]
    C_J = COLS.ap()[:, 15:16]

    H_D0 = HST.ap()[:, 0:16]
    H_D1 = HST.ap()[:, 16:32]
    H_D2 = HST.ap()[:, 32:48]
    H_SC = HST.ap()[:, 48:64]

    R_SCO = RW.ap()[:, 0:16]
    R_BIA = RW.ap()[:, 16:32]
    R_GS = RW.ap()[:, 32:36]
    R_GM = RW.ap()[:, 36:40]
    R_GMN = RW.ap()[:, 40:44]
    R_MSK = RW.ap()[:, 48:64]
    R_SEL = RW.ap()[:, 64:80]
    R_PAD = RW.ap()[:, 80:88]
    R_MX8 = RW.ap()[:, 88:96]
    R_T2 = RW.ap()[:, 44:45]
    R_T4 = RW.ap()[:, 45:46]
    R_WS = RW.ap()[:, 46:47]
    R_WI = RW.ap()[:, 47:48]
    R_D01 = RW.ap()[:, 100:104]
    R_D02 = RW.ap()[:, 104:108]
    R_MSK2 = RW.ap()[:, 80:96]
    R_EQ = RW.ap()[:, 96:112]
    A_T1 = RW.ap()[:, 0:16]
    A_T2 = RW.ap()[:, 16:32]
    A_T3 = RW.ap()[:, 32:48]
    A_T4 = RW.ap()[:, 48:64]

    def bfv(off, n):
        return OVL.ap()[:, off:off + n]

    def f32v(off, n):
        return OVL.ap()[:, off:off + 2 * n].bitcast(F32)

    XNA = bfv(0, 16384)
    XNA3 = XNA.rearrange("p (c n) -> p c n", n=1024)
    V_XT = f32v(16384, 2048)
    V_XNT = bfv(20480, 2048)
    TRIG_OFF = 22528

    def trig_sin(tt):
        return bfv(TRIG_OFF + tt * 256, 128)

    def trig_cos(tt):
        return bfv(TRIG_OFF + tt * 256 + 128, 128)

    V_RT1 = f32v(26624, 128)
    V_RT2 = f32v(26880, 128)
    V_RT3 = f32v(27136, 128)
    V_RT4 = f32v(27392, 128)
    V_STG0 = bfv(27648, 512)
    V_STG1 = bfv(28160, 512)
    V_TRA = f32v(28672, 128)
    V_TRB = f32v(28928, 128)
    V_LAMT = f32v(29184, 1024)

    def a_qb(par):
        return bfv(0, 4096)

    A_PT = bfv(4096, 2048)
    A_STG = bfv(4096, 512)

    def a_e0(ep):
        return bfv(6144 + ep * 4096, 2048)

    def a_e1(ep):
        return bfv(8192 + ep * 4096, 2048)

    def a_att(qtl, h):
        return bfv(14336 + qtl * 2048 + h * 128, 128)

    def a_att4(qtl, h0):
        return OVL.ap()[:, 14336 + qtl * 2048 + h0 * 128:
                        14336 + qtl * 2048 + (h0 + 4) * 128]

    F_HF = bfv(22528, 4096)
    F_SIL = bfv(26624, 512)

    def f_dtc(par, ib4):
        off = 27136 + par * 2048
        return OVL.ap()[:, off + ib4 * 512: off + (ib4 + 1) * 512]

    def f_dtc3(par):
        off = 27136 + par * 2048
        return OVL.ap()[:, off:off + 2048].rearrange(
            "p (c n) -> p c n", n=512)

    GWT = WST.ap()[:, 0:256]

    def wstq(bj, c):
        return WST.ap()[:, bj * 4096 + c * 512: bj * 4096 + (c + 1) * 512]

    def wstq3(bj):
        return WST.ap()[:, bj * 4096: bj * 4096 + 4096].rearrange(
            "p (c n) -> p c n", n=512)

    def wstq_g(bj, ec):
        return WST.ap()[:, bj * 4096 + ec * 128:
                        bj * 4096 + ec * 128 + 128]

    def wstq_g3(bj):
        return WST.ap()[:, bj * 4096: bj * 4096 + 2048].rearrange(
            "p (c n) -> p c n", n=128)

    def wstq_u(bj, ec):
        return WST.ap()[:, bj * 4096 + 2048 + ec * 128:
                        bj * 4096 + 2048 + ec * 128 + 128]

    def wstq_u3(bj):
        return WST.ap()[:, bj * 4096 + 2048:
                        bj * 4096 + 4096].rearrange(
            "p (c n) -> p c n", n=128)

    def wst(par, c):
        return WST.ap()[:, par * 8192 + c * 512: par * 8192 + (c + 1) * 512]

    def wst3(par, ncols=512):
        return WST.ap()[:, par * 8192: par * 8192 + 16 * ncols].rearrange(
            "p (c n) -> p c n", n=ncols)

    def wst_g(par, ec):
        return WST.ap()[:, par * 8192 + ec * 128:
                        par * 8192 + ec * 128 + 128]

    def wst_g3(par):
        return WST.ap()[:, par * 8192: par * 8192 + 2048].rearrange(
            "p (c n) -> p c n", n=128)

    def wst_u(par, ec):
        return WST.ap()[:, par * 8192 + 2048 + ec * 128:
                        par * 8192 + 2048 + ec * 128 + 128]

    def wst_u3(par):
        return WST.ap()[:, par * 8192 + 2048: par * 8192 + 4096].rearrange(
            "p (c n) -> p c n", n=128)

    XNB_ALL = ACC.ap().bitcast(BF16)
    XNH3 = XNH.ap().rearrange("p (c n) -> p c n", n=512)
    XNB3 = XNB_ALL[:, 0:16384].rearrange("p (c n) -> p c n", n=1024)

    def xn_tile(ec, tt):
        if tt < HALF:
            return XNA[:, ec * 1024 + tt * 128:
                       ec * 1024 + (tt + 1) * 128]
        return XNB_ALL[:, ec * 1024 + (tt - HALF) * 128:
                       ec * 1024 + (tt - HALF + 1) * 128]

    PA = Bank(e(nc.psum_tensor("PA", [128, 512], F32)))
    PB = Bank(e(nc.psum_tensor("PBK", [128, 512], F32)))
    PC = Bank(e(nc.psum_tensor("PCK", [128, 512], F32)))
    PD = Bank(e(nc.psum_tensor("PDK", [128, 512], F32)))
    PE_ = Bank(e(nc.psum_tensor("PEK", [128, 512], F32)))
    PF = Bank(e(nc.psum_tensor("PFK", [128, 512], F32)))
    PT1 = Bank(e(nc.psum_tensor("PT1", [128, 1024], BF16)))
    PT2 = Bank(e(nc.psum_tensor("PT2", [128, 1024], BF16)))

    sd = S(e(nc.semaphore("sd")))
    sg = S(e(nc.semaphore("sg")))
    sv = S(e(nc.semaphore("sv")))
    sa = S(e(nc.semaphore("sa")))
    sp = S(e(nc.semaphore("sp")))
    ESEM = {"vector": sv, "scalar": sa, "tensor": sp, "gpsimd": sg,
            "sync": sd}

    def vop(method, *args, waits=(), **kw):
        v = b.op("vector", method, *args, waits=waits, inc=sv, **kw)
        b.wait("vector", (sv, v))
        return v

    def pe_open(bank):
        if bank.cons:
            b.wait("tensor", *bank.cons)

    def pe_close(bank):
        bank.prod = (sp, sp.v)
        bank.cons = []

    def consume(bank, eng):
        assert bank.prod is not None
        b.wait(eng, bank.prod)

    def consumed(bank, eng):
        s = ESEM[eng]
        bank.cons.append((s, s.v))

    b.op("gpsimd", "memset", ONESROW, 1.0)
    b.op("gpsimd", "memset", COLS[:, :], 0.0)
    b.op("gpsimd", "memset", EPS5C, 1e-5)
    b.op("gpsimd", "memset", EPS8C, 1e-8)
    b.op("gpsimd", "memset", EPS20C, 1e-20)
    b.op("gpsimd", "memset", ONEC, 1.0)
    b.op("gpsimd", "memset", LI16, LAM_INIT)
    b.op("gpsimd", "memset", R_PAD, -3e38)
    b.op("gpsimd", "memset", TRIB[:, :], 0.0)
    b.op("gpsimd", "affine_select", TRIB[:, :], TRIB[:, :], [[-1, 128]],
         ALU.is_ge, NEG, base=0, channel_multiplier=1)
    b.op("gpsimd", "memset", IDB[:, :], 1.0)
    b.op("gpsimd", "affine_select", IDB[:, :], IDB[:, :], [[1, 128]],
         ALU.is_equal, 0.0, base=0, channel_multiplier=-1)
    b.op("gpsimd", "memset", IDF[:, :], 1.0)
    g_init = b.op("gpsimd", "affine_select", IDF[:, :], IDF[:, :], [[1, 16]],
                  ALU.is_equal, 0.0, base=0, channel_multiplier=-1, inc=sg)
    g_iota = b.op("gpsimd", "iota", THROW, [[1, 128]], base=0,
                  channel_multiplier=0,
                  allow_small_or_imprecise_dtypes=True, inc=sg)
    a_th = b.op("scalar", "activation", THROW, THROW, AF.Exp, bias=0.0,
                scale=-math.log(10000.0) / 128.0,
                waits=[(sg, g_iota)], inc=sa)
    v_th = b.op("vector", "tensor_scalar", THROW, THROW,
                1.0 / (2 * math.pi), None, ALU.mult,
                waits=[(sa, a_th)], inc=sv)

    d_rms = b.op("sync", "dma_start", out=RMSROW, in_=rms_h.ap(), inc=sd,
                 incn=16)
    lamt_q1 = V_LAMT[0:16, 0:128]
    lamt_k1 = V_LAMT[0:16, 128:256]
    lamt_q2 = V_LAMT[0:16, 256:384]
    lamt_k2 = V_LAMT[0:16, 384:512]
    b.op("sync", "dma_start", out=lamt_q1, in_=lq1_h.ap(), inc=sd, incn=16)
    b.op("sync", "dma_start", out=lamt_k1, in_=lk1_h.ap(), inc=sd, incn=16)
    b.op("sync", "dma_start", out=lamt_q2, in_=lq2_h.ap(), inc=sd, incn=16)
    d_l = b.op("sync", "dma_start", out=lamt_k2, in_=lk2_h.ap(), inc=sd,
               incn=16)
    d_eb = b.op("sync", "dma_start", out=EBROW, in_=eb_h.ap(), inc=sd,
                incn=16)
    d_gwt = b.op("gpsimd", "dma_start",
                 out=GWT.rearrange("p (c n) -> p c n", n=16),
                 in_=bass.AP(gwt_h, 0, [[16, 128], [128 * 16, 16], [1, 16]]),
                 inc=sg, incn=16)

    lam_c = V_LAMT[0:16, 512:513]
    lam_t = V_LAMT[0:16, 516:517]
    p1 = V_LAMT[0:16, 640:768]
    vop("tensor_tensor", p1, lamt_q1, lamt_k1, ALU.mult,
        waits=[(sd, d_l), (sg, g_init)])
    vop("reduce_sum", lam_c, p1, axis=AX.X)
    vop("tensor_tensor", p1, lamt_q2, lamt_k2, ALU.mult)
    v_l1 = vop("reduce_sum", lam_t, p1, axis=AX.X)
    b.op("scalar", "activation", lam_c, lam_c, AF.Exp, bias=0.0, scale=1.0,
         waits=[(sv, v_l1)])
    a_l2 = b.op("scalar", "activation", lam_t, lam_t, AF.Exp, bias=0.0,
                scale=1.0, inc=sa)
    vop("tensor_tensor", lam_c, lam_c, lam_t, ALU.subtract,
        waits=[(sa, a_l2)])
    vop("tensor_tensor", lam_c, lam_c, LI16[0:16, :], ALU.add)
    v_lam = vop("tensor_tensor", lam_c, ZERO1[0:16, :], lam_c, ALU.subtract)
    pe_open(PE_)
    b.op("tensor", "transpose", PE_.t[0:1, 0:16], lam_c, IDF[:, :],
         waits=[(sv, v_lam), (sg, g_init)], inc=sp)
    pe_close(PE_)
    consume(PE_, "vector")
    v_lr = b.op("vector", "tensor_copy", LAMROW, PE_.t[0:1, 0:16], inc=sv)
    consumed(PE_, "vector")
    pe_open(PE_)
    b.op("tensor", "matmul", PE_.t[:, 0:16], ONESROW, LAMROW,
         start=True, stop=True, waits=[(sv, v_lr)], inc=sp)
    pe_close(PE_)
    consume(PE_, "vector")
    v_lambc = b.op("vector", "tensor_copy", LAMBC[:, :], PE_.t[:, 0:16],
                   inc=sv)
    consumed(PE_, "vector")
    pe_open(PF)
    b.op("tensor", "matmul", PF.t[:, 0:16], ONESROW, EBROW,
         start=True, stop=True, waits=[(sd, d_eb)], inc=sp)
    pe_close(PF)
    consume(PF, "vector")
    v_ebrep = b.op("vector", "tensor_copy", EBREP[:, :], PF.t[:, 0:16],
                   inc=sv)
    consumed(PF, "vector")
    pe_open(PC)
    b.op("tensor", "matmul", PC.t[:, 0:128], ONESROW, RMSROW,
         start=True, stop=True, waits=[(sd, d_rms)], inc=sp)
    pe_close(PC)
    consume(PC, "vector")
    v_rms = b.op("vector", "tensor_scalar", RMSREP[:, :], PC.t[:, 0:128],
                 1.0 - LAM_INIT, None, ALU.mult, inc=sv)
    consumed(PC, "vector")

    for tt in range(NTT):
        g_tr = b.op("gpsimd", "iota", TOKR, [[1, 128]], base=tt * 128,
                    channel_multiplier=0,
                    allow_small_or_imprecise_dtypes=True, inc=sg,
                    waits=[(sa, sa.v)])
        bank = PE_ if tt % 2 == 0 else PF
        pe_open(bank)
        b.op("tensor", "matmul", bank.t[:, 0:128], TOKR, THROW,
             start=True, stop=True, waits=[(sg, g_tr), (sv, v_th)], inc=sp)
        pe_close(bank)
        consume(bank, "vector")
        b.op("vector", "tensor_scalar", V_TRA, bank.t[:, 0:128], MAGIC,
             MAGIC, ALU.add, ALU.subtract)
        b.op("vector", "tensor_tensor", trig_sin(tt), bank.t[:, 0:128],
             V_TRA, ALU.subtract)
        b.op("vector", "tensor_scalar", V_TRB, bank.t[:, 0:128], 0.25, None,
             ALU.add)
        consumed(bank, "vector")
        b.op("vector", "tensor_scalar", V_TRA, V_TRB, MAGIC, MAGIC,
             ALU.add, ALU.subtract)
        v_tr = b.op("vector", "tensor_tensor", trig_cos(tt), V_TRB, V_TRA,
                    ALU.subtract, inc=sv)
        b.op("scalar", "activation", trig_sin(tt), trig_sin(tt), AF.Sin,
             bias=0.0, scale=2 * math.pi, waits=[(sv, v_tr)])
        b.op("scalar", "activation", trig_cos(tt), trig_cos(tt),
             AF.Sin, bias=0.0, scale=2 * math.pi, inc=sa)
    trig_done = sa.v

    v_xt_guard = sv.v
    a_xt_guard = trig_done
    x_guard_sp = sp.v
    b.wait("vector", (sg, g_init))
    b.wait("scalar", (sg, g_init))
    b.wait("tensor", (sg, g_init))
    for tt in range(NTT if phases >= 1 else 0):
        half_a = tt < HALF
        d_x = b.op("sync", "dma_start", out=V_XT,
                   in_=bass.AP(x_h, tt * 128 * E, [[E, 128], [1, E]]),
                   waits=[(sv, v_xt_guard), (sa, a_xt_guard),
                          (sp, x_guard_sp)],
                   inc=sd, incn=16)
        v_m = vop("reduce_sum", C_A, V_XT, axis=AX.X, waits=[(sd, d_x)])
        b.op("scalar", "mul", C_B, C_A, 1.0 / E, waits=[(sv, v_m)])
        a_xt_guard = b.op("scalar", "activation", V_XNT, V_XT, AF.Square,
                          bias=0.0, scale=1.0, accum_out=C_C, inc=sa)
        vop("tensor_tensor", C_D, C_B, C_B, ALU.mult,
            waits=[(sa, a_xt_guard)])
        v_s = vop("tensor_tensor", C_E, EPS5C, C_D, ALU.subtract)
        a_v = b.op("scalar", "activation", C_G, C_C, AF.Sqrt, bias=C_E,
                   scale=1.0 / E, waits=[(sv, v_s)], inc=sa)
        vop("reciprocal", C_H, C_G, waits=[(sa, a_v)])
        vop("tensor_tensor", C_I, C_B, C_H, ALU.mult)
        vop("tensor_tensor", C_J, ZERO1, C_I, ALU.subtract)
        v_xn = b.op("vector", "tensor_scalar", V_XNT, V_XT, C_H, C_J,
                    ALU.mult, ALU.add, inc=sv)
        v_xt_guard = v_xn
        for grp in range(4):
            tb = PT1 if grp % 2 == 0 else PT2
            pe_open(tb)
            for j in range(4):
                ec = grp * 4 + j
                b.op("tensor", "transpose", tb.t[:, j * 128:(j + 1) * 128],
                     V_XNT[:, ec * 128:(ec + 1) * 128], IDB[:, :],
                     waits=[(sv, v_xn)] if j == 0 else (),
                     inc=sp if j == 3 else None)
            pe_close(tb)
            consume(tb, "vector")
            if half_a:
                dst = XNA3[:, grp * 4:grp * 4 + 4, tt * 128:(tt + 1) * 128]
            else:
                dst = XNB3[:, grp * 4:grp * 4 + 4,
                           (tt - HALF) * 128:(tt - HALF + 1) * 128]
            b.op("vector", "tensor_copy", dst,
                 tb.t.ap().rearrange("p (c n) -> p c n", n=128)[:, 0:4, :], inc=sv)
            consumed(tb, "vector")
        v_tiles = sv.v
        src3 = (XNA3 if half_a else XNB3)[
            :, :, (tt % HALF) * 128:(tt % HALF + 1) * 128]
        b.op("sync", "dma_start",
             out=bass.AP(xn_hbm, tt * 128,
                         [[16 * T, 128], [T, 16], [1, 128]]),
             in_=src3, waits=[(sv, v_tiles)], inc=sd, incn=16)
        lbank = PE_ if tt % 2 == 0 else PF
        pe_open(lbank)
        for ec in range(16):
            b.op("tensor", "matmul", lbank.t[:, 0:16],
                 xn_tile(ec, tt), GWT[:, ec * 16:(ec + 1) * 16],
                 start=(ec == 0), stop=(ec == 15),
                 waits=([(sv, v_tiles), (sg, d_gwt)] if ec == 0 else ()),
                 inc=sp if ec == 15 else None)
        pe_close(lbank)
        consume(lbank, "scalar")
        a_sig = b.op("scalar", "activation", R_SCO, lbank.t[:, 0:16],
                     AF.Sigmoid, bias=0.0, scale=1.0, inc=sa)
        consumed(lbank, "scalar")
        vop("tensor_tensor", R_BIA, R_SCO, EBREP[:, :], ALU.add,
            waits=[(sa, a_sig), (sv, v_ebrep)])
        A0 = RW.ap()[:, 16:32][:, 0::4]
        A1 = RW.ap()[:, 16:32][:, 1::4]
        A2 = RW.ap()[:, 16:32][:, 2::4]
        A3 = RW.ap()[:, 16:32][:, 3::4]
        T_M1 = RW.ap()[:, 96:100]
        T_m1 = RW.ap()[:, 100:104]
        T_M2 = RW.ap()[:, 104:108]
        T_m2 = RW.ap()[:, 108:112]
        vop("tensor_tensor", T_M1, A0, A1, ALU.max)
        vop("tensor_tensor", T_m1, A0, A1, ALU.min)
        vop("tensor_tensor", T_M2, A2, A3, ALU.max)
        vop("tensor_tensor", T_m2, A2, A3, ALU.min)
        vop("tensor_tensor", R_GS, T_M1, T_M2, ALU.max)
        vop("tensor_tensor", T_m1, T_m1, T_m2, ALU.max)
        vop("tensor_tensor", T_M1, T_M1, T_M2, ALU.min)
        vop("tensor_tensor", T_M1, T_M1, T_m1, ALU.max)
        vop("tensor_tensor", R_GS, R_GS, T_M1, ALU.add)
        g0 = R_GS[:, 0:1]; g1 = R_GS[:, 1:2]
        g2 = R_GS[:, 2:3]; g3 = R_GS[:, 3:4]
        c1 = RW.ap()[:, 96:97]; c2 = RW.ap()[:, 97:98]
        c3 = RW.ap()[:, 98:99]; c4 = RW.ap()[:, 99:100]
        vop("tensor_tensor", c1, g0, g1, ALU.max)
        vop("tensor_tensor", c2, g0, g1, ALU.min)
        vop("tensor_tensor", c3, g2, g3, ALU.max)
        vop("tensor_tensor", c4, g2, g3, ALU.min)
        vop("tensor_tensor", c2, c2, c4, ALU.max)
        vop("tensor_tensor", c1, c1, c3, ALU.min)
        vop("tensor_tensor", R_T2, c1, c2, ALU.max)
        vop("tensor_single_scalar", R_GM, R_GS, R_T2, ALU.is_ge)
        vop("tensor_scalar", R_GMN, R_GM, 1.0, BIG, ALU.subtract, ALU.mult)
        for g in range(4):
            vop("tensor_single_scalar", R_MSK[:, 4 * g:4 * g + 4],
                R_BIA[:, 4 * g:4 * g + 4], R_GMN[:, g:g + 1], ALU.add)
        vop("tensor_copy", R_MSK2, R_MSK)
        for kext in range(4):
            vop("reduce_max", R_T4, R_MSK2, axis=AX.X)
            if kext < 3:
                vop("tensor_single_scalar", R_EQ, R_MSK2, R_T4, ALU.is_ge)
                vop("tensor_scalar", R_EQ, R_EQ, -BIG, None, ALU.mult)
                vop("tensor_tensor", R_MSK2, R_MSK2, R_EQ, ALU.add)
        vop("tensor_single_scalar", R_SEL, R_MSK, R_T4, ALU.is_ge)
        vop("tensor_tensor", R_SEL, R_SEL, R_SCO, ALU.mult)
        vop("reduce_sum", R_WS, R_SEL, axis=AX.X)
        vop("tensor_tensor", R_WS, R_WS, EPS20C, ALU.add)
        vop("reciprocal", R_WI, R_WS)
        b.op("vector", "tensor_scalar", CW.ap()[:, tt * 16:(tt + 1) * 16],
             R_SEL, R_WI, 2.5, ALU.mult, ALU.mult, inc=sv)
    xn_all_v = sv.v

    wst_free = [None, None, None, None]

    def load_slab(par, src_h, col0, ncols=512):
        w = [x for x in (wst_free[2 * par], wst_free[2 * par + 1])
             if x is not None]
        return b.op("gpsimd", "dma_start", out=wst3(par)[:, :, 0:ncols],
                    in_=bass.AP(src_h, col0,
                                [[src_h.shape[1], 128],
                                 [128 * src_h.shape[1], 16], [1, ncols]]),
                    waits=w, inc=sg, incn=16)

    wst_free[0] = wst_free[1] = (sp, sp.v)
    d_wv = load_slab(0, wv_h, 0) if phases >= 2 else 0
    for tt in range(NTT if phases >= 2 else 0):
        bank = PC if tt % 2 == 0 else PD
        pe_open(bank)
        for ec in range(16):
            b.op("tensor", "matmul", bank.t[:, :], xn_tile(ec, tt),
                 wst(0, ec), start=(ec == 0), stop=(ec == 15),
                 waits=([(sg, d_wv), (sv, xn_all_v)] if ec == 0 else ()),
                 inc=sp if ec == 15 else None)
        pe_close(bank)
        consume(bank, "scalar")
        b.op("scalar", "activation", VTOK.ap()[:, tt * 512:(tt + 1) * 512],
             bank.t[:, :], AF.Copy, bias=0.0, scale=1.0, inc=sa)
        consumed(bank, "scalar")
    wst_free[0] = wst_free[1] = (sp, sp.v)
    vtok_done = sa.v

    def rope_emit(bank, tt, out_stage):
        last = None
        for hs in range(2):
            base = hs * 256
            re = bank.t[:, base:base + 256:2]
            im = bank.t[:, base + 1:base + 256:2]
            sre = out_stage[:, base:base + 256:2]
            sim = out_stage[:, base + 1:base + 256:2]
            b.op("vector", "tensor_tensor", V_RT1, re, trig_cos(tt),
                 ALU.mult)
            b.op("vector", "tensor_tensor", V_RT2, im, trig_sin(tt),
                 ALU.mult)
            b.op("vector", "tensor_tensor", sre, V_RT1, V_RT2, ALU.subtract)
            b.op("vector", "tensor_tensor", V_RT3, re, trig_sin(tt),
                 ALU.mult)
            b.op("vector", "tensor_tensor", V_RT4, im, trig_cos(tt),
                 ALU.mult)
            last = b.op("vector", "tensor_tensor", sim, V_RT3, V_RT4,
                        ALU.add, inc=sv if hs == 1 else None)
        return last

    stg0_guard = 0
    for kfc in range(2 if phases >= 2 else 0):
        par = kfc % 2
        d_wk = load_slab(par, wk_h, kfc * 512)
        for tt in range(NTT):
            bank = PC if tt % 2 == 0 else PD
            pe_open(bank)
            for ec in range(16):
                b.op("tensor", "matmul", bank.t[:, :], xn_tile(ec, tt),
                     wst(par, ec), start=(ec == 0), stop=(ec == 15),
                     waits=([(sg, d_wk)] if ec == 0 else ()),
                     inc=sp if ec == 15 else None)
            pe_close(bank)
            consume(bank, "vector")
            b.wait("vector", (sa, trig_done), (sp, stg0_guard))
            v_r = rope_emit(bank, tt, V_STG0)
            consumed(bank, "vector")
            tb = PT1 if tt % 2 == 0 else PT2
            pe_open(tb)
            for j in range(4):
                b.op("tensor", "transpose", tb.t[:, j * 128:(j + 1) * 128],
                     V_STG0[:, j * 128:(j + 1) * 128], IDB[:, :],
                     waits=[(sv, v_r)] if j == 0 else (),
                     inc=sp if j == 3 else None)
            pe_close(tb)
            stg0_guard = sp.v
            consume(tb, "vector")
            dst = KFM.ap().rearrange("p (c n) -> p c n", n=T)[
                :, kfc * 4:kfc * 4 + 4, tt * 128:(tt + 1) * 128]
            b.op("vector", "tensor_copy", dst,
                 tb.t.ap().rearrange("p (c n) -> p c n", n=128)[:, 0:4, :], inc=sv)
            consumed(tb, "vector")
        wst_free[2 * par] = wst_free[2 * par + 1] = (sp, sp.v)
    kfm_done = sv.v

    q_stg_dma = {0: 0, 1: 0}
    q_stg_sp = {0: stg0_guard, 1: 0}
    for qfc in range(8 if phases >= 2 else 0):
        par = qfc % 2
        d_wq = load_slab(par, wq_h, qfc * 512)
        for tt in range(NTT):
            bank = PC if tt % 2 == 0 else PD
            pe_open(bank)
            for ec in range(16):
                b.op("tensor", "matmul", bank.t[:, :], xn_tile(ec, tt),
                     wst(par, ec), start=(ec == 0), stop=(ec == 15),
                     waits=([(sg, d_wq)] if ec == 0 else ()),
                     inc=sp if ec == 15 else None)
            pe_close(bank)
            consume(bank, "vector")
            sid = tt % 2
            stg = V_STG0 if sid == 0 else V_STG1
            b.wait("vector", (sd, q_stg_dma[sid]), (sp, q_stg_sp[sid]))
            v_r = rope_emit(bank, tt, stg)
            consumed(bank, "vector")
            tb = PT1 if tt % 2 == 0 else PT2
            pe_open(tb)
            for j in range(4):
                b.op("tensor", "transpose", tb.t[:, j * 128:(j + 1) * 128],
                     stg[:, j * 128:(j + 1) * 128], IDB[:, :],
                     waits=[(sv, v_r)] if j == 0 else (),
                     inc=sp if j == 3 else None)
            pe_close(tb)
            q_stg_sp[sid] = sp.v
            consume(tb, "vector")
            v_q = b.op("vector", "tensor_copy",
                       stg.rearrange("p (c n) -> p c n", n=128),
                       tb.t.ap().rearrange("p (c n) -> p c n", n=128)[:, 0:4, :],
                       inc=sv)
            consumed(tb, "vector")
            c0 = qfc * 4
            q_stg_dma[sid] = b.op(
                "sync", "dma_start",
                out=bass.AP(q_hbm, c0 * T + tt * 128,
                            [[32 * T, 128], [T, 4], [1, 128]]),
                in_=stg.rearrange("p (c n) -> p c n", n=128),
                waits=[(sv, v_q)], inc=sd, incn=16)
        wst_free[2 * par] = wst_free[2 * par + 1] = (sp, sp.v)
    q_spill_done = sd.v

    att_guard = [0, 0]
    dtc_free = [(sp, sp.v)] * 4

    def superphase(qn):
        q0 = qn * 4
        entry_sp = sp.v
        entry_sv = sv.v

        xnh_ready = b.op(
            "sync", "dma_start", out=XNH3[:, :, :],
            in_=bass.AP(xn_hbm, q0 * 128,
                        [[16 * T, 128], [T, 16], [1, 512]]),
            waits=([(sd, q_spill_done), (sp, entry_sp)] if qn == 0
                   else [(sp, entry_sp)]),
            inc=sd, incn=16)
        acc_init = {}
        for tl in range(4):
            tt = q0 + tl
            acc_init[tl] = b.op(
                "sync", "dma_start",
                out=ACC.ap()[:, tl * E:(tl + 1) * E],
                in_=bass.AP(x_h, tt * 128 * E, [[E, 128], [1, E]]),
                inc=sd, incn=16)

        st_ = {
            "d_qb": [0, 0],
            "qb_guard": [entry_sp, entry_sp],
            "e_pt_guard": [entry_sp, entry_sp],
            "e_u_guard": [entry_sv, entry_sv],
            "pt_sp_prev": entry_sp,
            "a_p": {},
            "heads_a": 0,
            "epi_prev": entry_sv,
            "att_evac_v": 0,
            "last_acc_v": {tl: 0 for tl in range(4)},
        }

        def em_qload(qi):
            par = qi % 2
            st_["d_qb"][par] = b.op(
                "sync", "dma_start",
                out=a_qb(par).rearrange("p (c n) -> p c n", n=128),
                in_=bass.AP(q_hbm, (q0 + qi) * 128,
                            [[32 * T, 128], [T, 32], [1, 128]]),
                waits=[(sp, st_["qb_guard"][par])], inc=sd, incn=16)

        def em_part1(qi, h):
            qt = q0 + qi
            kn = qt + 1
            span = kn * 128
            nch = (span + 511) // 512
            par = qi % 2
            ep = h % 2
            kvh = h // 4
            ebuf = [a_e0(ep), a_e1(ep)]
            dcols = [R_D01, R_D02]
            for a_ in range(2):
                lhsT = a_qb(par)[:, (2 * h + a_) * 128:
                                 (2 * h + a_ + 1) * 128]
                kc = kvh * 2 + a_
                for ch in range(nch):
                    w = min(512, span - ch * 512)
                    bank = PA if ch % 2 == 0 else PB
                    is_last = ch == nch - 1
                    pe_open(bank)
                    b.op("tensor", "matmul", bank.t[:, 0:w], lhsT,
                         KFM.ap()[:, kc * T + ch * 512:
                                  kc * T + ch * 512 + w],
                         start=True, stop=not is_last,
                         waits=([(sd, st_["d_qb"][par]), (sv, kfm_done)]
                                if ch == 0 else ()),
                         inc=None if is_last else sp)
                    if is_last:
                        b.op("tensor", "matmul", bank.t[:, w - 128:w],
                             IDB[:, :], TRIB[:, :], start=False,
                             stop=True, inc=sp)
                    pe_close(bank)
                    consume(bank, "scalar")
                    if a_ == 0:
                        b.wait("scalar", (sp, st_["e_pt_guard"][ep]))
                    else:
                        b.wait("scalar", (sv, st_["e_u_guard"][ep]))
                    b.op("scalar", "activation",
                         ebuf[a_][:, ch * 512:ch * 512 + w],
                         bank.t[:, 0:w], AF.Exp, bias=0.0, scale=ESCALE,
                         accum_out=dcols[a_][:, ch:ch + 1], inc=sa)
                    consumed(bank, "scalar")
                dcol = (H_D0 if a_ == 0 else H_D1)[:, h:h + 1]
                vop("reduce_sum", dcol, dcols[a_][:, 0:nch], axis=AX.X,
                    waits=[(sa, sa.v)])
            b.op("vector", "reciprocal", C_A, H_D0[:, h:h + 1], inc=sv)
            v_rb = b.op("vector", "reciprocal", C_B, H_D1[:, h:h + 1],
                        inc=sv)
            b.wait("vector", (sv, v_rb))
            vop("tensor_tensor", C_B, C_B, LAMBC[:, h:h + 1], ALU.mult,
                waits=[(sv, v_lambc)])
            b.wait("vector", (sp, st_["e_pt_guard"][ep]))
            vop("tensor_single_scalar", a_e0(ep)[:, 0:span],
                a_e0(ep)[:, 0:span], C_A, ALU.mult)
            dsp = span - 128
            vop("tensor_tensor", a_e0(ep)[:, dsp:span],
                a_e0(ep)[:, dsp:span], TRIB[:, :], ALU.add)
            v_u = b.op("vector", "scalar_tensor_tensor",
                       a_e0(ep)[:, 0:span], a_e1(ep)[:, 0:span], C_B,
                       a_e0(ep)[:, 0:span], ALU.mult, ALU.add, inc=sv)
            st_["e_u_guard"][ep] = v_u
            st_["a_p"][h] = b.op(
                "scalar", "activation", a_e0(ep)[:, 0:span],
                a_e0(ep)[:, 0:span], AF.Exp, bias=0.0, scale=1.0,
                accum_out=H_D2[:, h:h + 1], waits=[(sv, v_u)], inc=sa)

        def em_part2(qi, h):
            qt = q0 + qi
            kn = qt + 1
            par = qi % 2
            ep = h % 2
            kvh = h // 4
            a_p = st_["a_p"][h]
            for grp in range((kn + 3) // 4):
                tb = PT1 if grp % 2 == 0 else PT2
                nj = min(4, kn - grp * 4)
                pe_open(tb)
                for j in range(nj):
                    kt = grp * 4 + j
                    b.op("tensor", "transpose",
                         tb.t[:, j * 128:(j + 1) * 128],
                         a_e0(ep)[:, kt * 128:(kt + 1) * 128], IDB[:, :],
                         waits=[(sa, a_p)] if j == 0 else (),
                         inc=sp if j == nj - 1 else None)
                pe_close(tb)
                consume(tb, "vector")
                b.op("vector", "tensor_copy",
                     A_PT.rearrange("p (c n) -> p c n", n=128)[
                         :, grp * 4:grp * 4 + nj, :],
                     tb.t.ap().rearrange("p (c n) -> p c n", n=128)[
                         :, 0:nj, :], inc=sv)
                consumed(tb, "vector")
            st_["e_pt_guard"][ep] = sp.v
            pt_done = sv.v
            ab = PE_ if h % 2 == 0 else PF
            pe_open(ab)
            for kt in range(kn):
                b.op("tensor", "matmul", ab.t[:, 0:128],
                     A_PT[:, kt * 128:(kt + 1) * 128],
                     VTOK.ap()[:, kt * 512 + kvh * 128:
                               kt * 512 + kvh * 128 + 128],
                     start=(kt == 0), stop=(kt == kn - 1),
                     waits=([(sv, pt_done), (sa, vtok_done)]
                            if kt == 0 else ()),
                     inc=sp if kt == kn - 1 else None)
            pe_close(ab)
            consume(ab, "scalar")
            b.wait("scalar", (sv, st_["epi_prev"]))
            b.wait("scalar", (sp, att_guard[0]))
            b.op("scalar", "activation", a_att(qi, h), ab.t[:, 0:128],
                 AF.Square, bias=0.0, scale=1.0,
                 accum_out=HSQ.ap()[:, h:h + 1])
            b.op("scalar", "activation", a_att(qi, h), ab.t[:, 0:128],
                 AF.Copy, bias=0.0, scale=1.0, inc=sa)
            consumed(ab, "scalar")

        def em_epi(qi):
            heads_a = sa.v
            b.wait("vector", (sa, heads_a))
            v_r3 = b.op("vector", "reciprocal", A_T3, H_D2, inc=sv)
            b.wait("vector", (sv, v_r3))
            vop("tensor_tensor", A_T4, A_T3, A_T3, ALU.mult)
            v_t = vop("tensor_tensor", A_T1, HSQ.ap()[:, :], A_T4,
                      ALU.mult)
            a_sq = b.op("scalar", "activation", A_T2, A_T1, AF.Sqrt,
                        bias=EPS8C, scale=1.0 / 128.0, waits=[(sv, v_t)],
                        inc=sa)
            v_r2 = b.op("vector", "reciprocal", A_T2, A_T2,
                        waits=[(sa, a_sq)], inc=sv)
            b.wait("vector", (sv, v_r2))
            v_sc = vop("tensor_tensor", H_SC, A_T2, A_T3, ALU.mult)
            stg_wait = sp.v
            for h4 in range(4):
                tb = PT1 if h4 % 2 == 0 else PT2
                b.wait("vector", (sp, stg_wait))
                v_stg = None
                for j in range(4):
                    h = h4 * 4 + j
                    v_stg = b.op("vector", "scalar_tensor_tensor",
                                 A_STG[:, j * 128:(j + 1) * 128],
                                 a_att(qi, h), H_SC[:, h:h + 1],
                                 RMSREP[:, :], ALU.mult, ALU.mult,
                                 waits=([(sv, v_sc), (sv, v_rms)]
                                        if j == 0 else ()),
                                 inc=sv if j == 3 else None)
                pe_open(tb)
                for j in range(4):
                    b.op("tensor", "transpose",
                         tb.t[:, j * 128:(j + 1) * 128],
                         A_STG[:, j * 128:(j + 1) * 128], IDB[:, :],
                         waits=[(sv, v_stg)] if j == 0 else (),
                         inc=sp if j == 3 else None)
                pe_close(tb)
                stg_wait = sp.v
                st_["pt_sp_prev"] = sp.v
                consume(tb, "vector")
                b.wait("vector", (sp, att_guard[0]))
                b.op("vector", "tensor_copy",
                     a_att4(qi, h4 * 4).rearrange("p (c n) -> p c n",
                                                  n=128),
                     tb.t.ap().rearrange("p (c n) -> p c n",
                                         n=128)[:, 0:4, :],
                     inc=sv)
                consumed(tb, "vector")
            st_["epi_prev"] = sv.v
            st_["qb_guard"][qi % 2] = sp.v

        def em_wo():
            att_v = st_["epi_prev"]
            for eg in range(4):
                for qtl in range(4):
                    obank = PE_ if (eg * 4 + qtl) % 2 == 0 else PF
                    pe_open(obank)
                    for hdh in range(2):
                        bj = (eg * 2 + hdh) % 4
                        w = ([wst_free[bj]] if wst_free[bj] is not None
                             else [])
                        if qtl == 0:
                            d_wo = b.op(
                                "gpsimd", "dma_start", out=wstq3(bj),
                                in_=bass.AP(wo_h,
                                            hdh * 8 * 128 * E + eg * 512,
                                            [[E, 128], [128 * E, 8],
                                             [1, 512]]),
                                waits=w, inc=sg, incn=16)
                            wo_dma[(eg, hdh)] = d_wo
                        for hd in range(8):
                            b.op("tensor", "matmul", obank.t[:, :],
                                 a_att(qtl, hdh * 8 + hd), wstq(bj, hd),
                                 start=(hdh == 0 and hd == 0),
                                 stop=(hdh == 1 and hd == 7),
                                 waits=([(sg, wo_dma[(eg, hdh)]),
                                         (sv, att_v)]
                                        if hd == 0 else ()),
                                 inc=sp if (hdh == 1 and hd == 7)
                                 else None)
                    pe_close(obank)
                    consume(obank, "vector")
                    dst = ACC.ap()[:, qtl * E + eg * 512:
                                   qtl * E + eg * 512 + 512]
                    st_["last_acc_v"][qtl] = b.op(
                        "vector", "tensor_tensor", dst, dst,
                        obank.t[:, :], ALU.add,
                        waits=[(sd, acc_init[qtl])], inc=sv)
                    consumed(obank, "vector")
                for hdh in range(2):
                    bj = (eg * 2 + hdh) % 4
                    wst_free[bj] = (sp, sp.v)
            att_guard[0] = sp.v
            att_guard[1] = sp.v

        fst = {"hf_guard": entry_sp, "hf_v": 0}

        def em_gu(ei, ib):
            bj = gu_buf[(ei, ib)]
            d_w = gu_dma[(ei, ib)]
            nxt = (ei, ib + 2) if ib < 6 else (ei + 1, ib - 6)
            if nxt[0] <= NEXP and nxt not in gu_dma:
                load_gu(nxt[0], nxt[1])
            pe_open(PC)
            for ec in range(16):
                b.op("tensor", "matmul", PC.t[:, :], wstq_g(bj, ec),
                     XNH.ap()[:, ec * 512:(ec + 1) * 512],
                     start=(ec == 0), stop=(ec == 15),
                     waits=([(sg, d_w), (sd, xnh_ready)]
                            if ec == 0 else ()),
                     inc=sp if ec == 15 else None)
            pe_close(PC)
            pe_open(PD)
            for ec in range(16):
                b.op("tensor", "matmul", PD.t[:, :], wstq_u(bj, ec),
                     XNH.ap()[:, ec * 512:(ec + 1) * 512],
                     start=(ec == 0), stop=(ec == 15),
                     inc=sp if ec == 15 else None)
            pe_close(PD)
            wst_free[bj] = (sp, sp.v)
            consume(PC, "scalar")
            a_sil = b.op("scalar", "activation", F_SIL, PC.t[:, :],
                         AF.Silu, bias=0.0, scale=1.0, inc=sa)
            consumed(PC, "scalar")
            consume(PD, "vector")
            b.wait("vector", (sp, fst["hf_guard"]))
            fst["hf_v"] = b.op(
                "vector", "tensor_tensor",
                F_HF[:, ib * 512:(ib + 1) * 512], F_SIL, PD.t[:, :],
                ALU.mult, waits=[(sa, a_sil)], inc=sv)
            consumed(PD, "vector")

        def em_down(ei, hc):
            d_d = {}
            for ibh in range(2):
                d_d[ibh] = dtc_dma.pop((ei, hc, ibh))
            if hc < 3:
                for ibh in range(2):
                    load_dtc(ei, hc + 1, ibh)
            elif ei < NEXP:
                for ibh in range(2):
                    load_dtc(ei + 1, 0, ibh)
            for tlp in range(2):
                for t2 in range(2):
                    pe_open([PC, PD][t2])
                for ib in range(8):
                    ibh = ib // 4
                    par = (hc % 2) * 2 + ibh
                    for t2 in range(2):
                        tl = tlp * 2 + t2
                        b.op("tensor", "matmul", [PC, PD][t2].t[:, :],
                             F_HF[:, ib * 512 + tl * 128:
                                  ib * 512 + tl * 128 + 128],
                             f_dtc(par, ib % 4),
                             start=(ib == 0), stop=(ib == 7),
                             waits=([(sv, fst["hf_v"]), (sg, d_d[ibh])]
                                    if t2 == 0 and ib % 4 == 0 else ()),
                             inc=sp if ib == 7 else None)
                for t2 in range(2):
                    pe_close([PC, PD][t2])
                for t2 in range(2):
                    tl = tlp * 2 + t2
                    cwcol = (CW.ap()[:, (q0 + tl) * 16 + ei:
                                     (q0 + tl) * 16 + ei + 1]
                             if ei < NEXP else ONEC)
                    bk = [PC, PD][t2]
                    consume(bk, "vector")
                    dst = ACC.ap()[:, tl * E + hc * 512:
                                   tl * E + hc * 512 + 512]
                    st_["last_acc_v"][tl] = b.op(
                        "vector", "scalar_tensor_tensor", dst,
                        bk.t[:, :], cwcol, dst, ALU.mult, ALU.add,
                        waits=[(sd, acc_init[tl])], inc=sv)
                    consumed(bk, "vector")
            for ibh in range(2):
                par = (hc % 2) * 2 + ibh
                dtc_free[par] = (sp, sp.v)
            if hc == 3:
                fst["hf_guard"] = sp.v

        gu_dma = {}
        gu_buf = {}
        gu_next_buf = [0]
        wo_dma = {}

        def load_gu(ei, ib):
            bj = gu_next_buf[0]
            gu_next_buf[0] = (bj + 1) % 4
            w = [wst_free[bj]] if wst_free[bj] is not None else []
            if ei < NEXP:
                gsrc = bass.AP(gt_h, ei * E * I_FF + ib * 128,
                               [[I_FF, 128], [128 * I_FF, 16], [1, 128]])
                usrc = bass.AP(ut_h, ei * E * I_FF + ib * 128,
                               [[I_FF, 128], [128 * I_FF, 16], [1, 128]])
            else:
                gsrc = bass.AP(sgt_h, ib * 128,
                               [[I_FF, 128], [128 * I_FF, 16], [1, 128]])
                usrc = bass.AP(sut_h, ib * 128,
                               [[I_FF, 128], [128 * I_FF, 16], [1, 128]])
            b.op("gpsimd", "dma_start", out=wstq_g3(bj), in_=gsrc,
                 waits=w, inc=sg, incn=16)
            gu_dma[(ei, ib)] = b.op("gpsimd", "dma_start",
                                    out=wstq_u3(bj), in_=usrc,
                                    inc=sg, incn=16)
            gu_buf[(ei, ib)] = bj

        dtc_dma = {}

        def load_dtc(ei, hc, ibh):
            par = (hc % 2) * 2 + ibh
            w = [dtc_free[par]]
            if ei < NEXP:
                s = bass.AP(dt_h,
                            ei * I_FF * E + ibh * 4 * 128 * E + hc * 512,
                            [[E, 128], [128 * E, 4], [1, 512]])
            else:
                s = bass.AP(sdt_h, ibh * 4 * 128 * E + hc * 512,
                            [[E, 128], [128 * E, 4], [1, 512]])
            dtc_dma[(ei, hc, ibh)] = b.op("gpsimd", "dma_start",
                                          out=f_dtc3(par), in_=s,
                                          waits=w, inc=sg, incn=16)

        load_gu(0, 0)
        load_gu(0, 1)
        load_dtc(0, 0, 0)
        load_dtc(0, 0, 1)

        att_stream = []
        for qi in range(4):
            att_stream.append(("qload", qi, None))
            for h in range(H):
                att_stream.append(("p1", qi, h))
                att_stream.append(("p2", qi, h))
            att_stream.append(("epi", qi, None))
        ordered = []
        pend_p2 = None
        for item in att_stream:
            if item[0] == "p1":
                ordered.append(item)
                if pend_p2 is not None:
                    ordered.append(pend_p2)
                pend_p2 = ("p2", item[1], item[2])
            elif item[0] == "p2":
                continue
            else:
                if item[0] == "epi" and pend_p2 is not None:
                    ordered.append(pend_p2)
                    pend_p2 = None
                ordered.append(item)

        ffn_stream = []
        for ei in range(NEXP + 1):
            for ib in range(8):
                ffn_stream.append(("gu", ei, ib))
            for hc in range(4):
                ffn_stream.append(("down", ei, hc))

        na, nf = len(ordered), len(ffn_stream)
        ai = fi = 0
        credit = 0.0
        ratio = na / nf
        while fi < nf or ai < na:
            if fi < nf:
                kind, x1, x2 = ffn_stream[fi]
                fi += 1
                if kind == "gu":
                    em_gu(x1, x2)
                else:
                    em_down(x1, x2)
            credit += ratio if fi <= nf else na
            while ai < na and (credit >= 1.0 or fi >= nf):
                credit -= 1.0
                kind, x1, x2 = ordered[ai]
                ai += 1
                if kind == "qload":
                    em_qload(x1)
                elif kind == "p1":
                    em_part1(x1, x2)
                elif kind == "p2":
                    em_part2(x1, x2)
                else:
                    em_epi(x1)
        em_wo()

        for tl in range(4):
            tt = q0 + tl
            b.op("sync", "dma_start",
                 out=bass.AP(out_h, tt * 128 * E, [[E, 128], [1, E]]),
                 in_=ACC.ap()[:, tl * E:(tl + 1) * E],
                 waits=[(sv, st_["last_acc_v"][tl])], inc=sd, incn=16)

    for qn in range(4):
        superphase(qn)

    b.wait("sync", (sd, sd.v))
    b.wait("gpsimd", (sg, sg.v))

    with nc.Block() as block:
        b.replay(block)
    ctx.close()
    return nc


def kernel(**inputs):
    x = np.ascontiguousarray(np.asarray(inputs["L0_attn_x"], np.float32))
    base = {
        "wq": np.ascontiguousarray(inputs["L0_attn_Wq"], dtype=np.float32),
        "wk": np.ascontiguousarray(inputs["L0_attn_Wk"], dtype=np.float32),
        "wv": np.ascontiguousarray(inputs["L0_attn_Wv"], dtype=np.float32),
        "wo": np.ascontiguousarray(inputs["L0_attn_Wo"], dtype=np.float32),
        "lq1": np.ascontiguousarray(inputs["L0_attn_lq1"], dtype=np.float32),
        "lq2": np.ascontiguousarray(inputs["L0_attn_lq2"], dtype=np.float32),
        "lk1": np.ascontiguousarray(inputs["L0_attn_lk1"], dtype=np.float32),
        "lk2": np.ascontiguousarray(inputs["L0_attn_lk2"], dtype=np.float32),
        "rms": np.ascontiguousarray(
            np.asarray(inputs["L0_attn_rms_w"], np.float32).reshape(1, D)),
        "gwt": np.ascontiguousarray(
            np.asarray(inputs["L0_ffn_gate_weight"], np.float32).T),
        "eb": np.ascontiguousarray(
            np.asarray(inputs["L0_ffn_expert_bias"],
                       np.float32).reshape(1, NEXP)),
        "gt": np.ascontiguousarray(
            np.asarray(inputs["L0_ffn_gate_proj"],
                       np.float32).transpose(0, 2, 1)),
        "ut": np.ascontiguousarray(
            np.asarray(inputs["L0_ffn_up_proj"],
                       np.float32).transpose(0, 2, 1)),
        "dtp": np.ascontiguousarray(
            np.asarray(inputs["L0_ffn_down_proj"],
                       np.float32).transpose(0, 2, 1)),
        "sgt": np.ascontiguousarray(
            np.asarray(inputs["L0_ffn_shared_gate"], np.float32).T),
        "sut": np.ascontiguousarray(
            np.asarray(inputs["L0_ffn_shared_up"], np.float32).T),
        "sdt": np.ascontiguousarray(
            np.asarray(inputs["L0_ffn_shared_down"], np.float32).T),
    }
    nc = bass.Bass("TRN2", target_bir_lowering=False)
    build(nc)
    in_maps = []
    for c in range(8):
        m = dict(base)
        m["x"] = np.ascontiguousarray(x[c])
        in_maps.append(m)
    res = run_bass_kernel_spmd(nc, in_maps, core_ids=list(range(8)))
    out = np.stack([np.asarray(r["out"]) for r in res.results], axis=0)
    return out.astype(np.float32)


# version 03
# speedup vs baseline: 1.2201x; 1.0897x over previous
import math
from contextlib import ExitStack

import numpy as np

import concourse.bass as bass
import concourse.mybir as mybir
from concourse.bass_utils import run_bass_kernel_spmd

F32 = mybir.dt.float32
BF16 = mybir.dt.bfloat16
AF = mybir.ActivationFunctionType
ALU = mybir.AluOpType
AX = mybir.AxisListType

T = 2048
E = 2048
H = 16
KVH = 4
D = 128
NTT = 16
HALF = 8
I_FF = 1024
NEXP = 16
NEG = -1e30
MAGIC = 12582912.0
ESCALE = 1.0 / math.sqrt(128.0)
LAM_INIT = 0.2
BIG = 1e32

ENGS = ("sync", "gpsimd", "vector", "scalar", "tensor")
ATT_SUB = 99
ATT_QMAX = 99


class S:
    def __init__(self, h):
        self.h = h
        self.v = 0


class Bank:
    def __init__(self, t):
        self.t = t
        self.prod = None
        self.cons = []


class Builder:
    def __init__(self, nc):
        self.nc = nc
        self.rec = {k: [] for k in ENGS}
        self.waited = {}

    def _filter(self, eng, waits):
        wl = []
        for (s, v) in waits:
            if v <= 0:
                continue
            key = (eng, s.h.name)
            if self.waited.get(key, -1) >= v:
                continue
            assert v <= s.v, (
                f"wait on future value {v} > {s.v} of {s.h.name} on {eng}")
            self.waited[key] = v
            wl.append((s.h, v))
        return tuple(wl)

    def op(self, eng, method, *args, waits=(), inc=None, incn=1, **kw):
        wl = self._filter(eng, waits)
        ih = None
        ret = None
        if inc is not None:
            inc.v += incn
            ret = inc.v
            ih = inc.h

        def fn(e, method=method, args=args, kw=kw, wl=wl, ih=ih, iv=incn):
            for (h, v) in wl:
                e.wait_ge(h, v)
            ins = getattr(e, method)(*args, **kw)
            if ih is not None:
                ins.then_inc(ih, iv)

        self.rec[eng].append(fn)
        return ret

    def wait(self, eng, *waits):
        wl = self._filter(eng, waits)
        if not wl:
            return

        def fn(e, wl=wl):
            for (h, v) in wl:
                e.wait_ge(h, v)

        self.rec[eng].append(fn)

    def replay(self, block):
        recs = self.rec

        @block.sync
        def _(e):
            for fn in recs["sync"]:
                fn(e)

        @block.gpsimd
        def _(e):
            for fn in recs["gpsimd"]:
                fn(e)

        @block.vector
        def _(e):
            for fn in recs["vector"]:
                fn(e)

        @block.scalar
        def _(e):
            for fn in recs["scalar"]:
                fn(e)

        @block.tensor
        def _(e):
            for fn in recs["tensor"]:
                fn(e)


def build(nc: bass.Bass, dbg: bool = False, phases: int = 99):
    b = Builder(nc)
    ctx = ExitStack()
    e = ctx.enter_context

    x_h = nc.dram_tensor("x", [T, E], F32, kind="ExternalInput")
    wq_h = nc.dram_tensor("wq", [E, 4096], F32, kind="ExternalInput")
    wk_h = nc.dram_tensor("wk", [E, 1024], F32, kind="ExternalInput")
    wv_h = nc.dram_tensor("wv", [E, 512], F32, kind="ExternalInput")
    wo_h = nc.dram_tensor("wo", [E, E], F32, kind="ExternalInput")
    lq1_h = nc.dram_tensor("lq1", [H, D], F32, kind="ExternalInput")
    lq2_h = nc.dram_tensor("lq2", [H, D], F32, kind="ExternalInput")
    lk1_h = nc.dram_tensor("lk1", [H, D], F32, kind="ExternalInput")
    lk2_h = nc.dram_tensor("lk2", [H, D], F32, kind="ExternalInput")
    rms_h = nc.dram_tensor("rms", [1, D], F32, kind="ExternalInput")
    gwt_h = nc.dram_tensor("gwt", [E, NEXP], F32, kind="ExternalInput")
    eb_h = nc.dram_tensor("eb", [1, NEXP], F32, kind="ExternalInput")
    gt_h = nc.dram_tensor("gt", [NEXP, E, I_FF], F32, kind="ExternalInput")
    ut_h = nc.dram_tensor("ut", [NEXP, E, I_FF], F32, kind="ExternalInput")
    dt_h = nc.dram_tensor("dtp", [NEXP, I_FF, E], F32, kind="ExternalInput")
    sgt_h = nc.dram_tensor("sgt", [E, I_FF], F32, kind="ExternalInput")
    sut_h = nc.dram_tensor("sut", [E, I_FF], F32, kind="ExternalInput")
    sdt_h = nc.dram_tensor("sdt", [I_FF, E], F32, kind="ExternalInput")
    out_h = nc.dram_tensor("out", [T, E], F32, kind="ExternalOutput")

    skind = "ExternalOutput" if dbg else "Internal"
    xn_hbm = nc.dram_tensor("xn_hbm", [128, 16 * T], BF16, kind=skind)
    q_hbm = nc.dram_tensor("q_hbm", [128, 32 * T], BF16, kind=skind)

    ACC = e(nc.sbuf_tensor("ACC", [128, 4 * E], F32))
    XNH = e(nc.sbuf_tensor("XNH", [128, 16 * 512], BF16))
    KFM = e(nc.sbuf_tensor("KFM", [128, 8 * T], BF16))
    VTOK = e(nc.sbuf_tensor("VTOK", [128, NTT * 512], BF16))
    CW = e(nc.sbuf_tensor("CWb", [128, NTT * NEXP], F32))
    WST = e(nc.sbuf_tensor("WST", [128, 2 * 8192], BF16))
    OVL = e(nc.sbuf_tensor("OVL", [128, 35328], BF16))
    TRIB = e(nc.sbuf_tensor("TRIB", [128, 128], BF16))
    IDB = e(nc.sbuf_tensor("IDB", [128, 128], BF16))
    IDF = e(nc.sbuf_tensor("IDF", [16, 16], F32))
    RMSREP = e(nc.sbuf_tensor("RMSREP", [128, 128], BF16))
    LAMBC = e(nc.sbuf_tensor("LAMBC", [128, 16], F32))
    EBREP = e(nc.sbuf_tensor("EBREP", [128, 16], F32))
    ROWS = e(nc.sbuf_tensor("ROWS", [1, 544], F32))
    COLS = e(nc.sbuf_tensor("COLS", [128, 16], F32))
    RW = e(nc.sbuf_tensor("RW", [128, 112], F32))
    HSQ = e(nc.sbuf_tensor("HSQ", [128, 16], F32))
    HST = e(nc.sbuf_tensor("HST", [128, 64], F32))

    ONESROW = ROWS.ap()[:, 0:128]
    THROW = ROWS.ap()[:, 128:256]
    TOKR = ROWS.ap()[:, 256:384]
    RMSROW = ROWS.ap()[:, 384:512]
    LAMROW = ROWS.ap()[:, 512:528]
    EBROW = ROWS.ap()[:, 528:544]

    ZERO1 = COLS.ap()[:, 0:1]
    EPS5C = COLS.ap()[:, 1:2]
    EPS8C = COLS.ap()[:, 2:3]
    EPS20C = COLS.ap()[:, 3:4]
    ONEC = COLS.ap()[:, 4:5]
    LI16 = COLS.ap()[:, 5:6]
    C_A = COLS.ap()[:, 6:7]
    C_B = COLS.ap()[:, 7:8]
    C_C = COLS.ap()[:, 8:9]
    C_D = COLS.ap()[:, 9:10]
    C_E = COLS.ap()[:, 10:11]
    C_F = COLS.ap()[:, 11:12]
    C_G = COLS.ap()[:, 12:13]
    C_H = COLS.ap()[:, 13:14]
    C_I = COLS.ap()[:, 14:15]
    C_J = COLS.ap()[:, 15:16]

    H_D0 = HST.ap()[:, 0:16]
    H_D1 = HST.ap()[:, 16:32]
    H_D2 = HST.ap()[:, 32:48]
    H_SC = HST.ap()[:, 48:64]

    R_SCO = RW.ap()[:, 0:16]
    R_BIA = RW.ap()[:, 16:32]
    R_GS = RW.ap()[:, 32:36]
    R_GM = RW.ap()[:, 36:40]
    R_GMN = RW.ap()[:, 40:44]
    R_MSK = RW.ap()[:, 48:64]
    R_SEL = RW.ap()[:, 64:80]
    R_PAD = RW.ap()[:, 80:88]
    R_MX8 = RW.ap()[:, 88:96]
    R_T2 = RW.ap()[:, 44:45]
    R_T4 = RW.ap()[:, 45:46]
    R_WS = RW.ap()[:, 46:47]
    R_WI = RW.ap()[:, 47:48]
    R_D01 = RW.ap()[:, 100:104]
    R_D02 = RW.ap()[:, 104:108]
    R_MSK2 = RW.ap()[:, 80:96]
    R_EQ = RW.ap()[:, 96:112]
    A_T1 = RW.ap()[:, 0:16]
    A_T2 = RW.ap()[:, 16:32]
    A_T3 = RW.ap()[:, 32:48]
    A_T4 = RW.ap()[:, 48:64]

    def bfv(off, n):
        return OVL.ap()[:, off:off + n]

    def f32v(off, n):
        return OVL.ap()[:, off:off + 2 * n].bitcast(F32)

    XNA = bfv(0, 16384)
    XNA3 = XNA.rearrange("p (c n) -> p c n", n=1024)
    V_XT = f32v(16384, 2048)
    V_XNT = bfv(20480, 2048)
    TRIG_OFF = 22528

    def trig_sin(tt):
        return bfv(TRIG_OFF + tt * 256, 128)

    def trig_cos(tt):
        return bfv(TRIG_OFF + tt * 256 + 128, 128)

    V_RT1 = f32v(26624, 128)
    V_RT2 = f32v(26880, 128)
    V_RT3 = f32v(27136, 128)
    V_RT4 = f32v(27392, 128)
    V_STG0 = bfv(27648, 512)
    V_STG1 = bfv(28160, 512)
    V_TRA = f32v(28672, 128)
    V_TRB = f32v(28928, 128)
    V_LAMT = f32v(29184, 1024)

    def a_qb(par):
        return bfv(0, 4096)

    A_PT = bfv(4096, 2048)
    A_STG = bfv(4096, 512)

    def a_e0(ep):
        return bfv(6144 + ep * 4096, 2048)

    def a_e1(ep):
        return bfv(8192 + ep * 4096, 2048)

    def a_att(qtl, h):
        return bfv(14336 + qtl * 2048 + h * 128, 128)

    def a_att4(qtl, h0):
        return OVL.ap()[:, 14336 + qtl * 2048 + h0 * 128:
                        14336 + qtl * 2048 + (h0 + 4) * 128]

    F_HF = bfv(22528, 4096)
    F_SIL = bfv(26624, 512)

    def f_dtc(par, ib4):
        off = 27136 + par * 2048
        return OVL.ap()[:, off + ib4 * 512: off + (ib4 + 1) * 512]

    def f_dtc3(par):
        off = 27136 + par * 2048
        return OVL.ap()[:, off:off + 2048].rearrange(
            "p (c n) -> p c n", n=512)

    GWT = WST.ap()[:, 0:256]

    def wstq(bj, c):
        return WST.ap()[:, bj * 4096 + c * 512: bj * 4096 + (c + 1) * 512]

    def wstq3(bj):
        return WST.ap()[:, bj * 4096: bj * 4096 + 4096].rearrange(
            "p (c n) -> p c n", n=512)

    def wstq_g(bj, ec):
        return WST.ap()[:, bj * 4096 + ec * 128:
                        bj * 4096 + ec * 128 + 128]

    def wstq_g3(bj):
        return WST.ap()[:, bj * 4096: bj * 4096 + 2048].rearrange(
            "p (c n) -> p c n", n=128)

    def wstq_u(bj, ec):
        return WST.ap()[:, bj * 4096 + 2048 + ec * 128:
                        bj * 4096 + 2048 + ec * 128 + 128]

    def wstq_u3(bj):
        return WST.ap()[:, bj * 4096 + 2048:
                        bj * 4096 + 4096].rearrange(
            "p (c n) -> p c n", n=128)

    def wst(par, c):
        return WST.ap()[:, par * 8192 + c * 512: par * 8192 + (c + 1) * 512]

    def wst3(par, ncols=512):
        return WST.ap()[:, par * 8192: par * 8192 + 16 * ncols].rearrange(
            "p (c n) -> p c n", n=ncols)

    def wst_g(par, ec):
        return WST.ap()[:, par * 8192 + ec * 128:
                        par * 8192 + ec * 128 + 128]

    def wst_g3(par):
        return WST.ap()[:, par * 8192: par * 8192 + 2048].rearrange(
            "p (c n) -> p c n", n=128)

    def wst_u(par, ec):
        return WST.ap()[:, par * 8192 + 2048 + ec * 128:
                        par * 8192 + 2048 + ec * 128 + 128]

    def wst_u3(par):
        return WST.ap()[:, par * 8192 + 2048: par * 8192 + 4096].rearrange(
            "p (c n) -> p c n", n=128)

    XNB_ALL = ACC.ap().bitcast(BF16)
    XNH3 = XNH.ap().rearrange("p (c n) -> p c n", n=512)
    XNB3 = XNB_ALL[:, 0:16384].rearrange("p (c n) -> p c n", n=1024)

    def xn_tile(ec, tt):
        if tt < HALF:
            return XNA[:, ec * 1024 + tt * 128:
                       ec * 1024 + (tt + 1) * 128]
        return XNB_ALL[:, ec * 1024 + (tt - HALF) * 128:
                       ec * 1024 + (tt - HALF + 1) * 128]

    PA = Bank(e(nc.psum_tensor("PA", [128, 512], F32)))
    PB = Bank(e(nc.psum_tensor("PBK", [128, 512], F32)))
    PC = Bank(e(nc.psum_tensor("PCK", [128, 512], F32)))
    PD = Bank(e(nc.psum_tensor("PDK", [128, 512], F32)))
    PE_ = Bank(e(nc.psum_tensor("PEK", [128, 512], F32)))
    PF = Bank(e(nc.psum_tensor("PFK", [128, 512], F32)))
    PT1 = Bank(e(nc.psum_tensor("PT1", [128, 1024], BF16)))
    PT2 = Bank(e(nc.psum_tensor("PT2", [128, 1024], BF16)))

    sd = S(e(nc.semaphore("sd")))
    sg = S(e(nc.semaphore("sg")))
    sv = S(e(nc.semaphore("sv")))
    sa = S(e(nc.semaphore("sa")))
    sp = S(e(nc.semaphore("sp")))
    ESEM = {"vector": sv, "scalar": sa, "tensor": sp, "gpsimd": sg,
            "sync": sd}

    def vop(method, *args, waits=(), **kw):
        v = b.op("vector", method, *args, waits=waits, inc=sv, **kw)
        b.wait("vector", (sv, v))
        return v

    def pe_open(bank):
        if bank.cons:
            b.wait("tensor", *bank.cons)

    def pe_close(bank):
        bank.prod = (sp, sp.v)
        bank.cons = []

    def consume(bank, eng):
        assert bank.prod is not None
        b.wait(eng, bank.prod)

    def consumed(bank, eng):
        s = ESEM[eng]
        bank.cons.append((s, s.v))

    b.op("gpsimd", "memset", ONESROW, 1.0)
    b.op("gpsimd", "memset", COLS[:, :], 0.0)
    b.op("gpsimd", "memset", EPS5C, 1e-5)
    b.op("gpsimd", "memset", EPS8C, 1e-8)
    b.op("gpsimd", "memset", EPS20C, 1e-20)
    b.op("gpsimd", "memset", ONEC, 1.0)
    b.op("gpsimd", "memset", LI16, LAM_INIT)
    b.op("gpsimd", "memset", R_PAD, -3e38)
    b.op("gpsimd", "memset", TRIB[:, :], 0.0)
    b.op("gpsimd", "affine_select", TRIB[:, :], TRIB[:, :], [[-1, 128]],
         ALU.is_ge, NEG, base=0, channel_multiplier=1)
    b.op("gpsimd", "memset", IDB[:, :], 1.0)
    b.op("gpsimd", "affine_select", IDB[:, :], IDB[:, :], [[1, 128]],
         ALU.is_equal, 0.0, base=0, channel_multiplier=-1)
    b.op("gpsimd", "memset", IDF[:, :], 1.0)
    g_init = b.op("gpsimd", "affine_select", IDF[:, :], IDF[:, :], [[1, 16]],
                  ALU.is_equal, 0.0, base=0, channel_multiplier=-1, inc=sg)
    g_iota = b.op("gpsimd", "iota", THROW, [[1, 128]], base=0,
                  channel_multiplier=0,
                  allow_small_or_imprecise_dtypes=True, inc=sg)
    a_th = b.op("scalar", "activation", THROW, THROW, AF.Exp, bias=0.0,
                scale=-math.log(10000.0) / 128.0,
                waits=[(sg, g_iota)], inc=sa)
    v_th = b.op("vector", "tensor_scalar", THROW, THROW,
                1.0 / (2 * math.pi), None, ALU.mult,
                waits=[(sa, a_th)], inc=sv)

    d_rms = b.op("sync", "dma_start", out=RMSROW, in_=rms_h.ap(), inc=sd,
                 incn=16)
    lamt_q1 = V_LAMT[0:16, 0:128]
    lamt_k1 = V_LAMT[0:16, 128:256]
    lamt_q2 = V_LAMT[0:16, 256:384]
    lamt_k2 = V_LAMT[0:16, 384:512]
    b.op("sync", "dma_start", out=lamt_q1, in_=lq1_h.ap(), inc=sd, incn=16)
    b.op("sync", "dma_start", out=lamt_k1, in_=lk1_h.ap(), inc=sd, incn=16)
    b.op("sync", "dma_start", out=lamt_q2, in_=lq2_h.ap(), inc=sd, incn=16)
    d_l = b.op("sync", "dma_start", out=lamt_k2, in_=lk2_h.ap(), inc=sd,
               incn=16)
    d_eb = b.op("sync", "dma_start", out=EBROW, in_=eb_h.ap(), inc=sd,
                incn=16)
    d_gwt = b.op("gpsimd", "dma_start",
                 out=GWT.rearrange("p (c n) -> p c n", n=16),
                 in_=bass.AP(gwt_h, 0, [[16, 128], [128 * 16, 16], [1, 16]]),
                 inc=sg, incn=16)

    lam_c = V_LAMT[0:16, 512:513]
    lam_t = V_LAMT[0:16, 516:517]
    p1 = V_LAMT[0:16, 640:768]
    vop("tensor_tensor", p1, lamt_q1, lamt_k1, ALU.mult,
        waits=[(sd, d_l), (sg, g_init)])
    vop("reduce_sum", lam_c, p1, axis=AX.X)
    vop("tensor_tensor", p1, lamt_q2, lamt_k2, ALU.mult)
    v_l1 = vop("reduce_sum", lam_t, p1, axis=AX.X)
    b.op("scalar", "activation", lam_c, lam_c, AF.Exp, bias=0.0, scale=1.0,
         waits=[(sv, v_l1)])
    a_l2 = b.op("scalar", "activation", lam_t, lam_t, AF.Exp, bias=0.0,
                scale=1.0, inc=sa)
    vop("tensor_tensor", lam_c, lam_c, lam_t, ALU.subtract,
        waits=[(sa, a_l2)])
    vop("tensor_tensor", lam_c, lam_c, LI16[0:16, :], ALU.add)
    v_lam = vop("tensor_tensor", lam_c, ZERO1[0:16, :], lam_c, ALU.subtract)
    pe_open(PE_)
    b.op("tensor", "transpose", PE_.t[0:1, 0:16], lam_c, IDF[:, :],
         waits=[(sv, v_lam), (sg, g_init)], inc=sp)
    pe_close(PE_)
    consume(PE_, "vector")
    v_lr = b.op("vector", "tensor_copy", LAMROW, PE_.t[0:1, 0:16], inc=sv)
    consumed(PE_, "vector")
    pe_open(PE_)
    b.op("tensor", "matmul", PE_.t[:, 0:16], ONESROW, LAMROW,
         start=True, stop=True, waits=[(sv, v_lr)], inc=sp)
    pe_close(PE_)
    consume(PE_, "vector")
    v_lambc = b.op("vector", "tensor_copy", LAMBC[:, :], PE_.t[:, 0:16],
                   inc=sv)
    consumed(PE_, "vector")
    pe_open(PF)
    b.op("tensor", "matmul", PF.t[:, 0:16], ONESROW, EBROW,
         start=True, stop=True, waits=[(sd, d_eb)], inc=sp)
    pe_close(PF)
    consume(PF, "vector")
    v_ebrep = b.op("vector", "tensor_copy", EBREP[:, :], PF.t[:, 0:16],
                   inc=sv)
    consumed(PF, "vector")
    pe_open(PC)
    b.op("tensor", "matmul", PC.t[:, 0:128], ONESROW, RMSROW,
         start=True, stop=True, waits=[(sd, d_rms)], inc=sp)
    pe_close(PC)
    consume(PC, "vector")
    v_rms = b.op("vector", "tensor_scalar", RMSREP[:, :], PC.t[:, 0:128],
                 1.0 - LAM_INIT, None, ALU.mult, inc=sv)
    consumed(PC, "vector")

    for tt in range(NTT):
        g_tr = b.op("gpsimd", "iota", TOKR, [[1, 128]], base=tt * 128,
                    channel_multiplier=0,
                    allow_small_or_imprecise_dtypes=True, inc=sg,
                    waits=[(sa, sa.v)])
        bank = PE_ if tt % 2 == 0 else PF
        pe_open(bank)
        b.op("tensor", "matmul", bank.t[:, 0:128], TOKR, THROW,
             start=True, stop=True, waits=[(sg, g_tr), (sv, v_th)], inc=sp)
        pe_close(bank)
        consume(bank, "vector")
        b.op("vector", "tensor_scalar", V_TRA, bank.t[:, 0:128], MAGIC,
             MAGIC, ALU.add, ALU.subtract)
        b.op("vector", "tensor_tensor", trig_sin(tt), bank.t[:, 0:128],
             V_TRA, ALU.subtract)
        b.op("vector", "tensor_scalar", V_TRB, bank.t[:, 0:128], 0.25, None,
             ALU.add)
        consumed(bank, "vector")
        b.op("vector", "tensor_scalar", V_TRA, V_TRB, MAGIC, MAGIC,
             ALU.add, ALU.subtract)
        v_tr = b.op("vector", "tensor_tensor", trig_cos(tt), V_TRB, V_TRA,
                    ALU.subtract, inc=sv)
        b.op("scalar", "activation", trig_sin(tt), trig_sin(tt), AF.Sin,
             bias=0.0, scale=2 * math.pi, waits=[(sv, v_tr)])
        b.op("scalar", "activation", trig_cos(tt), trig_cos(tt),
             AF.Sin, bias=0.0, scale=2 * math.pi, inc=sa)
    trig_done = sa.v

    v_xt_guard = sv.v
    a_xt_guard = trig_done
    x_guard_sp = sp.v
    b.wait("vector", (sg, g_init))
    b.wait("scalar", (sg, g_init))
    b.wait("tensor", (sg, g_init))
    for tt in range(NTT if phases >= 1 else 0):
        half_a = tt < HALF
        d_x = b.op("sync", "dma_start", out=V_XT,
                   in_=bass.AP(x_h, tt * 128 * E, [[E, 128], [1, E]]),
                   waits=[(sv, v_xt_guard), (sa, a_xt_guard),
                          (sp, x_guard_sp)],
                   inc=sd, incn=16)
        v_m = vop("reduce_sum", C_A, V_XT, axis=AX.X, waits=[(sd, d_x)])
        b.op("scalar", "mul", C_B, C_A, 1.0 / E, waits=[(sv, v_m)])
        a_xt_guard = b.op("scalar", "activation", V_XNT, V_XT, AF.Square,
                          bias=0.0, scale=1.0, accum_out=C_C, inc=sa)
        vop("tensor_tensor", C_D, C_B, C_B, ALU.mult,
            waits=[(sa, a_xt_guard)])
        v_s = vop("tensor_tensor", C_E, EPS5C, C_D, ALU.subtract)
        a_v = b.op("scalar", "activation", C_G, C_C, AF.Sqrt, bias=C_E,
                   scale=1.0 / E, waits=[(sv, v_s)], inc=sa)
        vop("reciprocal", C_H, C_G, waits=[(sa, a_v)])
        vop("tensor_tensor", C_I, C_B, C_H, ALU.mult)
        vop("tensor_tensor", C_J, ZERO1, C_I, ALU.subtract)
        v_xn = b.op("vector", "tensor_scalar", V_XNT, V_XT, C_H, C_J,
                    ALU.mult, ALU.add, inc=sv)
        v_xt_guard = v_xn
        for grp in range(4):
            tb = PT1 if grp % 2 == 0 else PT2
            pe_open(tb)
            for j in range(4):
                ec = grp * 4 + j
                b.op("tensor", "transpose", tb.t[:, j * 128:(j + 1) * 128],
                     V_XNT[:, ec * 128:(ec + 1) * 128], IDB[:, :],
                     waits=[(sv, v_xn)] if j == 0 else (),
                     inc=sp if j == 3 else None)
            pe_close(tb)
            consume(tb, "vector")
            if half_a:
                dst = XNA3[:, grp * 4:grp * 4 + 4, tt * 128:(tt + 1) * 128]
            else:
                dst = XNB3[:, grp * 4:grp * 4 + 4,
                           (tt - HALF) * 128:(tt - HALF + 1) * 128]
            b.op("vector", "tensor_copy", dst,
                 tb.t.ap().rearrange("p (c n) -> p c n", n=128)[:, 0:4, :], inc=sv)
            consumed(tb, "vector")
        v_tiles = sv.v
        src3 = (XNA3 if half_a else XNB3)[
            :, :, (tt % HALF) * 128:(tt % HALF + 1) * 128]
        b.op("sync", "dma_start",
             out=bass.AP(xn_hbm, tt * 128,
                         [[16 * T, 128], [T, 16], [1, 128]]),
             in_=src3, waits=[(sv, v_tiles)], inc=sd, incn=16)
        lbank = PE_ if tt % 2 == 0 else PF
        pe_open(lbank)
        for ec in range(16):
            b.op("tensor", "matmul", lbank.t[:, 0:16],
                 xn_tile(ec, tt), GWT[:, ec * 16:(ec + 1) * 16],
                 start=(ec == 0), stop=(ec == 15),
                 waits=([(sv, v_tiles), (sg, d_gwt)] if ec == 0 else ()),
                 inc=sp if ec == 15 else None)
        pe_close(lbank)
        consume(lbank, "scalar")
        a_sig = b.op("scalar", "activation", R_SCO, lbank.t[:, 0:16],
                     AF.Sigmoid, bias=0.0, scale=1.0, inc=sa)
        consumed(lbank, "scalar")
        vop("tensor_tensor", R_BIA, R_SCO, EBREP[:, :], ALU.add,
            waits=[(sa, a_sig), (sv, v_ebrep)])
        A0 = RW.ap()[:, 16:32][:, 0::4]
        A1 = RW.ap()[:, 16:32][:, 1::4]
        A2 = RW.ap()[:, 16:32][:, 2::4]
        A3 = RW.ap()[:, 16:32][:, 3::4]
        T_M1 = RW.ap()[:, 96:100]
        T_m1 = RW.ap()[:, 100:104]
        T_M2 = RW.ap()[:, 104:108]
        T_m2 = RW.ap()[:, 108:112]
        vop("tensor_tensor", T_M1, A0, A1, ALU.max)
        vop("tensor_tensor", T_m1, A0, A1, ALU.min)
        vop("tensor_tensor", T_M2, A2, A3, ALU.max)
        vop("tensor_tensor", T_m2, A2, A3, ALU.min)
        vop("tensor_tensor", R_GS, T_M1, T_M2, ALU.max)
        vop("tensor_tensor", T_m1, T_m1, T_m2, ALU.max)
        vop("tensor_tensor", T_M1, T_M1, T_M2, ALU.min)
        vop("tensor_tensor", T_M1, T_M1, T_m1, ALU.max)
        vop("tensor_tensor", R_GS, R_GS, T_M1, ALU.add)
        g0 = R_GS[:, 0:1]; g1 = R_GS[:, 1:2]
        g2 = R_GS[:, 2:3]; g3 = R_GS[:, 3:4]
        c1 = RW.ap()[:, 96:97]; c2 = RW.ap()[:, 97:98]
        c3 = RW.ap()[:, 98:99]; c4 = RW.ap()[:, 99:100]
        vop("tensor_tensor", c1, g0, g1, ALU.max)
        vop("tensor_tensor", c2, g0, g1, ALU.min)
        vop("tensor_tensor", c3, g2, g3, ALU.max)
        vop("tensor_tensor", c4, g2, g3, ALU.min)
        vop("tensor_tensor", c2, c2, c4, ALU.max)
        vop("tensor_tensor", c1, c1, c3, ALU.min)
        vop("tensor_tensor", R_T2, c1, c2, ALU.max)
        vop("tensor_single_scalar", R_GM, R_GS, R_T2, ALU.is_ge)
        vop("tensor_scalar", R_GMN, R_GM, 1.0, BIG, ALU.subtract, ALU.mult)
        for g in range(4):
            vop("tensor_single_scalar", R_MSK[:, 4 * g:4 * g + 4],
                R_BIA[:, 4 * g:4 * g + 4], R_GMN[:, g:g + 1], ALU.add)
        vop("tensor_copy", R_MSK2, R_MSK)
        for kext in range(4):
            vop("reduce_max", R_T4, R_MSK2, axis=AX.X)
            if kext < 3:
                vop("tensor_single_scalar", R_EQ, R_MSK2, R_T4, ALU.is_ge)
                vop("tensor_scalar", R_EQ, R_EQ, -BIG, None, ALU.mult)
                vop("tensor_tensor", R_MSK2, R_MSK2, R_EQ, ALU.add)
        vop("tensor_single_scalar", R_SEL, R_MSK, R_T4, ALU.is_ge)
        vop("tensor_tensor", R_SEL, R_SEL, R_SCO, ALU.mult)
        vop("reduce_sum", R_WS, R_SEL, axis=AX.X)
        vop("tensor_tensor", R_WS, R_WS, EPS20C, ALU.add)
        vop("reciprocal", R_WI, R_WS)
        b.op("vector", "tensor_scalar", CW.ap()[:, tt * 16:(tt + 1) * 16],
             R_SEL, R_WI, 2.5, ALU.mult, ALU.mult, inc=sv)
    xn_all_v = sv.v

    wst_free = [None, None, None, None]

    def load_slab(par, src_h, col0, ncols=512):
        w = [x for x in (wst_free[2 * par], wst_free[2 * par + 1])
             if x is not None]
        return b.op("gpsimd", "dma_start", out=wst3(par)[:, :, 0:ncols],
                    in_=bass.AP(src_h, col0,
                                [[src_h.shape[1], 128],
                                 [128 * src_h.shape[1], 16], [1, ncols]]),
                    waits=w, inc=sg, incn=16)

    wst_free[0] = wst_free[1] = (sp, sp.v)
    d_wv = load_slab(0, wv_h, 0) if phases >= 2 else 0
    for tt in range(NTT if phases >= 2 else 0):
        bank = PC if tt % 2 == 0 else PD
        pe_open(bank)
        for ec in range(16):
            b.op("tensor", "matmul", bank.t[:, :], xn_tile(ec, tt),
                 wst(0, ec), start=(ec == 0), stop=(ec == 15),
                 waits=([(sg, d_wv), (sv, xn_all_v)] if ec == 0 else ()),
                 inc=sp if ec == 15 else None)
        pe_close(bank)
        consume(bank, "scalar")
        b.op("scalar", "activation", VTOK.ap()[:, tt * 512:(tt + 1) * 512],
             bank.t[:, :], AF.Copy, bias=0.0, scale=1.0, inc=sa)
        consumed(bank, "scalar")
    wst_free[0] = wst_free[1] = (sp, sp.v)
    vtok_done = sa.v

    def rope_emit(bank, tt, out_stage):
        last = None
        for hs in range(2):
            base = hs * 256
            re = bank.t[:, base:base + 256:2]
            im = bank.t[:, base + 1:base + 256:2]
            sre = out_stage[:, base:base + 256:2]
            sim = out_stage[:, base + 1:base + 256:2]
            b.op("vector", "tensor_tensor", V_RT1, re, trig_cos(tt),
                 ALU.mult)
            b.op("vector", "tensor_tensor", V_RT2, im, trig_sin(tt),
                 ALU.mult)
            b.op("vector", "tensor_tensor", sre, V_RT1, V_RT2, ALU.subtract)
            b.op("vector", "tensor_tensor", V_RT3, re, trig_sin(tt),
                 ALU.mult)
            b.op("vector", "tensor_tensor", V_RT4, im, trig_cos(tt),
                 ALU.mult)
            last = b.op("vector", "tensor_tensor", sim, V_RT3, V_RT4,
                        ALU.add, inc=sv if hs == 1 else None)
        return last

    stg0_guard = 0
    for kfc in range(2 if phases >= 2 else 0):
        par = kfc % 2
        d_wk = load_slab(par, wk_h, kfc * 512)
        for tt in range(NTT):
            bank = PC if tt % 2 == 0 else PD
            pe_open(bank)
            for ec in range(16):
                b.op("tensor", "matmul", bank.t[:, :], xn_tile(ec, tt),
                     wst(par, ec), start=(ec == 0), stop=(ec == 15),
                     waits=([(sg, d_wk)] if ec == 0 else ()),
                     inc=sp if ec == 15 else None)
            pe_close(bank)
            consume(bank, "vector")
            b.wait("vector", (sa, trig_done), (sp, stg0_guard))
            v_r = rope_emit(bank, tt, V_STG0)
            consumed(bank, "vector")
            tb = PT1 if tt % 2 == 0 else PT2
            pe_open(tb)
            for j in range(4):
                b.op("tensor", "transpose", tb.t[:, j * 128:(j + 1) * 128],
                     V_STG0[:, j * 128:(j + 1) * 128], IDB[:, :],
                     waits=[(sv, v_r)] if j == 0 else (),
                     inc=sp if j == 3 else None)
            pe_close(tb)
            stg0_guard = sp.v
            consume(tb, "vector")
            dst = KFM.ap().rearrange("p (c n) -> p c n", n=T)[
                :, kfc * 4:kfc * 4 + 4, tt * 128:(tt + 1) * 128]
            b.op("vector", "tensor_copy", dst,
                 tb.t.ap().rearrange("p (c n) -> p c n", n=128)[:, 0:4, :], inc=sv)
            consumed(tb, "vector")
        wst_free[2 * par] = wst_free[2 * par + 1] = (sp, sp.v)
    kfm_done = sv.v

    q_stg_dma = {0: 0, 1: 0}
    q_stg_sp = {0: stg0_guard, 1: 0}
    for qfc in range(8 if phases >= 2 else 0):
        par = qfc % 2
        d_wq = load_slab(par, wq_h, qfc * 512)
        for tt in range(NTT):
            bank = PC if tt % 2 == 0 else PD
            pe_open(bank)
            for ec in range(16):
                b.op("tensor", "matmul", bank.t[:, :], xn_tile(ec, tt),
                     wst(par, ec), start=(ec == 0), stop=(ec == 15),
                     waits=([(sg, d_wq)] if ec == 0 else ()),
                     inc=sp if ec == 15 else None)
            pe_close(bank)
            consume(bank, "vector")
            sid = tt % 2
            stg = V_STG0 if sid == 0 else V_STG1
            b.wait("vector", (sd, q_stg_dma[sid]), (sp, q_stg_sp[sid]))
            v_r = rope_emit(bank, tt, stg)
            consumed(bank, "vector")
            tb = PT1 if tt % 2 == 0 else PT2
            pe_open(tb)
            for j in range(4):
                b.op("tensor", "transpose", tb.t[:, j * 128:(j + 1) * 128],
                     stg[:, j * 128:(j + 1) * 128], IDB[:, :],
                     waits=[(sv, v_r)] if j == 0 else (),
                     inc=sp if j == 3 else None)
            pe_close(tb)
            q_stg_sp[sid] = sp.v
            consume(tb, "vector")
            v_q = b.op("vector", "tensor_copy",
                       stg.rearrange("p (c n) -> p c n", n=128),
                       tb.t.ap().rearrange("p (c n) -> p c n", n=128)[:, 0:4, :],
                       inc=sv)
            consumed(tb, "vector")
            c0 = qfc * 4
            q_stg_dma[sid] = b.op(
                "sync", "dma_start",
                out=bass.AP(q_hbm, c0 * T + tt * 128,
                            [[32 * T, 128], [T, 4], [1, 128]]),
                in_=stg.rearrange("p (c n) -> p c n", n=128),
                waits=[(sv, v_q)], inc=sd, incn=16)
        wst_free[2 * par] = wst_free[2 * par + 1] = (sp, sp.v)
    q_spill_done = sd.v

    att_guard = [0, 0]
    xnh_guard = [0]
    dtc_free = [(sp, sp.v)] * 4

    def superphase(qn):
        q0 = qn * 4
        entry_sp = sp.v
        entry_sv = sv.v

        xnh_ready = b.op(
            "sync", "dma_start", out=XNH3[:, :, :],
            in_=bass.AP(xn_hbm, q0 * 128,
                        [[16 * T, 128], [T, 16], [1, 512]]),
            waits=([(sd, q_spill_done), (sp, entry_sp)] if qn == 0
                   else [(sp, xnh_guard[0])]),
            inc=sd, incn=16)
        acc_init = {}
        for tl in range(4):
            tt = q0 + tl
            acc_init[tl] = b.op(
                "sync", "dma_start",
                out=ACC.ap()[:, tl * E:(tl + 1) * E],
                in_=bass.AP(x_h, tt * 128 * E, [[E, 128], [1, E]]),
                inc=sd, incn=16)

        st_ = {
            "d_qb": [0, 0],
            "qb_guard": [entry_sp, entry_sp],
            "e_pt_guard": [entry_sp, entry_sp],
            "e_u_guard": [entry_sv, entry_sv],
            "pt_sp_prev": entry_sp,
            "a_p": {},
            "heads_a": 0,
            "epi_prev": entry_sv,
            "att_evac_v": 0,
            "last_acc_v": {tl: 0 for tl in range(4)},
        }

        def em_qload(qi):
            par = qi % 2
            st_["d_qb"][par] = b.op(
                "sync", "dma_start",
                out=a_qb(par).rearrange("p (c n) -> p c n", n=128),
                in_=bass.AP(q_hbm, (q0 + qi) * 128,
                            [[32 * T, 128], [T, 32], [1, 128]]),
                waits=[(sp, st_["qb_guard"][0])], inc=sd, incn=16)

        def em_part1(qi, h):
            qt = q0 + qi
            kn = qt + 1
            span = kn * 128
            nch = (span + 511) // 512
            par = qi % 2
            ep = h % 2
            kvh = h // 4
            ebuf = [a_e0(ep), a_e1(ep)]
            dcols = [R_D01, R_D02]
            for a_ in range(2):
                lhsT = a_qb(par)[:, (2 * h + a_) * 128:
                                 (2 * h + a_ + 1) * 128]
                kc = kvh * 2 + a_
                for ch in range(nch):
                    w = min(512, span - ch * 512)
                    bank = PA if ch % 2 == 0 else PB
                    is_last = ch == nch - 1
                    pe_open(bank)
                    b.op("tensor", "matmul", bank.t[:, 0:w], lhsT,
                         KFM.ap()[:, kc * T + ch * 512:
                                  kc * T + ch * 512 + w],
                         start=True, stop=not is_last,
                         waits=([(sd, st_["d_qb"][par]), (sv, kfm_done)]
                                if ch == 0 else ()),
                         inc=None if is_last else sp)
                    if is_last:
                        b.op("tensor", "matmul", bank.t[:, w - 128:w],
                             IDB[:, :], TRIB[:, :], start=False,
                             stop=True, inc=sp)
                    pe_close(bank)
                    consume(bank, "scalar")
                    if a_ == 0:
                        b.wait("scalar", (sp, st_["e_pt_guard"][ep]))
                    else:
                        b.wait("scalar", (sv, st_["e_u_guard"][ep]))
                    b.op("scalar", "activation",
                         ebuf[a_][:, ch * 512:ch * 512 + w],
                         bank.t[:, 0:w], AF.Exp, bias=0.0, scale=ESCALE,
                         accum_out=dcols[a_][:, ch:ch + 1], inc=sa)
                    consumed(bank, "scalar")
                dcol = (H_D0 if a_ == 0 else H_D1)[:, h:h + 1]
                vop("reduce_sum", dcol, dcols[a_][:, 0:nch], axis=AX.X,
                    waits=[(sa, sa.v)])
            b.op("vector", "reciprocal", C_A, H_D0[:, h:h + 1], inc=sv)
            v_rb = b.op("vector", "reciprocal", C_B, H_D1[:, h:h + 1],
                        inc=sv)
            b.wait("vector", (sv, v_rb))
            vop("tensor_tensor", C_B, C_B, LAMBC[:, h:h + 1], ALU.mult,
                waits=[(sv, v_lambc)])
            b.wait("vector", (sp, st_["e_pt_guard"][ep]))
            vop("tensor_single_scalar", a_e0(ep)[:, 0:span],
                a_e0(ep)[:, 0:span], C_A, ALU.mult)
            dsp = span - 128
            vop("tensor_tensor", a_e0(ep)[:, dsp:span],
                a_e0(ep)[:, dsp:span], TRIB[:, :], ALU.add)
            v_u = b.op("vector", "scalar_tensor_tensor",
                       a_e0(ep)[:, 0:span], a_e1(ep)[:, 0:span], C_B,
                       a_e0(ep)[:, 0:span], ALU.mult, ALU.add, inc=sv)
            st_["e_u_guard"][ep] = v_u
            st_["a_p"][h] = b.op(
                "scalar", "activation", a_e0(ep)[:, 0:span],
                a_e0(ep)[:, 0:span], AF.Exp, bias=0.0, scale=1.0,
                accum_out=H_D2[:, h:h + 1], waits=[(sv, v_u)], inc=sa)

        def em_part2(qi, h):
            qt = q0 + qi
            kn = qt + 1
            par = qi % 2
            ep = h % 2
            kvh = h // 4
            a_p = st_["a_p"][h]
            for grp in range((kn + 3) // 4):
                tb = PT1 if grp % 2 == 0 else PT2
                nj = min(4, kn - grp * 4)
                pe_open(tb)
                for j in range(nj):
                    kt = grp * 4 + j
                    b.op("tensor", "transpose",
                         tb.t[:, j * 128:(j + 1) * 128],
                         a_e0(ep)[:, kt * 128:(kt + 1) * 128], IDB[:, :],
                         waits=[(sa, a_p)] if j == 0 else (),
                         inc=sp if j == nj - 1 else None)
                pe_close(tb)
                consume(tb, "vector")
                b.op("vector", "tensor_copy",
                     A_PT.rearrange("p (c n) -> p c n", n=128)[
                         :, grp * 4:grp * 4 + nj, :],
                     tb.t.ap().rearrange("p (c n) -> p c n", n=128)[
                         :, 0:nj, :], inc=sv)
                consumed(tb, "vector")
            st_["e_pt_guard"][ep] = sp.v
            pt_done = sv.v
            ab = PE_ if h % 2 == 0 else PF
            pe_open(ab)
            for kt in range(kn):
                b.op("tensor", "matmul", ab.t[:, 0:128],
                     A_PT[:, kt * 128:(kt + 1) * 128],
                     VTOK.ap()[:, kt * 512 + kvh * 128:
                               kt * 512 + kvh * 128 + 128],
                     start=(kt == 0), stop=(kt == kn - 1),
                     waits=([(sv, pt_done), (sa, vtok_done)]
                            if kt == 0 else ()),
                     inc=sp if kt == kn - 1 else None)
            pe_close(ab)
            consume(ab, "scalar")
            b.wait("scalar", (sv, st_["epi_prev"]))
            b.wait("scalar", (sp, att_guard[0]))
            b.op("scalar", "activation", a_att(qi, h), ab.t[:, 0:128],
                 AF.Square, bias=0.0, scale=1.0,
                 accum_out=HSQ.ap()[:, h:h + 1])
            b.op("scalar", "activation", a_att(qi, h), ab.t[:, 0:128],
                 AF.Copy, bias=0.0, scale=1.0, inc=sa)
            consumed(ab, "scalar")

        def em_epi(qi):
            heads_a = sa.v
            b.wait("vector", (sa, heads_a))
            v_r3 = b.op("vector", "reciprocal", A_T3, H_D2, inc=sv)
            b.wait("vector", (sv, v_r3))
            vop("tensor_tensor", A_T4, A_T3, A_T3, ALU.mult)
            v_t = vop("tensor_tensor", A_T1, HSQ.ap()[:, :], A_T4,
                      ALU.mult)
            a_sq = b.op("scalar", "activation", A_T2, A_T1, AF.Sqrt,
                        bias=EPS8C, scale=1.0 / 128.0, waits=[(sv, v_t)],
                        inc=sa)
            v_r2 = b.op("vector", "reciprocal", A_T2, A_T2,
                        waits=[(sa, a_sq)], inc=sv)
            b.wait("vector", (sv, v_r2))
            v_sc = vop("tensor_tensor", H_SC, A_T2, A_T3, ALU.mult)
            stg_wait = sp.v
            for h4 in range(4):
                tb = PT1 if h4 % 2 == 0 else PT2
                b.wait("vector", (sp, stg_wait))
                v_stg = None
                for j in range(4):
                    h = h4 * 4 + j
                    v_stg = b.op("vector", "scalar_tensor_tensor",
                                 A_STG[:, j * 128:(j + 1) * 128],
                                 a_att(qi, h), H_SC[:, h:h + 1],
                                 RMSREP[:, :], ALU.mult, ALU.mult,
                                 waits=([(sv, v_sc), (sv, v_rms)]
                                        if j == 0 else ()),
                                 inc=sv if j == 3 else None)
                pe_open(tb)
                for j in range(4):
                    b.op("tensor", "transpose",
                         tb.t[:, j * 128:(j + 1) * 128],
                         A_STG[:, j * 128:(j + 1) * 128], IDB[:, :],
                         waits=[(sv, v_stg)] if j == 0 else (),
                         inc=sp if j == 3 else None)
                pe_close(tb)
                stg_wait = sp.v
                st_["pt_sp_prev"] = sp.v
                consume(tb, "vector")
                b.wait("vector", (sp, att_guard[0]))
                b.op("vector", "tensor_copy",
                     a_att4(qi, h4 * 4).rearrange("p (c n) -> p c n",
                                                  n=128),
                     tb.t.ap().rearrange("p (c n) -> p c n",
                                         n=128)[:, 0:4, :],
                     inc=sv)
                consumed(tb, "vector")
            st_["epi_prev"] = sv.v
            st_["qb_guard"][0] = sp.v

        def em_wo():
            att_v = st_["epi_prev"]
            for eg in range(4):
                for qtl in range(4):
                    obank = PE_ if (eg * 4 + qtl) % 2 == 0 else PF
                    pe_open(obank)
                    for hdh in range(2):
                        bj = (eg * 2 + hdh) % 4
                        w = ([wst_free[bj]] if wst_free[bj] is not None
                             else [])
                        if qtl == 0:
                            d_wo = b.op(
                                "gpsimd", "dma_start", out=wstq3(bj),
                                in_=bass.AP(wo_h,
                                            hdh * 8 * 128 * E + eg * 512,
                                            [[E, 128], [128 * E, 8],
                                             [1, 512]]),
                                waits=w, inc=sg, incn=16)
                            wo_dma[(eg, hdh)] = d_wo
                        for hd in range(8):
                            b.op("tensor", "matmul", obank.t[:, :],
                                 a_att(qtl, hdh * 8 + hd), wstq(bj, hd),
                                 start=(hdh == 0 and hd == 0),
                                 stop=(hdh == 1 and hd == 7),
                                 waits=([(sg, wo_dma[(eg, hdh)]),
                                         (sv, att_v)]
                                        if hd == 0 else ()),
                                 inc=sp if (hdh == 1 and hd == 7)
                                 else None)
                    pe_close(obank)
                    consume(obank, "vector")
                    dst = ACC.ap()[:, qtl * E + eg * 512:
                                   qtl * E + eg * 512 + 512]
                    st_["last_acc_v"][qtl] = b.op(
                        "vector", "tensor_tensor", dst, dst,
                        obank.t[:, :], ALU.add,
                        waits=[(sd, acc_init[qtl])], inc=sv)
                    consumed(obank, "vector")
                for hdh in range(2):
                    bj = (eg * 2 + hdh) % 4
                    wst_free[bj] = (sp, sp.v)
            att_guard[0] = sp.v
            att_guard[1] = sp.v

        fst = {"hf_guard": entry_sp, "hf_v": 0, "fb": 0}
        fbanks = [PC, PD, PE_]

        def em_gu(ei, ib):
            bj = gu_buf[(ei, ib)]
            d_w = gu_dma[(ei, ib)]
            nxt = (ei, ib + 2) if ib < 6 else (ei + 1, ib - 6)
            if nxt[0] <= NEXP and nxt not in gu_dma:
                load_gu(nxt[0], nxt[1])
            bg = fbanks[fst["fb"] % 3]
            bu = fbanks[(fst["fb"] + 1) % 3]
            fst["fb"] += 2
            pe_open(bg)
            for ec in range(16):
                b.op("tensor", "matmul", bg.t[:, :], wstq_g(bj, ec),
                     XNH.ap()[:, ec * 512:(ec + 1) * 512],
                     start=(ec == 0), stop=(ec == 15),
                     waits=([(sg, d_w), (sd, xnh_ready)]
                            if ec == 0 else ()),
                     inc=sp if ec == 15 else None)
            pe_close(bg)
            pe_open(bu)
            for ec in range(16):
                b.op("tensor", "matmul", bu.t[:, :], wstq_u(bj, ec),
                     XNH.ap()[:, ec * 512:(ec + 1) * 512],
                     start=(ec == 0), stop=(ec == 15),
                     inc=sp if ec == 15 else None)
            pe_close(bu)
            wst_free[bj] = (sp, sp.v)
            consume(bg, "scalar")
            a_sil = b.op("scalar", "activation", F_SIL, bg.t[:, :],
                         AF.Silu, bias=0.0, scale=1.0, inc=sa)
            consumed(bg, "scalar")
            consume(bu, "vector")
            b.wait("vector", (sp, fst["hf_guard"]))
            fst["hf_v"] = b.op(
                "vector", "tensor_tensor",
                F_HF[:, ib * 512:(ib + 1) * 512], F_SIL, bu.t[:, :],
                ALU.mult, waits=[(sa, a_sil)], inc=sv)
            consumed(bu, "vector")

        def em_down(ei, hc):
            d_d = {}
            for ibh in range(2):
                d_d[ibh] = dtc_dma.pop((ei, hc, ibh))
            if hc < 3:
                for ibh in range(2):
                    load_dtc(ei, hc + 1, ibh)
            elif ei < NEXP:
                for ibh in range(2):
                    load_dtc(ei + 1, 0, ibh)
            for tlp in range(2):
                bks = [fbanks[fst["fb"] % 3], fbanks[(fst["fb"] + 1) % 3]]
                fst["fb"] += 2
                for t2 in range(2):
                    pe_open(bks[t2])
                for ib in range(8):
                    ibh = ib // 4
                    par = (hc % 2) * 2 + ibh
                    for t2 in range(2):
                        tl = tlp * 2 + t2
                        b.op("tensor", "matmul", bks[t2].t[:, :],
                             F_HF[:, ib * 512 + tl * 128:
                                  ib * 512 + tl * 128 + 128],
                             f_dtc(par, ib % 4),
                             start=(ib == 0), stop=(ib == 7),
                             waits=([(sv, fst["hf_v"]), (sg, d_d[ibh])]
                                    if t2 == 0 and ib % 4 == 0 else ()),
                             inc=sp if ib == 7 else None)
                for t2 in range(2):
                    pe_close(bks[t2])
                for t2 in range(2):
                    tl = tlp * 2 + t2
                    cwcol = (CW.ap()[:, (q0 + tl) * 16 + ei:
                                     (q0 + tl) * 16 + ei + 1]
                             if ei < NEXP else ONEC)
                    bk = bks[t2]
                    consume(bk, "vector")
                    dst = ACC.ap()[:, tl * E + hc * 512:
                                   tl * E + hc * 512 + 512]
                    st_["last_acc_v"][tl] = b.op(
                        "vector", "scalar_tensor_tensor", dst,
                        bk.t[:, :], cwcol, dst, ALU.mult, ALU.add,
                        waits=[(sd, acc_init[tl])], inc=sv)
                    consumed(bk, "vector")
            for ibh in range(2):
                par = (hc % 2) * 2 + ibh
                dtc_free[par] = (sp, sp.v)
            if hc == 3:
                fst["hf_guard"] = sp.v

        gu_dma = {}
        gu_buf = {}
        gu_next_buf = [0]
        wo_dma = {}

        def load_gu(ei, ib):
            bj = gu_next_buf[0]
            gu_next_buf[0] = (bj + 1) % 4
            w = [wst_free[bj]] if wst_free[bj] is not None else []
            if ei < NEXP:
                gsrc = bass.AP(gt_h, ei * E * I_FF + ib * 128,
                               [[I_FF, 128], [128 * I_FF, 16], [1, 128]])
                usrc = bass.AP(ut_h, ei * E * I_FF + ib * 128,
                               [[I_FF, 128], [128 * I_FF, 16], [1, 128]])
            else:
                gsrc = bass.AP(sgt_h, ib * 128,
                               [[I_FF, 128], [128 * I_FF, 16], [1, 128]])
                usrc = bass.AP(sut_h, ib * 128,
                               [[I_FF, 128], [128 * I_FF, 16], [1, 128]])
            b.op("gpsimd", "dma_start", out=wstq_g3(bj), in_=gsrc,
                 waits=w, inc=sg, incn=16)
            gu_dma[(ei, ib)] = b.op("gpsimd", "dma_start",
                                    out=wstq_u3(bj), in_=usrc,
                                    inc=sg, incn=16)
            gu_buf[(ei, ib)] = bj

        dtc_dma = {}

        def load_dtc(ei, hc, ibh):
            par = (hc % 2) * 2 + ibh
            w = [dtc_free[par]]
            if ei < NEXP:
                s = bass.AP(dt_h,
                            ei * I_FF * E + ibh * 4 * 128 * E + hc * 512,
                            [[E, 128], [128 * E, 4], [1, 512]])
            else:
                s = bass.AP(sdt_h, ibh * 4 * 128 * E + hc * 512,
                            [[E, 128], [128 * E, 4], [1, 512]])
            dtc_dma[(ei, hc, ibh)] = b.op("gpsimd", "dma_start",
                                          out=f_dtc3(par), in_=s,
                                          waits=w, inc=sg, incn=16)

        load_gu(0, 0)
        load_gu(0, 1)
        load_dtc(0, 0, 0)
        load_dtc(0, 0, 1)

        att_stream = []
        for qi in range(4):
            att_stream.append(("qload", qi, None))
            for h in range(H):
                att_stream.append(("p1", qi, h))
                att_stream.append(("p2", qi, h))
            att_stream.append(("epi", qi, None))
        ordered = []
        pend_p2 = None
        for item in att_stream:
            if item[0] == "p1":
                ordered.append(item)
                if pend_p2 is not None:
                    ordered.append(pend_p2)
                pend_p2 = ("p2", item[1], item[2])
            elif item[0] == "p2":
                continue
            else:
                if item[0] == "epi" and pend_p2 is not None:
                    ordered.append(pend_p2)
                    pend_p2 = None
                ordered.append(item)

        ffn_stream = []
        for ei in range(NEXP + 1):
            for ib in range(8):
                ffn_stream.append(("gu", ei, ib))
            for hc in range(4):
                ffn_stream.append(("down", ei, hc))

        na, nf = len(ordered), len(ffn_stream)
        ai = fi = 0
        credit = 0.0
        ratio = na / nf
        while fi < nf or ai < na:
            if fi < nf:
                kind, x1, x2 = ffn_stream[fi]
                fi += 1
                if kind == "gu":
                    em_gu(x1, x2)
                else:
                    em_down(x1, x2)
            credit += ratio if fi <= nf else na
            while ai < na and (credit >= 1.0 or fi >= nf):
                credit -= 1.0
                kind, x1, x2 = ordered[ai]
                ai += 1
                if kind == "qload":
                    em_qload(x1)
                elif kind == "p1":
                    em_part1(x1, x2)
                elif kind == "p2":
                    em_part2(x1, x2)
                else:
                    em_epi(x1)
        xnh_guard[0] = sp.v
        em_wo()

        for tl in range(4):
            tt = q0 + tl
            b.op("sync", "dma_start",
                 out=bass.AP(out_h, tt * 128 * E, [[E, 128], [1, E]]),
                 in_=ACC.ap()[:, tl * E:(tl + 1) * E],
                 waits=[(sv, st_["last_acc_v"][tl])], inc=sd, incn=16)

    for qn in range(4):
        superphase(qn)

    b.wait("sync", (sd, sd.v))
    b.wait("gpsimd", (sg, sg.v))

    with nc.Block() as block:
        b.replay(block)
    ctx.close()
    return nc


def kernel(**inputs):
    x = np.ascontiguousarray(np.asarray(inputs["L0_attn_x"], np.float32))
    base = {
        "wq": np.ascontiguousarray(inputs["L0_attn_Wq"], dtype=np.float32),
        "wk": np.ascontiguousarray(inputs["L0_attn_Wk"], dtype=np.float32),
        "wv": np.ascontiguousarray(inputs["L0_attn_Wv"], dtype=np.float32),
        "wo": np.ascontiguousarray(inputs["L0_attn_Wo"], dtype=np.float32),
        "lq1": np.ascontiguousarray(inputs["L0_attn_lq1"], dtype=np.float32),
        "lq2": np.ascontiguousarray(inputs["L0_attn_lq2"], dtype=np.float32),
        "lk1": np.ascontiguousarray(inputs["L0_attn_lk1"], dtype=np.float32),
        "lk2": np.ascontiguousarray(inputs["L0_attn_lk2"], dtype=np.float32),
        "rms": np.ascontiguousarray(
            np.asarray(inputs["L0_attn_rms_w"], np.float32).reshape(1, D)),
        "gwt": np.ascontiguousarray(
            np.asarray(inputs["L0_ffn_gate_weight"], np.float32).T),
        "eb": np.ascontiguousarray(
            np.asarray(inputs["L0_ffn_expert_bias"],
                       np.float32).reshape(1, NEXP)),
        "gt": np.ascontiguousarray(
            np.asarray(inputs["L0_ffn_gate_proj"],
                       np.float32).transpose(0, 2, 1)),
        "ut": np.ascontiguousarray(
            np.asarray(inputs["L0_ffn_up_proj"],
                       np.float32).transpose(0, 2, 1)),
        "dtp": np.ascontiguousarray(
            np.asarray(inputs["L0_ffn_down_proj"],
                       np.float32).transpose(0, 2, 1)),
        "sgt": np.ascontiguousarray(
            np.asarray(inputs["L0_ffn_shared_gate"], np.float32).T),
        "sut": np.ascontiguousarray(
            np.asarray(inputs["L0_ffn_shared_up"], np.float32).T),
        "sdt": np.ascontiguousarray(
            np.asarray(inputs["L0_ffn_shared_down"], np.float32).T),
    }
    nc = bass.Bass("TRN2", target_bir_lowering=False)
    build(nc)
    in_maps = []
    for c in range(8):
        m = dict(base)
        m["x"] = np.ascontiguousarray(x[c])
        in_maps.append(m)
    res = run_bass_kernel_spmd(nc, in_maps, core_ids=list(range(8)))
    out = np.stack([np.asarray(r["out"]) for r in res.results], axis=0)
    return out.astype(np.float32)


# version 05 (final)
# speedup vs baseline: 1.2343x; 1.0116x over previous
import math
from contextlib import ExitStack

import numpy as np

import concourse.bass as bass
import concourse.mybir as mybir
from concourse.bass_utils import run_bass_kernel_spmd

F32 = mybir.dt.float32
BF16 = mybir.dt.bfloat16
AF = mybir.ActivationFunctionType
ALU = mybir.AluOpType
AX = mybir.AxisListType

T = 2048
E = 2048
H = 16
KVH = 4
D = 128
NTT = 16
HALF = 8
I_FF = 1024
NEXP = 16
NEG = -1e30
MAGIC = 12582912.0
ESCALE = 1.0 / math.sqrt(128.0)
LAM_INIT = 0.2
BIG = 1e32

ENGS = ("sync", "gpsimd", "vector", "scalar", "tensor")
ATT_SUB = 99
ATT_QMAX = 99


class S:
    def __init__(self, h):
        self.h = h
        self.v = 0


class Bank:
    def __init__(self, t):
        self.t = t
        self.prod = None
        self.cons = []


class Builder:
    def __init__(self, nc):
        self.nc = nc
        self.rec = {k: [] for k in ENGS}
        self.waited = {}

    def _filter(self, eng, waits):
        wl = []
        for (s, v) in waits:
            if v <= 0:
                continue
            key = (eng, s.h.name)
            if self.waited.get(key, -1) >= v:
                continue
            assert v <= s.v, (
                f"wait on future value {v} > {s.v} of {s.h.name} on {eng}")
            self.waited[key] = v
            wl.append((s.h, v))
        return tuple(wl)

    def op(self, eng, method, *args, waits=(), inc=None, incn=1, **kw):
        wl = self._filter(eng, waits)
        ih = None
        ret = None
        if inc is not None:
            inc.v += incn
            ret = inc.v
            ih = inc.h

        def fn(e, method=method, args=args, kw=kw, wl=wl, ih=ih, iv=incn):
            for (h, v) in wl:
                e.wait_ge(h, v)
            ins = getattr(e, method)(*args, **kw)
            if ih is not None:
                ins.then_inc(ih, iv)

        self.rec[eng].append(fn)
        return ret

    def wait(self, eng, *waits):
        wl = self._filter(eng, waits)
        if not wl:
            return

        def fn(e, wl=wl):
            for (h, v) in wl:
                e.wait_ge(h, v)

        self.rec[eng].append(fn)

    def replay(self, block):
        recs = self.rec

        @block.sync
        def _(e):
            for fn in recs["sync"]:
                fn(e)

        @block.gpsimd
        def _(e):
            for fn in recs["gpsimd"]:
                fn(e)

        @block.vector
        def _(e):
            for fn in recs["vector"]:
                fn(e)

        @block.scalar
        def _(e):
            for fn in recs["scalar"]:
                fn(e)

        @block.tensor
        def _(e):
            for fn in recs["tensor"]:
                fn(e)


def build(nc: bass.Bass, dbg: bool = False, phases: int = 99):
    b = Builder(nc)
    ctx = ExitStack()
    e = ctx.enter_context

    x_h = nc.dram_tensor("x", [T, E], F32, kind="ExternalInput")
    wq_h = nc.dram_tensor("wq", [E, 4096], F32, kind="ExternalInput")
    wk_h = nc.dram_tensor("wk", [E, 1024], F32, kind="ExternalInput")
    wv_h = nc.dram_tensor("wv", [E, 512], F32, kind="ExternalInput")
    wo_h = nc.dram_tensor("wo", [E, E], F32, kind="ExternalInput")
    lq1_h = nc.dram_tensor("lq1", [H, D], F32, kind="ExternalInput")
    lq2_h = nc.dram_tensor("lq2", [H, D], F32, kind="ExternalInput")
    lk1_h = nc.dram_tensor("lk1", [H, D], F32, kind="ExternalInput")
    lk2_h = nc.dram_tensor("lk2", [H, D], F32, kind="ExternalInput")
    rms_h = nc.dram_tensor("rms", [1, D], F32, kind="ExternalInput")
    gwt_h = nc.dram_tensor("gwt", [E, NEXP], F32, kind="ExternalInput")
    eb_h = nc.dram_tensor("eb", [1, NEXP], F32, kind="ExternalInput")
    gt_h = nc.dram_tensor("gt", [NEXP, E, I_FF], F32, kind="ExternalInput")
    ut_h = nc.dram_tensor("ut", [NEXP, E, I_FF], F32, kind="ExternalInput")
    dt_h = nc.dram_tensor("dtp", [NEXP, I_FF, E], F32, kind="ExternalInput")
    sgt_h = nc.dram_tensor("sgt", [E, I_FF], F32, kind="ExternalInput")
    sut_h = nc.dram_tensor("sut", [E, I_FF], F32, kind="ExternalInput")
    sdt_h = nc.dram_tensor("sdt", [I_FF, E], F32, kind="ExternalInput")
    out_h = nc.dram_tensor("out", [T, E], F32, kind="ExternalOutput")

    skind = "ExternalOutput" if dbg else "Internal"
    xn_hbm = nc.dram_tensor("xn_hbm", [128, 16 * T], BF16, kind=skind)
    q_hbm = nc.dram_tensor("q_hbm", [128, 32 * T], BF16, kind=skind)

    ACC = e(nc.sbuf_tensor("ACC", [128, 4 * E], F32))
    XNH = e(nc.sbuf_tensor("XNH", [128, 16 * 512], BF16))
    KFM = e(nc.sbuf_tensor("KFM", [128, 8 * T], BF16))
    VTOK = e(nc.sbuf_tensor("VTOK", [128, NTT * 512], BF16))
    CW = e(nc.sbuf_tensor("CWb", [128, NTT * NEXP], F32))
    WST = e(nc.sbuf_tensor("WST", [128, 2 * 8192], BF16))
    OVL = e(nc.sbuf_tensor("OVL", [128, 35328], BF16))
    TRIB = e(nc.sbuf_tensor("TRIB", [128, 128], BF16))
    IDB = e(nc.sbuf_tensor("IDB", [128, 128], BF16))
    IDF = e(nc.sbuf_tensor("IDF", [16, 16], F32))
    RMSREP = e(nc.sbuf_tensor("RMSREP", [128, 128], BF16))
    LAMBC = e(nc.sbuf_tensor("LAMBC", [128, 16], F32))
    EBREP = e(nc.sbuf_tensor("EBREP", [128, 16], F32))
    ROWS = e(nc.sbuf_tensor("ROWS", [1, 544], F32))
    COLS = e(nc.sbuf_tensor("COLS", [128, 16], F32))
    RW = e(nc.sbuf_tensor("RW", [128, 112], F32))
    HSQ = e(nc.sbuf_tensor("HSQ", [128, 16], F32))
    HST = e(nc.sbuf_tensor("HST", [128, 64], F32))

    ONESROW = ROWS.ap()[:, 0:128]
    THROW = ROWS.ap()[:, 128:256]
    TOKR = ROWS.ap()[:, 256:384]
    RMSROW = ROWS.ap()[:, 384:512]
    LAMROW = ROWS.ap()[:, 512:528]
    EBROW = ROWS.ap()[:, 528:544]

    ZERO1 = COLS.ap()[:, 0:1]
    EPS5C = COLS.ap()[:, 1:2]
    EPS8C = COLS.ap()[:, 2:3]
    EPS20C = COLS.ap()[:, 3:4]
    ONEC = COLS.ap()[:, 4:5]
    LI16 = COLS.ap()[:, 5:6]
    C_A = COLS.ap()[:, 6:7]
    C_B = COLS.ap()[:, 7:8]
    C_C = COLS.ap()[:, 8:9]
    C_D = COLS.ap()[:, 9:10]
    C_E = COLS.ap()[:, 10:11]
    C_F = COLS.ap()[:, 11:12]
    C_G = COLS.ap()[:, 12:13]
    C_H = COLS.ap()[:, 13:14]
    C_I = COLS.ap()[:, 14:15]
    C_J = COLS.ap()[:, 15:16]

    H_D0 = HST.ap()[:, 0:16]
    H_D1 = HST.ap()[:, 16:32]
    H_D2 = HST.ap()[:, 32:48]
    H_SC = HST.ap()[:, 48:64]

    R_SCO = RW.ap()[:, 0:16]
    R_BIA = RW.ap()[:, 16:32]
    R_GS = RW.ap()[:, 32:36]
    R_GM = RW.ap()[:, 36:40]
    R_GMN = RW.ap()[:, 40:44]
    R_MSK = RW.ap()[:, 48:64]
    R_SEL = RW.ap()[:, 64:80]
    R_PAD = RW.ap()[:, 80:88]
    R_MX8 = RW.ap()[:, 88:96]
    R_T2 = RW.ap()[:, 44:45]
    R_T4 = RW.ap()[:, 45:46]
    R_WS = RW.ap()[:, 46:47]
    R_WI = RW.ap()[:, 47:48]
    R_D01 = RW.ap()[:, 100:104]
    R_D02 = RW.ap()[:, 104:108]
    R_MSK2 = RW.ap()[:, 80:96]
    R_EQ = RW.ap()[:, 96:112]
    A_T1 = RW.ap()[:, 0:16]
    A_T2 = RW.ap()[:, 16:32]
    A_T3 = RW.ap()[:, 32:48]
    A_T4 = RW.ap()[:, 48:64]

    def bfv(off, n):
        return OVL.ap()[:, off:off + n]

    def f32v(off, n):
        return OVL.ap()[:, off:off + 2 * n].bitcast(F32)

    XNA = bfv(0, 16384)
    XNA3 = XNA.rearrange("p (c n) -> p c n", n=1024)
    V_XT = f32v(16384, 2048)
    V_XNT = bfv(20480, 2048)
    TRIG_OFF = 22528

    def trig_sin(tt):
        return bfv(TRIG_OFF + tt * 256, 128)

    def trig_cos(tt):
        return bfv(TRIG_OFF + tt * 256 + 128, 128)

    V_RT1 = f32v(26624, 128)
    V_RT2 = f32v(26880, 128)
    V_RT3 = f32v(27136, 128)
    V_RT4 = f32v(27392, 128)
    V_STG0 = bfv(27648, 512)
    V_STG1 = bfv(28160, 512)
    V_TRA = f32v(28672, 128)
    V_TRB = f32v(28928, 128)
    V_LAMT = f32v(29184, 1024)

    def a_qb(par):
        return bfv(0, 4096)

    A_PT = bfv(4096, 2048)
    A_STG = bfv(4096, 512)

    def a_e0(ep):
        return bfv(6144 + ep * 4096, 2048)

    def a_e1(ep):
        return bfv(8192 + ep * 4096, 2048)

    def a_att(qtl, h):
        return bfv(14336 + qtl * 2048 + h * 128, 128)

    def a_att4(qtl, h0):
        return OVL.ap()[:, 14336 + qtl * 2048 + h0 * 128:
                        14336 + qtl * 2048 + (h0 + 4) * 128]

    F_HF = bfv(22528, 4096)
    F_SIL = bfv(26624, 512)

    def f_dtc(par, ib4):
        off = 27136 + par * 2048
        return OVL.ap()[:, off + ib4 * 512: off + (ib4 + 1) * 512]

    def f_dtc3(par):
        off = 27136 + par * 2048
        return OVL.ap()[:, off:off + 2048].rearrange(
            "p (c n) -> p c n", n=512)

    GWT = WST.ap()[:, 0:256]

    def wstq(bj, c):
        return WST.ap()[:, bj * 4096 + c * 512: bj * 4096 + (c + 1) * 512]

    def wstq3(bj):
        return WST.ap()[:, bj * 4096: bj * 4096 + 4096].rearrange(
            "p (c n) -> p c n", n=512)

    def wstq_g(bj, ec):
        return WST.ap()[:, bj * 4096 + ec * 128:
                        bj * 4096 + ec * 128 + 128]

    def wstq_g3(bj):
        return WST.ap()[:, bj * 4096: bj * 4096 + 2048].rearrange(
            "p (c n) -> p c n", n=128)

    def wstq_u(bj, ec):
        return WST.ap()[:, bj * 4096 + 2048 + ec * 128:
                        bj * 4096 + 2048 + ec * 128 + 128]

    def wstq_u3(bj):
        return WST.ap()[:, bj * 4096 + 2048:
                        bj * 4096 + 4096].rearrange(
            "p (c n) -> p c n", n=128)

    def wst(par, c):
        return WST.ap()[:, par * 8192 + c * 512: par * 8192 + (c + 1) * 512]

    def wst3(par, ncols=512):
        return WST.ap()[:, par * 8192: par * 8192 + 16 * ncols].rearrange(
            "p (c n) -> p c n", n=ncols)

    def wst_g(par, ec):
        return WST.ap()[:, par * 8192 + ec * 128:
                        par * 8192 + ec * 128 + 128]

    def wst_g3(par):
        return WST.ap()[:, par * 8192: par * 8192 + 2048].rearrange(
            "p (c n) -> p c n", n=128)

    def wst_u(par, ec):
        return WST.ap()[:, par * 8192 + 2048 + ec * 128:
                        par * 8192 + 2048 + ec * 128 + 128]

    def wst_u3(par):
        return WST.ap()[:, par * 8192 + 2048: par * 8192 + 4096].rearrange(
            "p (c n) -> p c n", n=128)

    XNB_ALL = ACC.ap().bitcast(BF16)
    XNH3 = XNH.ap().rearrange("p (c n) -> p c n", n=512)
    XNB3 = XNB_ALL[:, 0:16384].rearrange("p (c n) -> p c n", n=1024)

    def xn_tile(ec, tt):
        if tt < HALF:
            return XNA[:, ec * 1024 + tt * 128:
                       ec * 1024 + (tt + 1) * 128]
        return XNB_ALL[:, ec * 1024 + (tt - HALF) * 128:
                       ec * 1024 + (tt - HALF + 1) * 128]

    PA = Bank(e(nc.psum_tensor("PA", [128, 512], F32)))
    PB = Bank(e(nc.psum_tensor("PBK", [128, 512], F32)))
    PC = Bank(e(nc.psum_tensor("PCK", [128, 512], F32)))
    PD = Bank(e(nc.psum_tensor("PDK", [128, 512], F32)))
    PE_ = Bank(e(nc.psum_tensor("PEK", [128, 512], F32)))
    PF = Bank(e(nc.psum_tensor("PFK", [128, 512], F32)))
    PT1 = Bank(e(nc.psum_tensor("PT1", [128, 1024], BF16)))
    PT2 = Bank(e(nc.psum_tensor("PT2", [128, 1024], BF16)))

    sd = S(e(nc.semaphore("sd")))
    sg = S(e(nc.semaphore("sg")))
    sv = S(e(nc.semaphore("sv")))
    sa = S(e(nc.semaphore("sa")))
    sp = S(e(nc.semaphore("sp")))
    ESEM = {"vector": sv, "scalar": sa, "tensor": sp, "gpsimd": sg,
            "sync": sd}

    def vop(method, *args, waits=(), **kw):
        v = b.op("vector", method, *args, waits=waits, inc=sv, **kw)
        b.wait("vector", (sv, v))
        return v

    def pe_open(bank):
        if bank.cons:
            b.wait("tensor", *bank.cons)

    def pe_close(bank):
        bank.prod = (sp, sp.v)
        bank.cons = []

    def consume(bank, eng):
        assert bank.prod is not None
        b.wait(eng, bank.prod)

    def consumed(bank, eng):
        s = ESEM[eng]
        bank.cons.append((s, s.v))

    b.op("gpsimd", "memset", ONESROW, 1.0)
    g0 = b.op("gpsimd", "memset", COLS[:, :], 0.0, inc=sg)
    b.wait("gpsimd", (sg, g0))
    b.op("gpsimd", "memset", EPS5C, 1e-5)
    b.op("gpsimd", "memset", EPS8C, 1e-8)
    b.op("gpsimd", "memset", EPS20C, 1e-20)
    b.op("gpsimd", "memset", ONEC, 1.0)
    b.op("gpsimd", "memset", LI16, LAM_INIT)
    b.op("gpsimd", "memset", R_PAD, -3e38)
    g0 = b.op("gpsimd", "memset", TRIB[:, :], 0.0, inc=sg)
    b.wait("gpsimd", (sg, g0))
    b.op("gpsimd", "affine_select", TRIB[:, :], TRIB[:, :], [[-1, 128]],
         ALU.is_ge, NEG, base=0, channel_multiplier=1)
    g0 = b.op("gpsimd", "memset", IDB[:, :], 1.0, inc=sg)
    b.wait("gpsimd", (sg, g0))
    b.op("gpsimd", "affine_select", IDB[:, :], IDB[:, :], [[1, 128]],
         ALU.is_equal, 0.0, base=0, channel_multiplier=-1)
    g0 = b.op("gpsimd", "memset", IDF[:, :], 1.0, inc=sg)
    b.wait("gpsimd", (sg, g0))
    g_init = b.op("gpsimd", "affine_select", IDF[:, :], IDF[:, :], [[1, 16]],
                  ALU.is_equal, 0.0, base=0, channel_multiplier=-1, inc=sg)
    g_iota = b.op("gpsimd", "iota", THROW, [[1, 128]], base=0,
                  channel_multiplier=0,
                  allow_small_or_imprecise_dtypes=True, inc=sg)
    a_th = b.op("scalar", "activation", THROW, THROW, AF.Exp, bias=0.0,
                scale=-math.log(10000.0) / 128.0,
                waits=[(sg, g_iota)], inc=sa)
    v_th = b.op("vector", "tensor_scalar", THROW, THROW,
                1.0 / (2 * math.pi), None, ALU.mult,
                waits=[(sa, a_th)], inc=sv)

    d_rms = b.op("sync", "dma_start", out=RMSROW, in_=rms_h.ap(), inc=sd,
                 incn=16)
    lamt_q1 = V_LAMT[0:16, 0:128]
    lamt_k1 = V_LAMT[0:16, 128:256]
    lamt_q2 = V_LAMT[0:16, 256:384]
    lamt_k2 = V_LAMT[0:16, 384:512]
    b.op("sync", "dma_start", out=lamt_q1, in_=lq1_h.ap(), inc=sd, incn=16)
    b.op("sync", "dma_start", out=lamt_k1, in_=lk1_h.ap(), inc=sd, incn=16)
    b.op("sync", "dma_start", out=lamt_q2, in_=lq2_h.ap(), inc=sd, incn=16)
    d_l = b.op("sync", "dma_start", out=lamt_k2, in_=lk2_h.ap(), inc=sd,
               incn=16)
    d_eb = b.op("sync", "dma_start", out=EBROW, in_=eb_h.ap(), inc=sd,
                incn=16)
    d_gwt = b.op("gpsimd", "dma_start",
                 out=GWT.rearrange("p (c n) -> p c n", n=16),
                 in_=bass.AP(gwt_h, 0, [[16, 128], [128 * 16, 16], [1, 16]]),
                 waits=[(sg, g_iota)], inc=sg, incn=16)

    lam_c = V_LAMT[0:16, 512:513]
    lam_t = V_LAMT[0:16, 516:517]
    p1 = V_LAMT[0:16, 640:768]
    vop("tensor_tensor", p1, lamt_q1, lamt_k1, ALU.mult,
        waits=[(sd, d_l), (sg, g_init)])
    vop("reduce_sum", lam_c, p1, axis=AX.X)
    vop("tensor_tensor", p1, lamt_q2, lamt_k2, ALU.mult)
    v_l1 = vop("reduce_sum", lam_t, p1, axis=AX.X)
    b.op("scalar", "activation", lam_c, lam_c, AF.Exp, bias=0.0, scale=1.0,
         waits=[(sv, v_l1)])
    a_l2 = b.op("scalar", "activation", lam_t, lam_t, AF.Exp, bias=0.0,
                scale=1.0, inc=sa)
    vop("tensor_tensor", lam_c, lam_c, lam_t, ALU.subtract,
        waits=[(sa, a_l2)])
    vop("tensor_tensor", lam_c, lam_c, LI16[0:16, :], ALU.add)
    v_lam = vop("tensor_tensor", lam_c, ZERO1[0:16, :], lam_c, ALU.subtract)
    pe_open(PE_)
    b.op("tensor", "transpose", PE_.t[0:1, 0:16], lam_c, IDF[:, :],
         waits=[(sv, v_lam), (sg, g_init)], inc=sp)
    pe_close(PE_)
    consume(PE_, "vector")
    v_lr = b.op("vector", "tensor_copy", LAMROW, PE_.t[0:1, 0:16], inc=sv)
    consumed(PE_, "vector")
    pe_open(PE_)
    b.op("tensor", "matmul", PE_.t[:, 0:16], ONESROW, LAMROW,
         start=True, stop=True, waits=[(sv, v_lr)], inc=sp)
    pe_close(PE_)
    consume(PE_, "vector")
    v_lambc = b.op("vector", "tensor_copy", LAMBC[:, :], PE_.t[:, 0:16],
                   inc=sv)
    consumed(PE_, "vector")
    pe_open(PF)
    b.op("tensor", "matmul", PF.t[:, 0:16], ONESROW, EBROW,
         start=True, stop=True, waits=[(sd, d_eb)], inc=sp)
    pe_close(PF)
    consume(PF, "vector")
    v_ebrep = b.op("vector", "tensor_copy", EBREP[:, :], PF.t[:, 0:16],
                   inc=sv)
    consumed(PF, "vector")
    pe_open(PC)
    b.op("tensor", "matmul", PC.t[:, 0:128], ONESROW, RMSROW,
         start=True, stop=True, waits=[(sd, d_rms)], inc=sp)
    pe_close(PC)
    consume(PC, "vector")
    v_rms = b.op("vector", "tensor_scalar", RMSREP[:, :], PC.t[:, 0:128],
                 1.0 - LAM_INIT, None, ALU.mult, inc=sv)
    consumed(PC, "vector")

    for tt in range(NTT):
        g_tr = b.op("gpsimd", "iota", TOKR, [[1, 128]], base=tt * 128,
                    channel_multiplier=0,
                    allow_small_or_imprecise_dtypes=True, inc=sg,
                    waits=[(sa, sa.v)])
        bank = PE_ if tt % 2 == 0 else PF
        pe_open(bank)
        b.op("tensor", "matmul", bank.t[:, 0:128], TOKR, THROW,
             start=True, stop=True, waits=[(sg, g_tr), (sv, v_th)], inc=sp)
        pe_close(bank)
        consume(bank, "vector")
        b.op("vector", "tensor_scalar", V_TRA, bank.t[:, 0:128], MAGIC,
             MAGIC, ALU.add, ALU.subtract)
        b.op("vector", "tensor_tensor", trig_sin(tt), bank.t[:, 0:128],
             V_TRA, ALU.subtract)
        b.op("vector", "tensor_scalar", V_TRB, bank.t[:, 0:128], 0.25, None,
             ALU.add)
        consumed(bank, "vector")
        b.op("vector", "tensor_scalar", V_TRA, V_TRB, MAGIC, MAGIC,
             ALU.add, ALU.subtract)
        v_tr = b.op("vector", "tensor_tensor", trig_cos(tt), V_TRB, V_TRA,
                    ALU.subtract, inc=sv)
        b.op("scalar", "activation", trig_sin(tt), trig_sin(tt), AF.Sin,
             bias=0.0, scale=2 * math.pi, waits=[(sv, v_tr)])
        b.op("scalar", "activation", trig_cos(tt), trig_cos(tt),
             AF.Sin, bias=0.0, scale=2 * math.pi, inc=sa)
    trig_done = sa.v

    v_xt_guard = sv.v
    a_xt_guard = trig_done
    x_guard_sp = sp.v
    b.wait("vector", (sg, g_init))
    b.wait("scalar", (sg, g_init))
    b.wait("tensor", (sg, g_init))
    for tt in range(NTT if phases >= 1 else 0):
        half_a = tt < HALF
        d_x = b.op("sync", "dma_start", out=V_XT,
                   in_=bass.AP(x_h, tt * 128 * E, [[E, 128], [1, E]]),
                   waits=[(sv, v_xt_guard), (sa, a_xt_guard),
                          (sp, x_guard_sp)],
                   inc=sd, incn=16)
        v_m = vop("reduce_sum", C_A, V_XT, axis=AX.X, waits=[(sd, d_x)])
        b.op("scalar", "mul", C_B, C_A, 1.0 / E, waits=[(sv, v_m)])
        a_xt_guard = b.op("scalar", "activation", V_XNT, V_XT, AF.Square,
                          bias=0.0, scale=1.0, accum_out=C_C, inc=sa)
        vop("tensor_tensor", C_D, C_B, C_B, ALU.mult,
            waits=[(sa, a_xt_guard)])
        v_s = vop("tensor_tensor", C_E, EPS5C, C_D, ALU.subtract)
        a_v = b.op("scalar", "activation", C_G, C_C, AF.Sqrt, bias=C_E,
                   scale=1.0 / E, waits=[(sv, v_s)], inc=sa)
        vop("reciprocal", C_H, C_G, waits=[(sa, a_v)])
        vop("tensor_tensor", C_I, C_B, C_H, ALU.mult)
        vop("tensor_tensor", C_J, ZERO1, C_I, ALU.subtract)
        v_xn = b.op("vector", "tensor_scalar", V_XNT, V_XT, C_H, C_J,
                    ALU.mult, ALU.add, inc=sv)
        v_xt_guard = v_xn
        for grp in range(4):
            tb = PT1 if grp % 2 == 0 else PT2
            pe_open(tb)
            for j in range(4):
                ec = grp * 4 + j
                b.op("tensor", "transpose", tb.t[:, j * 128:(j + 1) * 128],
                     V_XNT[:, ec * 128:(ec + 1) * 128], IDB[:, :],
                     waits=[(sv, v_xn)] if j == 0 else (),
                     inc=sp if j == 3 else None)
            pe_close(tb)
            consume(tb, "vector")
            if half_a:
                dst = XNA3[:, grp * 4:grp * 4 + 4, tt * 128:(tt + 1) * 128]
            else:
                dst = XNB3[:, grp * 4:grp * 4 + 4,
                           (tt - HALF) * 128:(tt - HALF + 1) * 128]
            b.op("vector", "tensor_copy", dst,
                 tb.t.ap().rearrange("p (c n) -> p c n", n=128)[:, 0:4, :], inc=sv)
            consumed(tb, "vector")
        v_tiles = sv.v
        src3 = (XNA3 if half_a else XNB3)[
            :, :, (tt % HALF) * 128:(tt % HALF + 1) * 128]
        b.op("sync", "dma_start",
             out=bass.AP(xn_hbm, tt * 128,
                         [[16 * T, 128], [T, 16], [1, 128]]),
             in_=src3, waits=[(sv, v_tiles)], inc=sd, incn=16)
        lbank = PE_ if tt % 2 == 0 else PF
        pe_open(lbank)
        for ec in range(16):
            b.op("tensor", "matmul", lbank.t[:, 0:16],
                 xn_tile(ec, tt), GWT[:, ec * 16:(ec + 1) * 16],
                 start=(ec == 0), stop=(ec == 15),
                 waits=([(sv, v_tiles), (sg, d_gwt)] if ec == 0 else ()),
                 inc=sp if ec == 15 else None)
        pe_close(lbank)
        consume(lbank, "scalar")
        a_sig = b.op("scalar", "activation", R_SCO, lbank.t[:, 0:16],
                     AF.Sigmoid, bias=0.0, scale=1.0, inc=sa)
        consumed(lbank, "scalar")
        vop("tensor_tensor", R_BIA, R_SCO, EBREP[:, :], ALU.add,
            waits=[(sa, a_sig), (sv, v_ebrep)])
        A0 = RW.ap()[:, 16:32][:, 0::4]
        A1 = RW.ap()[:, 16:32][:, 1::4]
        A2 = RW.ap()[:, 16:32][:, 2::4]
        A3 = RW.ap()[:, 16:32][:, 3::4]
        T_M1 = RW.ap()[:, 96:100]
        T_m1 = RW.ap()[:, 100:104]
        T_M2 = RW.ap()[:, 104:108]
        T_m2 = RW.ap()[:, 108:112]
        vop("tensor_tensor", T_M1, A0, A1, ALU.max)
        vop("tensor_tensor", T_m1, A0, A1, ALU.min)
        vop("tensor_tensor", T_M2, A2, A3, ALU.max)
        vop("tensor_tensor", T_m2, A2, A3, ALU.min)
        vop("tensor_tensor", R_GS, T_M1, T_M2, ALU.max)
        vop("tensor_tensor", T_m1, T_m1, T_m2, ALU.max)
        vop("tensor_tensor", T_M1, T_M1, T_M2, ALU.min)
        vop("tensor_tensor", T_M1, T_M1, T_m1, ALU.max)
        vop("tensor_tensor", R_GS, R_GS, T_M1, ALU.add)
        g0 = R_GS[:, 0:1]; g1 = R_GS[:, 1:2]
        g2 = R_GS[:, 2:3]; g3 = R_GS[:, 3:4]
        c1 = RW.ap()[:, 96:97]; c2 = RW.ap()[:, 97:98]
        c3 = RW.ap()[:, 98:99]; c4 = RW.ap()[:, 99:100]
        vop("tensor_tensor", c1, g0, g1, ALU.max)
        vop("tensor_tensor", c2, g0, g1, ALU.min)
        vop("tensor_tensor", c3, g2, g3, ALU.max)
        vop("tensor_tensor", c4, g2, g3, ALU.min)
        vop("tensor_tensor", c2, c2, c4, ALU.max)
        vop("tensor_tensor", c1, c1, c3, ALU.min)
        vop("tensor_tensor", R_T2, c1, c2, ALU.max)
        vop("tensor_single_scalar", R_GM, R_GS, R_T2, ALU.is_ge)
        vop("tensor_scalar", R_GMN, R_GM, 1.0, BIG, ALU.subtract, ALU.mult)
        for g in range(4):
            vop("tensor_single_scalar", R_MSK[:, 4 * g:4 * g + 4],
                R_BIA[:, 4 * g:4 * g + 4], R_GMN[:, g:g + 1], ALU.add)
        vop("tensor_copy", R_MSK2, R_MSK)
        for kext in range(4):
            vop("reduce_max", R_T4, R_MSK2, axis=AX.X)
            if kext < 3:
                vop("tensor_single_scalar", R_EQ, R_MSK2, R_T4, ALU.is_ge)
                vop("tensor_scalar", R_EQ, R_EQ, -BIG, None, ALU.mult)
                vop("tensor_tensor", R_MSK2, R_MSK2, R_EQ, ALU.add)
        vop("tensor_single_scalar", R_SEL, R_MSK, R_T4, ALU.is_ge)
        vop("tensor_tensor", R_SEL, R_SEL, R_SCO, ALU.mult)
        vop("reduce_sum", R_WS, R_SEL, axis=AX.X)
        vop("tensor_tensor", R_WS, R_WS, EPS20C, ALU.add)
        vop("reciprocal", R_WI, R_WS)
        b.op("vector", "tensor_scalar", CW.ap()[:, tt * 16:(tt + 1) * 16],
             R_SEL, R_WI, 2.5, ALU.mult, ALU.mult, inc=sv)
    xn_all_v = sv.v

    wst_free = [None, None, None, None]

    def load_slab(par, src_h, col0, ncols=512):
        w = [x for x in (wst_free[2 * par], wst_free[2 * par + 1])
             if x is not None]
        return b.op("gpsimd", "dma_start", out=wst3(par)[:, :, 0:ncols],
                    in_=bass.AP(src_h, col0,
                                [[src_h.shape[1], 128],
                                 [128 * src_h.shape[1], 16], [1, ncols]]),
                    waits=w, inc=sg, incn=16)

    wst_free[0] = wst_free[1] = (sp, sp.v)
    d_wv = load_slab(0, wv_h, 0) if phases >= 2 else 0
    for tt in range(NTT if phases >= 2 else 0):
        bank = PC if tt % 2 == 0 else PD
        pe_open(bank)
        for ec in range(16):
            b.op("tensor", "matmul", bank.t[:, :], xn_tile(ec, tt),
                 wst(0, ec), start=(ec == 0), stop=(ec == 15),
                 waits=([(sg, d_wv), (sv, xn_all_v)] if ec == 0 else ()),
                 inc=sp if ec == 15 else None)
        pe_close(bank)
        consume(bank, "scalar")
        b.op("scalar", "activation", VTOK.ap()[:, tt * 512:(tt + 1) * 512],
             bank.t[:, :], AF.Copy, bias=0.0, scale=1.0, inc=sa)
        consumed(bank, "scalar")
    wst_free[0] = wst_free[1] = (sp, sp.v)
    vtok_done = sa.v

    def rope_emit(bank, tt, out_stage):
        last = None
        for hs in range(2):
            base = hs * 256
            re = bank.t[:, base:base + 256:2]
            im = bank.t[:, base + 1:base + 256:2]
            sre = out_stage[:, base:base + 256:2]
            sim = out_stage[:, base + 1:base + 256:2]
            b.op("vector", "tensor_tensor", V_RT1, re, trig_cos(tt),
                 ALU.mult)
            b.op("vector", "tensor_tensor", V_RT2, im, trig_sin(tt),
                 ALU.mult)
            b.op("vector", "tensor_tensor", sre, V_RT1, V_RT2, ALU.subtract)
            b.op("vector", "tensor_tensor", V_RT3, re, trig_sin(tt),
                 ALU.mult)
            b.op("vector", "tensor_tensor", V_RT4, im, trig_cos(tt),
                 ALU.mult)
            last = b.op("vector", "tensor_tensor", sim, V_RT3, V_RT4,
                        ALU.add, inc=sv if hs == 1 else None)
        return last

    stg0_guard = 0
    for kfc in range(2 if phases >= 2 else 0):
        par = kfc % 2
        d_wk = load_slab(par, wk_h, kfc * 512)
        for tt in range(NTT):
            bank = PC if tt % 2 == 0 else PD
            pe_open(bank)
            for ec in range(16):
                b.op("tensor", "matmul", bank.t[:, :], xn_tile(ec, tt),
                     wst(par, ec), start=(ec == 0), stop=(ec == 15),
                     waits=([(sg, d_wk)] if ec == 0 else ()),
                     inc=sp if ec == 15 else None)
            pe_close(bank)
            consume(bank, "vector")
            b.wait("vector", (sa, trig_done), (sp, stg0_guard))
            v_r = rope_emit(bank, tt, V_STG0)
            consumed(bank, "vector")
            tb = PT1 if tt % 2 == 0 else PT2
            pe_open(tb)
            for j in range(4):
                b.op("tensor", "transpose", tb.t[:, j * 128:(j + 1) * 128],
                     V_STG0[:, j * 128:(j + 1) * 128], IDB[:, :],
                     waits=[(sv, v_r)] if j == 0 else (),
                     inc=sp if j == 3 else None)
            pe_close(tb)
            stg0_guard = sp.v
            consume(tb, "vector")
            dst = KFM.ap().rearrange("p (c n) -> p c n", n=T)[
                :, kfc * 4:kfc * 4 + 4, tt * 128:(tt + 1) * 128]
            b.op("vector", "tensor_copy", dst,
                 tb.t.ap().rearrange("p (c n) -> p c n", n=128)[:, 0:4, :], inc=sv)
            consumed(tb, "vector")
        wst_free[2 * par] = wst_free[2 * par + 1] = (sp, sp.v)
    kfm_done = sv.v

    q_stg_dma = {0: 0, 1: 0}
    q_stg_sp = {0: stg0_guard, 1: 0}
    for qfc in range(8 if phases >= 2 else 0):
        par = qfc % 2
        d_wq = load_slab(par, wq_h, qfc * 512)
        for tt in range(NTT):
            bank = PC if tt % 2 == 0 else PD
            pe_open(bank)
            for ec in range(16):
                b.op("tensor", "matmul", bank.t[:, :], xn_tile(ec, tt),
                     wst(par, ec), start=(ec == 0), stop=(ec == 15),
                     waits=([(sg, d_wq)] if ec == 0 else ()),
                     inc=sp if ec == 15 else None)
            pe_close(bank)
            consume(bank, "vector")
            sid = tt % 2
            stg = V_STG0 if sid == 0 else V_STG1
            b.wait("vector", (sd, q_stg_dma[sid]), (sp, q_stg_sp[sid]))
            v_r = rope_emit(bank, tt, stg)
            consumed(bank, "vector")
            tb = PT1 if tt % 2 == 0 else PT2
            pe_open(tb)
            for j in range(4):
                b.op("tensor", "transpose", tb.t[:, j * 128:(j + 1) * 128],
                     stg[:, j * 128:(j + 1) * 128], IDB[:, :],
                     waits=[(sv, v_r)] if j == 0 else (),
                     inc=sp if j == 3 else None)
            pe_close(tb)
            q_stg_sp[sid] = sp.v
            consume(tb, "vector")
            v_q = b.op("vector", "tensor_copy",
                       stg.rearrange("p (c n) -> p c n", n=128),
                       tb.t.ap().rearrange("p (c n) -> p c n", n=128)[:, 0:4, :],
                       inc=sv)
            consumed(tb, "vector")
            c0 = qfc * 4
            q_stg_dma[sid] = b.op(
                "sync", "dma_start",
                out=bass.AP(q_hbm, c0 * T + tt * 128,
                            [[32 * T, 128], [T, 4], [1, 128]]),
                in_=stg.rearrange("p (c n) -> p c n", n=128),
                waits=[(sv, v_q)], inc=sd, incn=16)
        wst_free[2 * par] = wst_free[2 * par + 1] = (sp, sp.v)
    q_spill_done = sd.v

    att_guard = [0, 0]
    xnh_guard = [0]
    dtc_free = [(sp, sp.v)] * 4

    def superphase(qn):
        q0 = qn * 4
        entry_sp = sp.v
        entry_sv = sv.v

        xnh_ready = b.op(
            "sync", "dma_start", out=XNH3[:, :, :],
            in_=bass.AP(xn_hbm, q0 * 128,
                        [[16 * T, 128], [T, 16], [1, 512]]),
            waits=([(sd, q_spill_done), (sp, entry_sp)] if qn == 0
                   else [(sp, xnh_guard[0])]),
            inc=sd, incn=16)
        acc_init = {}
        for tl in range(4):
            tt = q0 + tl
            acc_init[tl] = b.op(
                "sync", "dma_start",
                out=ACC.ap()[:, tl * E:(tl + 1) * E],
                in_=bass.AP(x_h, tt * 128 * E, [[E, 128], [1, E]]),
                inc=sd, incn=16)

        st_ = {
            "d_qb": [0, 0],
            "qb_guard": [entry_sp, entry_sp],
            "e_pt_guard": [entry_sp, entry_sp],
            "e_u_guard": [entry_sv, entry_sv],
            "pt_sp_prev": entry_sp,
            "a_p": {},
            "heads_a": 0,
            "epi_prev": entry_sv,
            "att_evac_v": 0,
            "last_acc_v": {tl: 0 for tl in range(4)},
        }

        def em_qload(qi):
            par = qi % 2
            st_["d_qb"][par] = b.op(
                "sync", "dma_start",
                out=a_qb(par).rearrange("p (c n) -> p c n", n=128),
                in_=bass.AP(q_hbm, (q0 + qi) * 128,
                            [[32 * T, 128], [T, 32], [1, 128]]),
                waits=[(sp, st_["qb_guard"][0])], inc=sd, incn=16)

        def em_part1(qi, h):
            qt = q0 + qi
            kn = qt + 1
            span = kn * 128
            nch = (span + 511) // 512
            par = qi % 2
            ep = h % 2
            kvh = h // 4
            ebuf = [a_e0(ep), a_e1(ep)]
            dcols = [R_D01, R_D02]
            for a_ in range(2):
                lhsT = a_qb(par)[:, (2 * h + a_) * 128:
                                 (2 * h + a_ + 1) * 128]
                kc = kvh * 2 + a_
                for ch in range(nch):
                    w = min(512, span - ch * 512)
                    bank = PA if ch % 2 == 0 else PB
                    is_last = ch == nch - 1
                    pe_open(bank)
                    b.op("tensor", "matmul", bank.t[:, 0:w], lhsT,
                         KFM.ap()[:, kc * T + ch * 512:
                                  kc * T + ch * 512 + w],
                         start=True, stop=not is_last,
                         waits=([(sd, st_["d_qb"][par]), (sv, kfm_done)]
                                if ch == 0 else ()),
                         inc=None if is_last else sp)
                    if is_last:
                        b.op("tensor", "matmul", bank.t[:, w - 128:w],
                             IDB[:, :], TRIB[:, :], start=False,
                             stop=True, inc=sp)
                    pe_close(bank)
                    consume(bank, "scalar")
                    if a_ == 0:
                        b.wait("scalar", (sp, st_["e_pt_guard"][ep]))
                    else:
                        b.wait("scalar", (sv, st_["e_u_guard"][ep]))
                    b.op("scalar", "activation",
                         ebuf[a_][:, ch * 512:ch * 512 + w],
                         bank.t[:, 0:w], AF.Exp, bias=0.0, scale=ESCALE,
                         accum_out=dcols[a_][:, ch:ch + 1], inc=sa)
                    consumed(bank, "scalar")
                dcol = (H_D0 if a_ == 0 else H_D1)[:, h:h + 1]
                vop("reduce_sum", dcol, dcols[a_][:, 0:nch], axis=AX.X,
                    waits=[(sa, sa.v)])
            b.op("vector", "reciprocal", C_A, H_D0[:, h:h + 1], inc=sv)
            v_rb = b.op("vector", "reciprocal", C_B, H_D1[:, h:h + 1],
                        inc=sv)
            b.wait("vector", (sv, v_rb))
            vop("tensor_tensor", C_B, C_B, LAMBC[:, h:h + 1], ALU.mult,
                waits=[(sv, v_lambc)])
            b.wait("vector", (sp, st_["e_pt_guard"][ep]))
            vop("tensor_single_scalar", a_e0(ep)[:, 0:span],
                a_e0(ep)[:, 0:span], C_A, ALU.mult)
            dsp = span - 128
            vop("tensor_tensor", a_e0(ep)[:, dsp:span],
                a_e0(ep)[:, dsp:span], TRIB[:, :], ALU.add)
            v_u = b.op("vector", "scalar_tensor_tensor",
                       a_e0(ep)[:, 0:span], a_e1(ep)[:, 0:span], C_B,
                       a_e0(ep)[:, 0:span], ALU.mult, ALU.add, inc=sv)
            st_["e_u_guard"][ep] = v_u
            st_["a_p"][h] = b.op(
                "scalar", "activation", a_e0(ep)[:, 0:span],
                a_e0(ep)[:, 0:span], AF.Exp, bias=0.0, scale=1.0,
                accum_out=H_D2[:, h:h + 1], waits=[(sv, v_u)], inc=sa)

        def em_part2(qi, h):
            qt = q0 + qi
            kn = qt + 1
            par = qi % 2
            ep = h % 2
            kvh = h // 4
            a_p = st_["a_p"][h]
            for grp in range((kn + 3) // 4):
                tb = PT1 if grp % 2 == 0 else PT2
                nj = min(4, kn - grp * 4)
                pe_open(tb)
                for j in range(nj):
                    kt = grp * 4 + j
                    b.op("tensor", "transpose",
                         tb.t[:, j * 128:(j + 1) * 128],
                         a_e0(ep)[:, kt * 128:(kt + 1) * 128], IDB[:, :],
                         waits=[(sa, a_p)] if j == 0 else (),
                         inc=sp if j == nj - 1 else None)
                pe_close(tb)
                consume(tb, "vector")
                b.op("vector", "tensor_copy",
                     A_PT.rearrange("p (c n) -> p c n", n=128)[
                         :, grp * 4:grp * 4 + nj, :],
                     tb.t.ap().rearrange("p (c n) -> p c n", n=128)[
                         :, 0:nj, :], inc=sv)
                consumed(tb, "vector")
            st_["e_pt_guard"][ep] = sp.v
            pt_done = sv.v
            ab = PE_ if h % 2 == 0 else PF
            pe_open(ab)
            for kt in range(kn):
                b.op("tensor", "matmul", ab.t[:, 0:128],
                     A_PT[:, kt * 128:(kt + 1) * 128],
                     VTOK.ap()[:, kt * 512 + kvh * 128:
                               kt * 512 + kvh * 128 + 128],
                     start=(kt == 0), stop=(kt == kn - 1),
                     waits=([(sv, pt_done), (sa, vtok_done)]
                            if kt == 0 else ()),
                     inc=sp if kt == kn - 1 else None)
            pe_close(ab)
            consume(ab, "scalar")
            b.wait("scalar", (sv, st_["epi_prev"]))
            b.wait("scalar", (sp, att_guard[0]))
            b.op("scalar", "activation", a_att(qi, h), ab.t[:, 0:128],
                 AF.Square, bias=0.0, scale=1.0,
                 accum_out=HSQ.ap()[:, h:h + 1])
            b.op("scalar", "activation", a_att(qi, h), ab.t[:, 0:128],
                 AF.Copy, bias=0.0, scale=1.0, inc=sa)
            consumed(ab, "scalar")

        def em_epi(qi):
            heads_a = sa.v
            b.wait("vector", (sa, heads_a))
            v_r3 = b.op("vector", "reciprocal", A_T3, H_D2, inc=sv)
            b.wait("vector", (sv, v_r3))
            vop("tensor_tensor", A_T4, A_T3, A_T3, ALU.mult)
            v_t = vop("tensor_tensor", A_T1, HSQ.ap()[:, :], A_T4,
                      ALU.mult)
            a_sq = b.op("scalar", "activation", A_T2, A_T1, AF.Sqrt,
                        bias=EPS8C, scale=1.0 / 128.0, waits=[(sv, v_t)],
                        inc=sa)
            v_r2 = b.op("vector", "reciprocal", A_T2, A_T2,
                        waits=[(sa, a_sq)], inc=sv)
            b.wait("vector", (sv, v_r2))
            v_sc = vop("tensor_tensor", H_SC, A_T2, A_T3, ALU.mult)
            stg_wait = sp.v
            for h4 in range(4):
                tb = PT1 if h4 % 2 == 0 else PT2
                b.wait("vector", (sp, stg_wait))
                v_stg = None
                for j in range(4):
                    h = h4 * 4 + j
                    v_stg = b.op("vector", "scalar_tensor_tensor",
                                 A_STG[:, j * 128:(j + 1) * 128],
                                 a_att(qi, h), H_SC[:, h:h + 1],
                                 RMSREP[:, :], ALU.mult, ALU.mult,
                                 waits=([(sv, v_sc), (sv, v_rms)]
                                        if j == 0 else ()),
                                 inc=sv if j == 3 else None)
                pe_open(tb)
                for j in range(4):
                    b.op("tensor", "transpose",
                         tb.t[:, j * 128:(j + 1) * 128],
                         A_STG[:, j * 128:(j + 1) * 128], IDB[:, :],
                         waits=[(sv, v_stg)] if j == 0 else (),
                         inc=sp if j == 3 else None)
                pe_close(tb)
                stg_wait = sp.v
                st_["pt_sp_prev"] = sp.v
                consume(tb, "vector")
                b.wait("vector", (sp, att_guard[0]))
                b.op("vector", "tensor_copy",
                     a_att4(qi, h4 * 4).rearrange("p (c n) -> p c n",
                                                  n=128),
                     tb.t.ap().rearrange("p (c n) -> p c n",
                                         n=128)[:, 0:4, :],
                     inc=sv)
                consumed(tb, "vector")
            st_["epi_prev"] = sv.v
            st_["qb_guard"][0] = sp.v

        def em_wo():
            att_v = st_["epi_prev"]
            for eg in range(4):
                for qtl in range(4):
                    obank = PE_ if (eg * 4 + qtl) % 2 == 0 else PF
                    pe_open(obank)
                    for hdh in range(2):
                        bj = (eg * 2 + hdh) % 4
                        w = ([wst_free[bj]] if wst_free[bj] is not None
                             else [])
                        if qtl == 0:
                            d_wo = b.op(
                                "gpsimd", "dma_start", out=wstq3(bj),
                                in_=bass.AP(wo_h,
                                            hdh * 8 * 128 * E + eg * 512,
                                            [[E, 128], [128 * E, 8],
                                             [1, 512]]),
                                waits=w, inc=sg, incn=16)
                            wo_dma[(eg, hdh)] = d_wo
                        for hd in range(8):
                            b.op("tensor", "matmul", obank.t[:, :],
                                 a_att(qtl, hdh * 8 + hd), wstq(bj, hd),
                                 start=(hdh == 0 and hd == 0),
                                 stop=(hdh == 1 and hd == 7),
                                 waits=([(sg, wo_dma[(eg, hdh)]),
                                         (sv, att_v)]
                                        if hd == 0 else ()),
                                 inc=sp if (hdh == 1 and hd == 7)
                                 else None)
                    pe_close(obank)
                    consume(obank, "vector")
                    dst = ACC.ap()[:, qtl * E + eg * 512:
                                   qtl * E + eg * 512 + 512]
                    st_["last_acc_v"][qtl] = b.op(
                        "vector", "tensor_tensor", dst, dst,
                        obank.t[:, :], ALU.add,
                        waits=[(sd, acc_init[qtl])], inc=sv)
                    consumed(obank, "vector")
                for hdh in range(2):
                    bj = (eg * 2 + hdh) % 4
                    wst_free[bj] = (sp, sp.v)
            att_guard[0] = sp.v
            att_guard[1] = sp.v

        fst = {"hf_guard": entry_sp, "hf_v": 0, "fb": 0}
        fbanks = [PC, PD, PE_]

        def em_gu(ei, ib):
            bj = gu_buf[(ei, ib)]
            d_w = gu_dma[(ei, ib)]
            nxt = (ei, ib + 2) if ib < 6 else (ei + 1, ib - 6)
            if nxt[0] <= NEXP and nxt not in gu_dma:
                load_gu(nxt[0], nxt[1])
            bg = fbanks[fst["fb"] % 3]
            bu = fbanks[(fst["fb"] + 1) % 3]
            fst["fb"] += 2
            pe_open(bg)
            for ec in range(16):
                b.op("tensor", "matmul", bg.t[:, :], wstq_g(bj, ec),
                     XNH.ap()[:, ec * 512:(ec + 1) * 512],
                     start=(ec == 0), stop=(ec == 15),
                     waits=([(sg, d_w), (sd, xnh_ready)]
                            if ec == 0 else ()),
                     inc=sp if ec == 15 else None)
            pe_close(bg)
            pe_open(bu)
            for ec in range(16):
                b.op("tensor", "matmul", bu.t[:, :], wstq_u(bj, ec),
                     XNH.ap()[:, ec * 512:(ec + 1) * 512],
                     start=(ec == 0), stop=(ec == 15),
                     inc=sp if ec == 15 else None)
            pe_close(bu)
            wst_free[bj] = (sp, sp.v)
            consume(bg, "scalar")
            a_sil = b.op("scalar", "activation", F_SIL, bg.t[:, :],
                         AF.Silu, bias=0.0, scale=1.0, inc=sa)
            consumed(bg, "scalar")
            consume(bu, "vector")
            b.wait("vector", (sp, fst["hf_guard"]))
            fst["hf_v"] = b.op(
                "vector", "tensor_tensor",
                F_HF[:, ib * 512:(ib + 1) * 512], F_SIL, bu.t[:, :],
                ALU.mult, waits=[(sa, a_sil)], inc=sv)
            consumed(bu, "vector")

        def em_down(ei, hc):
            d_d = {}
            for ibh in range(2):
                d_d[ibh] = dtc_dma.pop((ei, hc, ibh))
            if hc < 3:
                for ibh in range(2):
                    load_dtc(ei, hc + 1, ibh)
            elif ei < NEXP:
                for ibh in range(2):
                    load_dtc(ei + 1, 0, ibh)
            for tlp in range(2):
                bks = [fbanks[fst["fb"] % 3], fbanks[(fst["fb"] + 1) % 3]]
                fst["fb"] += 2
                for t2 in range(2):
                    pe_open(bks[t2])
                for ib in range(8):
                    ibh = ib // 4
                    par = (hc % 2) * 2 + ibh
                    for t2 in range(2):
                        tl = tlp * 2 + t2
                        b.op("tensor", "matmul", bks[t2].t[:, :],
                             F_HF[:, ib * 512 + tl * 128:
                                  ib * 512 + tl * 128 + 128],
                             f_dtc(par, ib % 4),
                             start=(ib == 0), stop=(ib == 7),
                             waits=([(sv, fst["hf_v"]), (sg, d_d[ibh])]
                                    if t2 == 0 and ib % 4 == 0 else ()),
                             inc=sp if ib == 7 else None)
                for t2 in range(2):
                    pe_close(bks[t2])
                for t2 in range(2):
                    tl = tlp * 2 + t2
                    cwcol = (CW.ap()[:, (q0 + tl) * 16 + ei:
                                     (q0 + tl) * 16 + ei + 1]
                             if ei < NEXP else ONEC)
                    bk = bks[t2]
                    consume(bk, "vector")
                    dst = ACC.ap()[:, tl * E + hc * 512:
                                   tl * E + hc * 512 + 512]
                    st_["last_acc_v"][tl] = b.op(
                        "vector", "scalar_tensor_tensor", dst,
                        bk.t[:, :], cwcol, dst, ALU.mult, ALU.add,
                        waits=[(sd, acc_init[tl])], inc=sv)
                    consumed(bk, "vector")
            for ibh in range(2):
                par = (hc % 2) * 2 + ibh
                dtc_free[par] = (sp, sp.v)
            if hc == 3:
                fst["hf_guard"] = sp.v

        gu_dma = {}
        gu_buf = {}
        gu_next_buf = [0]
        wo_dma = {}

        def load_gu(ei, ib):
            bj = gu_next_buf[0]
            gu_next_buf[0] = (bj + 1) % 4
            w = [wst_free[bj]] if wst_free[bj] is not None else []
            if ei < NEXP:
                gsrc = bass.AP(gt_h, ei * E * I_FF + ib * 128,
                               [[I_FF, 128], [128 * I_FF, 16], [1, 128]])
                usrc = bass.AP(ut_h, ei * E * I_FF + ib * 128,
                               [[I_FF, 128], [128 * I_FF, 16], [1, 128]])
            else:
                gsrc = bass.AP(sgt_h, ib * 128,
                               [[I_FF, 128], [128 * I_FF, 16], [1, 128]])
                usrc = bass.AP(sut_h, ib * 128,
                               [[I_FF, 128], [128 * I_FF, 16], [1, 128]])
            b.op("gpsimd", "dma_start", out=wstq_g3(bj), in_=gsrc,
                 waits=w, inc=sg, incn=16)
            gu_dma[(ei, ib)] = b.op("gpsimd", "dma_start",
                                    out=wstq_u3(bj), in_=usrc,
                                    inc=sg, incn=16)
            gu_buf[(ei, ib)] = bj

        dtc_dma = {}

        def load_dtc(ei, hc, ibh):
            par = (hc % 2) * 2 + ibh
            w = [dtc_free[par]]
            if ei < NEXP:
                s = bass.AP(dt_h,
                            ei * I_FF * E + ibh * 4 * 128 * E + hc * 512,
                            [[E, 128], [128 * E, 4], [1, 512]])
            else:
                s = bass.AP(sdt_h, ibh * 4 * 128 * E + hc * 512,
                            [[E, 128], [128 * E, 4], [1, 512]])
            dtc_dma[(ei, hc, ibh)] = b.op("gpsimd", "dma_start",
                                          out=f_dtc3(par), in_=s,
                                          waits=w, inc=sg, incn=16)

        load_gu(0, 0)
        load_gu(0, 1)
        load_dtc(0, 0, 0)
        load_dtc(0, 0, 1)

        att_stream = []
        for qi in range(4):
            att_stream.append(("qload", qi, None))
            for h in range(H):
                att_stream.append(("p1", qi, h))
                att_stream.append(("p2", qi, h))
            att_stream.append(("epi", qi, None))
        ordered = []
        pend_p2 = None
        for item in att_stream:
            if item[0] == "p1":
                ordered.append(item)
                if pend_p2 is not None:
                    ordered.append(pend_p2)
                pend_p2 = ("p2", item[1], item[2])
            elif item[0] == "p2":
                continue
            else:
                if item[0] == "epi" and pend_p2 is not None:
                    ordered.append(pend_p2)
                    pend_p2 = None
                ordered.append(item)

        ffn_stream = []
        for ei in range(NEXP + 1):
            for ib in range(8):
                ffn_stream.append(("gu", ei, ib))
            for hc in range(4):
                ffn_stream.append(("down", ei, hc))

        na, nf = len(ordered), len(ffn_stream)
        ai = fi = 0
        credit = 0.0
        ratio = na / nf
        while fi < nf or ai < na:
            if fi < nf:
                kind, x1, x2 = ffn_stream[fi]
                fi += 1
                if kind == "gu":
                    em_gu(x1, x2)
                else:
                    em_down(x1, x2)
            credit += ratio if fi <= nf else na
            while ai < na and (credit >= 1.0 or fi >= nf):
                credit -= 1.0
                kind, x1, x2 = ordered[ai]
                ai += 1
                if kind == "qload":
                    em_qload(x1)
                elif kind == "p1":
                    em_part1(x1, x2)
                elif kind == "p2":
                    em_part2(x1, x2)
                else:
                    em_epi(x1)
        xnh_guard[0] = sp.v
        em_wo()

        for tl in range(4):
            tt = q0 + tl
            b.op("sync", "dma_start",
                 out=bass.AP(out_h, tt * 128 * E, [[E, 128], [1, E]]),
                 in_=ACC.ap()[:, tl * E:(tl + 1) * E],
                 waits=[(sv, st_["last_acc_v"][tl])], inc=sd, incn=16)

    for qn in range(4):
        superphase(qn)

    b.wait("sync", (sd, sd.v))
    b.wait("gpsimd", (sg, sg.v))

    with nc.Block() as block:
        b.replay(block)
    ctx.close()
    return nc


def kernel(**inputs):
    x = np.ascontiguousarray(np.asarray(inputs["L0_attn_x"], np.float32))
    base = {
        "wq": np.ascontiguousarray(inputs["L0_attn_Wq"], dtype=np.float32),
        "wk": np.ascontiguousarray(inputs["L0_attn_Wk"], dtype=np.float32),
        "wv": np.ascontiguousarray(inputs["L0_attn_Wv"], dtype=np.float32),
        "wo": np.ascontiguousarray(inputs["L0_attn_Wo"], dtype=np.float32),
        "lq1": np.ascontiguousarray(inputs["L0_attn_lq1"], dtype=np.float32),
        "lq2": np.ascontiguousarray(inputs["L0_attn_lq2"], dtype=np.float32),
        "lk1": np.ascontiguousarray(inputs["L0_attn_lk1"], dtype=np.float32),
        "lk2": np.ascontiguousarray(inputs["L0_attn_lk2"], dtype=np.float32),
        "rms": np.ascontiguousarray(
            np.asarray(inputs["L0_attn_rms_w"], np.float32).reshape(1, D)),
        "gwt": np.ascontiguousarray(
            np.asarray(inputs["L0_ffn_gate_weight"], np.float32).T),
        "eb": np.ascontiguousarray(
            np.asarray(inputs["L0_ffn_expert_bias"],
                       np.float32).reshape(1, NEXP)),
        "gt": np.ascontiguousarray(
            np.asarray(inputs["L0_ffn_gate_proj"],
                       np.float32).transpose(0, 2, 1)),
        "ut": np.ascontiguousarray(
            np.asarray(inputs["L0_ffn_up_proj"],
                       np.float32).transpose(0, 2, 1)),
        "dtp": np.ascontiguousarray(
            np.asarray(inputs["L0_ffn_down_proj"],
                       np.float32).transpose(0, 2, 1)),
        "sgt": np.ascontiguousarray(
            np.asarray(inputs["L0_ffn_shared_gate"], np.float32).T),
        "sut": np.ascontiguousarray(
            np.asarray(inputs["L0_ffn_shared_up"], np.float32).T),
        "sdt": np.ascontiguousarray(
            np.asarray(inputs["L0_ffn_shared_down"], np.float32).T),
    }
    nc = bass.Bass("TRN2", target_bir_lowering=False)
    build(nc)
    in_maps = []
    for c in range(8):
        m = dict(base)
        m["x"] = np.ascontiguousarray(x[c])
        in_maps.append(m)
    res = run_bass_kernel_spmd(nc, in_maps, core_ids=list(range(8)))
    out = np.stack([np.asarray(r["out"]) for r in res.results], axis=0)
    return out.astype(np.float32)
